# version 06
# baseline (speedup 1.0000x reference)
import math
import numpy as np
import ml_dtypes
import concourse.bass as bass
import concourse.mybir as mybir
from concourse.bass_utils import run_bass_kernel_spmd

F32 = mybir.dt.float32
BF16 = mybir.dt.bfloat16
FP8 = mybir.dt.float8e4
AX = mybir.AxisListType.X
OP_ = mybir.AluOpType
AF = mybir.ActivationFunctionType

NDC = 16
TPC = 1024
DEBUG_DUMPS = False
ZIG = [[0, 3, 4, 7, 8, 11, 12, 15], [1, 2, 5, 6, 9, 10, 13, 14]]
OWNER = [0 if g in ZIG[0] else 1 for g in range(16)]
LOCIDX = [ZIG[OWNER[g]].index(g) for g in range(16)]
LEN = [[len([b_ for b_ in ZIG[p] if b_ >= j]) for j in range(16)]
       for p in (0, 1)]
LPROF = [max(LEN[0][j], LEN[1][j]) for j in range(16)]
SOFF = [(8 - LPROF[j]) * 128 for j in range(16)]
WJ = [LPROF[j] * 128 for j in range(16)]
INV_SQRT_D = 1.0 / math.sqrt(128.0)

SV_GNW, SV_GNB, SV_SGB, SV_SUB, SV_SDB, SV_BQ, SV_BK, SV_L1B = \
    0, 16, 32, 40, 48, 64, 80, 84
CN_PSWAP, CN_ECOL, CN_ONE, CN_ONER, CN_RWB = 0, 128, 384, 385, 516


def col_chunks(c0, c1):
    out = []
    c = c0
    while c < c1:
        nxt = min(c1, (c // 512 + 1) * 512)
        out.append((c, nxt))
        c = nxt
    return out


def build_program():
    nc = bass.Bass("TRN2", target_bir_lowering=False)

    def dram_in(name, shape, dt=F32):
        return nc.dram_tensor(name, shape, dt, kind="ExternalInput")

    x0 = dram_in("x0", [NDC, 128, TPC])
    wqkv_st = dram_in("wqkv_st", [32, 128, 2048])
    wv_mov = dram_in("wv_mov", [16, 128, 2048])
    wg_st = dram_in("wg_st", [16, 128, 2048])
    wp_st = dram_in("wp_st", [16, 128, 2048])
    rw_st = dram_in("rw_st", [16, 128, 8])
    wgu_st = dram_in("wgu_st", [2, 8, 8, 128, 2048])
    wd_st = dram_in("wd_st", [8, 16, 128, 1024])
    sgu_st = dram_in("sgu_st", [2, 8, 128, 2048])
    sd_st = dram_in("sd_st", [16, 128, 1024])
    smallw = dram_in("smallw", [128, 128])
    l1q_st = dram_in("l1q_st", [16, 128, 2048])
    l1k_st = dram_in("l1k_st", [4, 128, 2048])
    l1v_mov = dram_in("l1v_mov", [16, 128, 512])
    bv_row = dram_in("bv_row", [1, 512])
    l1w_st = dram_in("l1w_st", [16, 128, 2048], BF16)
    l1b_bf = dram_in("l1b_bf", [128, 16], BF16)
    tabq = dram_in("tabq", [16, 2, 128, TPC], BF16)
    tabk = dram_in("tabk", [16, 2, 128, TPC], BF16)
    am_t = dram_in("am_t", [16, 128, 256], BF16)
    consts = dram_in("consts", [128, 648], BF16)
    identf = dram_in("identf", [128, 128])

    ag0_in = nc.dram_tensor("ag0_in", [4096, 1024], BF16)
    ag0_out = nc.dram_tensor("ag0_out", [8192, 1024], BF16)
    ag1_in = nc.dram_tensor("ag1_in", [1024, 1024], BF16)
    ag1_out = nc.dram_tensor("ag1_out", [2048, 1024], BF16)
    ar_in = nc.dram_tensor("ar_in", [1, 16], F32)
    ar_out = nc.dram_tensor("ar_out", [1, 16], F32)
    scl_dram = nc.dram_tensor("scl_dram", [1, 4], F32)

    xout = nc.dram_tensor("xout", [NDC, 128, TPC], F32, kind="ExternalOutput")
    dbg = {}
    if DEBUG_DUMPS:
        for nm, dt_ in [("dbg_xn1", BF16), ("dbg_q0", BF16), ("dbg_y0", BF16),
                        ("dbg_g0", BF16), ("dbg_gated", BF16),
                        ("dbg_x1", F32), ("dbg_q1", BF16), ("dbg_y1", BF16),
                        ("dbg_g1", BF16)]:
            dbg[nm] = nc.dram_tensor(nm, [128, NDC * 1024], dt_)

    ctx = []

    def sb(name, shape, dt):
        t = nc.sbuf_tensor(name, shape, dt)
        ctx.append(t)
        return t.__enter__()

    def psum(name, shape, dt=F32):
        t = nc.psum_tensor(name, shape, dt)
        ctx.append(t)
        return t.__enter__()

    X = sb("X", [128, NDC * 1024], F32)
    A = sb("A", [128, NDC * 1024], BF16)
    Bq = sb("Bq", [128, NDC * 1024], BF16)
    Cb = sb("Cb", [128, NDC * 1024], BF16)
    Dg = sb("Dg", [128, NDC * 1024], BF16)
    WT0 = sb("WT0", [128, 2048], BF16)
    WT1 = sb("WT1", [128, 2048], BF16)
    SM0 = sb("SM0", [128, 1024], BF16)
    SM1 = sb("SM1", [128, 1024], BF16)
    SVC = sb("SVC", [128, 128], F32)
    CONS = sb("CONS", [128, 648], BF16)
    IDF = sb("IDF", [128, 128], F32)
    STT = sb("STT", [128, 256], F32)
    XSR = sb("XSR", [1, 16], F32)
    AMX = sb("AMX", [128, 32], F32)
    SM8 = sb("SM8", [128, 256], FP8)
    SCL = sb("SCL", [128, 4], F32)

    PA = psum("PA", [128, 1024])
    PB = psum("PB", [128, 1024])
    PC = psum("PC", [128, 1024])
    PD = psum("PD", [128, 1024])
    PSUM = {"PA": PA, "PB": PB, "PC": PC, "PD": PD}

    ONES_COL = CONS[:, CN_ONE:CN_ONE + 1]
    ONER = CONS[0:1, CN_ONER:CN_ONER + 128]
    PSWAP = CONS[:, CN_PSWAP:CN_PSWAP + 128]
    RWB = CONS[:, CN_RWB:CN_RWB + 128]

    def ECOLt(h):
        return CONS[:, CN_ECOL + h * 16:CN_ECOL + (h + 1) * 16]

    TBL = [Dg[:, 0:2048], Dg[:, 2048:4096]]
    T1S = [Dg[:, 4096:5120], Dg[:, 5120:6144]]
    RB_D = Dg[0:1, 6144:7168]
    BVC = Dg[:, 4096:4608]
    HCB = Cb[:, 14 * 1024:15 * 1024]
    RB_C = Cb[0:1, 13 * 1024:13 * 1024 + 1024]
    CT = Cb[0:8, 15 * 1024:16 * 1024]
    AK = A[:, 0:2048]
    AVt = A[:, 2048:4096]
    AE = [A[:, 4096:5120], A[:, 5120:6144]]
    AMm = A[:, 6144:10240]
    GSA = A[0:16, 10240:11264]
    GSB = A[0:16, 11264:12288]
    RB_A = A[0:1, 12288:13312]

    A_FAM = ["A", "AKV", "AE0", "AE1", "AM", "GS", "RB_A"]
    CB_FAM = ["Cb", "HCB", "RB_C", "CT"]
    DG_FAM = ["Dg", "TBL0", "TBL1", "T1S0", "T1S1", "RB_D", "BVC"]

    blk = nc.Block()
    blk.__enter__()
    ctx.append(blk)

    sems = {}

    def sem(name):
        h = nc.semaphore(name)
        ctx.append(h)
        sems[name] = [h.__enter__(), 0]

    for s in ["eT", "eV", "eS", "cAG0", "cAG1", "cAR"]:
        sem(s)

    ENGSEM = {"TE": "eT", "VE": "eV", "SE": "eS"}
    DMAE = {"SY", "GP", "SQ"}

    lastw = {}
    readers = {}
    seen = {k: {} for k in ("TE", "VE", "SE", "SY", "GP", "SQ")}
    PARENT = {"TBL0": "Dg", "TBL1": "Dg", "T1S0": "Dg", "T1S1": "Dg",
              "RB_D": "Dg", "BVC": "Dg", "HCB": "Cb", "RB_C": "Cb",
              "CT": "Cb", "AKV": "A", "AE0": "A", "AE1": "A", "AM": "A",
              "GS": "A", "GS2": "A", "RB_A": "A"}

    def OP(engname, eng, reads, writes, emit, n_dma=1):
        toks = []
        for r in reads:
            t = lastw.get(r)
            if t:
                toks.append(t)
            p = PARENT.get(r)
            if p:
                t = lastw.get(p)
                if t:
                    toks.append(t)
        for w in writes:
            t = lastw.get(w)
            if t:
                toks.append(t)
            toks.extend(readers.get(w, ()))
            p = PARENT.get(w)
            if p:
                t = lastw.get(p)
                if t:
                    toks.append(t)
                toks.extend(readers.get(p, ()))
        agg = {}
        for (s, v) in toks:
            agg[s] = max(agg.get(s, 0), v)
        for s, v in sorted(agg.items()):
            if seen[engname].get(s, 0) < v:
                eng.wait_ge(sems[s][0], v)
                seen[engname][s] = v
        out = emit(eng)
        if engname in DMAE:
            sname = "q%s_%s" % (engname, writes[0])
            if sname not in sems:
                sem(sname)
            step = 16
        else:
            sname = ENGSEM[engname]
            step = 1
        if isinstance(out, (list, tuple)):
            for ins_ in out:
                sems[sname][1] += step
                ins_.then_inc(sems[sname][0], step)
        else:
            sems[sname][1] += step
            out.then_inc(sems[sname][0], step)
        tok = (sname, sems[sname][1])
        for r in reads:
            readers.setdefault(r, []).append(tok)
        for w in writes:
            lastw[w] = tok
            readers[w] = []
        return tok

    def TOK(name, v=None):
        return (name, sems[name][1] if v is None else v)

    def note(res, tok):
        lastw[res] = tok
        readers[res] = []

    def TE(fn):
        blk.tensor(fn)

    def VE(fn):
        blk.vector(fn)

    def SE(fn):
        blk.scalar(fn)

    def GP(fn):
        blk.gpsimd(fn)

    def SY(fn):
        blk.sync(fn)

    def mmsteps(te, P, c0, c1, steps, rbase=None, start=True, stop=True,
                rows=128):
        rb = c0 if rbase is None else rbase
        n = len(steps)
        last = None
        for (a, b_) in col_chunks(c0, c1):
            for i, (lh, rh) in enumerate(steps):
                last = te.matmul(P[0:rows, a:b_], lh, rh[:, a - rb:b_ - rb],
                                 start=(start and i == 0),
                                 stop=(stop and i == n - 1))
        return last

    def _(sy):
        OP("SY", sy, [], ["X"], lambda e: [
            e.dma_start(out=X[:, dc * 1024:(dc + 1) * 1024], in_=x0[dc])
            for dc in range(NDC)])
        OP("SY", sy, [], ["SVC"], lambda e: e.dma_start(
            out=SVC[:, 0:SV_L1B], in_=smallw[0:128, 0:SV_L1B]))
        OP("SY", sy, [], ["CONS"], lambda e: e.dma_start(
            out=CONS[:, 0:CN_RWB], in_=consts[0:128, 0:CN_RWB]))
        OP("SY", sy, [], ["IDF"], lambda e: e.dma_start(
            out=IDF[:, :], in_=identf.ap()))
    SY(_)

    def _(gp):
        OP("GP", gp, [], ["RWB"], lambda e: [
            e.dma_start(out=RWB[:, dc * 8:(dc + 1) * 8], in_=rw_st[dc])
            for dc in range(NDC)])
        OP("GP", gp, [], ["SVCb"], lambda e: e.dma_start(
            out=SVC[:, SV_L1B:SV_L1B + 16], in_=l1b_bf.ap()))
    GP(_)

    def _(ve):
        OP("VE", ve, [], ["XSR"], lambda e: e.memset(XSR[:, :], 0.0))
    VE(_)

    def emit_layernorm(rb_row, rb_res):
        def _(ve):
            OP("VE", ve, ["X"], ["A"] + A_FAM[1:], lambda e: [
                e.tensor_copy(A[:, dc * 1024:(dc + 1) * 1024],
                              X[:, dc * 1024:(dc + 1) * 1024])
                for dc in range(NDC)][-1:])
        VE(_)

        def _(se):
            OP("SE", se, ["A"], CB_FAM, lambda e: [
                e.activation(Cb[:, dc * 1024:(dc + 1) * 1024],
                             A[:, dc * 1024:(dc + 1) * 1024], AF.Square)
                for dc in range(NDC)][-1:])
        SE(_)

        def _(te):
            def em(e):
                last = None
                for tb in range(8):
                    for dc in range(NDC):
                        last = e.matmul(
                            PA[:, tb:tb + 1],
                            A[:, dc * 1024 + tb * 128:
                              dc * 1024 + (tb + 1) * 128],
                            ONES_COL, start=(dc == 0), stop=(dc == 15))
                    for dc in range(NDC):
                        last = e.matmul(
                            PA[:, 512 + tb:512 + tb + 1],
                            Cb[:, dc * 1024 + tb * 128:
                               dc * 1024 + (tb + 1) * 128],
                            ONES_COL, start=(dc == 0), stop=(dc == 15))
                return last
            OP("TE", te, ["A", "Cb", "CONS"], ["PA"], em)
        TE(_)

        def _(ve):
            OP("VE", ve, ["PA"], ["STTa"], lambda e: e.tensor_scalar(
                STT[:, 0:8], PA[:, 0:8], 1.0 / 2048.0, None, OP_.mult))
            OP("VE", ve, ["PA"], ["STTb"], lambda e: e.tensor_scalar(
                STT[:, 8:16], PA[:, 512:520], 1.0 / 2048.0, None, OP_.mult))
            OP("VE", ve, ["STTa"], ["STTc"], lambda e: e.tensor_tensor(
                STT[:, 16:24], STT[:, 0:8], STT[:, 0:8], OP_.mult))
            OP("VE", ve, ["STTb", "STTc"], ["STTb"], lambda e: e.tensor_tensor(
                STT[:, 8:16], STT[:, 8:16], STT[:, 16:24], OP_.subtract))
            OP("VE", ve, ["STTb"], ["STTb"], lambda e: e.tensor_scalar(
                STT[:, 8:16], STT[:, 8:16], 1e-5, None, OP_.add))
        VE(_)

        def _(se):
            OP("SE", se, ["STTb"], ["STTc"], lambda e: e.activation(
                STT[:, 16:24], STT[:, 8:16], AF.Sqrt))
        SE(_)

        def _(ve):
            OP("VE", ve, ["STTc"], ["STTb"], lambda e: e.reciprocal(
                STT[:, 8:16], STT[:, 16:24]))
            OP("VE", ve, ["STTa", "STTb"], ["STTc"], lambda e: e.tensor_tensor(
                STT[:, 16:24], STT[:, 0:8], STT[:, 8:16], OP_.mult))
            OP("VE", ve, ["STTc"], ["STTc"], lambda e: e.tensor_scalar(
                STT[:, 16:24], STT[:, 16:24], -1.0, None, OP_.mult))
        VE(_)

        def _(te):
            def em(e):
                e.transpose(PC[0:8, 0:128], STT[:, 8:16], IDF[:, :])
                return e.transpose(PC[0:8, 128:256], STT[:, 16:24],
                                   IDF[:, :])
            OP("TE", te, ["STTb", "STTc", "IDF"], ["PC"], em)
        TE(_)

        def _(ve):
            OP("VE", ve, ["PC"], ["SM0"], lambda e: e.tensor_copy(
                SM0[0:8, 0:256], PC[0:8, 0:256]))
        VE(_)

        def _(sy):
            OP("SY", sy, ["SM0"], [rb_res], lambda e: e.dma_start(
                out=rb_row, in_=SM0[0:8, 0:128]))
        SY(_)

        def _(te):
            OP("TE", te, [rb_res, "CONS"], ["PA"], lambda e: mmsteps(
                e, PA, 0, 1024, [(ONER, rb_row)]))
        TE(_)

        def _(sy):
            OP("SY", sy, ["SM0"], [rb_res], lambda e: e.dma_start(
                out=rb_row, in_=SM0[0:8, 128:256]))
        SY(_)

        def _(te):
            OP("TE", te, [rb_res, "CONS"], ["PB"], lambda e: mmsteps(
                e, PB, 0, 1024, [(ONER, rb_row)]))
        TE(_)

        def _(ve):
            OP("VE", ve, ["X", "PA"], CB_FAM, lambda e: [
                e.tensor_tensor(Cb[:, dc * 1024:(dc + 1) * 1024],
                                X[:, dc * 1024:(dc + 1) * 1024],
                                PA[:, 0:1024], OP_.mult)
                for dc in range(NDC)][-1:])
            OP("VE", ve, ["Cb", "PB"], ["A"] + A_FAM[1:], lambda e: [
                e.tensor_tensor(A[:, dc * 1024:(dc + 1) * 1024],
                                Cb[:, dc * 1024:(dc + 1) * 1024],
                                PB[:, 0:1024], OP_.add)
                for dc in range(NDC)][-1:])
        VE(_)

    emit_layernorm(RB_D, "RB_D")


    if DEBUG_DUMPS:
        def _(sy):
            OP("SY", sy, ["A"], ["dbg_xn1"], lambda e: e.dma_start(
                out=dbg["dbg_xn1"].ap(), in_=A[:, :]))
        SY(_)
    for cc in range(32):
        sl = cc % 2
        pn = "PA" if sl == 0 else "PB"
        rn = "PC" if sl == 0 else "PD"
        qb = [SM0, SM1][sl]
        qn = ["SM0", "SM1"][sl]
        t1 = T1S[sl]
        tn = ["T1S0", "T1S1"][sl]
        wn = ["WT0", "WT1"][sl]
        tbl = TBL[sl]
        tbn = ["TBL0", "TBL1"][sl]

        def _(gp, cc=cc, sl=sl, wn=wn):
            OP("GP", gp, [], [wn], lambda e: e.dma_start(
                out=[WT0, WT1][sl][:, 0:2048], in_=wqkv_st[cc]))
        GP(_)

        def _(sq, cc=cc, sl=sl, tbl=tbl, tbn=tbn):
            src = tabq if cc < 16 else tabk
            OP("SQ", sq, [], [tbn], lambda e: e.dma_start(
                out=tbl[:, 0:2048],
                in_=src[cc % 16].rearrange("w p t -> p w t")))
        SE(_)

        def _(te, cc=cc, sl=sl, pn=pn, wn=wn):
            OP("TE", te, [wn, "A"], [pn], lambda e: mmsteps(
                e, PSUM[pn], 0, 1024,
                [([WT0, WT1][sl][:, dc * 128:(dc + 1) * 128],
                  A[:, dc * 1024:(dc + 1) * 1024]) for dc in range(NDC)]))
        TE(_)

        def _(ve, cc=cc, sl=sl, pn=pn, qb=qb, qn=qn):
            OP("VE", ve, [pn], [qn], lambda e: e.tensor_copy(
                qb[:, 0:1024], PSUM[pn][:, 0:1024]))
        VE(_)

        def _(te, cc=cc, sl=sl, rn=rn, qb=qb, qn=qn):
            OP("TE", te, [qn, "CONS"], [rn], lambda e: mmsteps(
                e, PSUM[rn], 0, 1024, [(PSWAP, qb[:, 0:1024])]))
        TE(_)

        def _(ve, cc=cc, sl=sl, rn=rn, qb=qb, qn=qn, t1=t1, tn=tn, tbn=tbn):
            tbl_ = TBL[sl]
            OP("VE", ve, [qn, tbn], [tn], lambda e: e.tensor_tensor(
                t1[:, 0:1024], qb[:, 0:1024], tbl_[:, 0:1024], OP_.mult))
            OP("VE", ve, [rn, tbn], [qn], lambda e: e.tensor_tensor(
                qb[:, 0:1024], PSUM[rn][:, 0:1024], tbl_[:, 1024:2048],
                OP_.mult))
            if cc < 16:
                OP("VE", ve, [tn, qn], ["Bq"], lambda e: e.tensor_tensor(
                    Bq[:, cc * 1024:(cc + 1) * 1024], t1[:, 0:1024],
                    qb[:, 0:1024], OP_.add))
            else:
                OP("VE", ve, [tn, qn], [tn], lambda e: e.tensor_tensor(
                    t1[:, 0:1024], t1[:, 0:1024], qb[:, 0:1024], OP_.add))
        VE(_)

        if cc >= 16:
            def _(sy, cc=cc, t1=t1, tn=tn):
                OP("SY", sy, [tn], ["ag0_in"], lambda e: e.dma_start(
                    out=ag0_in[(cc - 16) * 128:(cc - 15) * 128, :],
                    in_=t1[:, 0:1024]))
            SY(_)

    for dvq in range(2):
        def _(gp, dvq=dvq):
            OP("GP", gp, [], CB_FAM, lambda e: [
                e.dma_start(out=Cb[:, dc * 1024:(dc + 1) * 1024],
                            in_=wv_mov[dc, :, dvq * 1024:(dvq + 1) * 1024])
                for dc in range(NDC)])
        GP(_)

        for tb in range(8):
            sl = tb % 2
            pn = "PA" if sl == 0 else "PB"

            def _(te, dvq=dvq, tb=tb, sl=sl, pn=pn):
                OP("TE", te, ["Cb", "A"], [pn], lambda e: mmsteps(
                    e, PSUM[pn], 0, 1024,
                    [(A[:, dc * 1024 + tb * 128:dc * 1024 + (tb + 1) * 128],
                      Cb[:, dc * 1024:(dc + 1) * 1024])
                     for dc in range(NDC)]))
            TE(_)

            def _(ve, dvq=dvq, tb=tb, sl=sl, pn=pn):
                OP("VE", ve, [pn], [["SM0", "SM1"][sl]],
                   lambda e: e.tensor_copy([SM0, SM1][sl][:, 0:1024],
                                           PSUM[pn][:, 0:1024]))
            VE(_)

            def _(sy, dvq=dvq, tb=tb, sl=sl):
                r0 = 2048 + (tb * 2 + dvq) * 128
                OP("SY", sy, [["SM0", "SM1"][sl]], ["ag0_in"],
                   lambda e: e.dma_start(out=ag0_in[r0:r0 + 128, :],
                                         in_=[SM0, SM1][sl][:, 0:1024]))
            SY(_)

    def _(gp):
        def em(e):
            return e.collective_compute(
                "AllGather", OP_.bypass,
                replica_groups=[[0, 1], [2, 3], [4, 5], [6, 7]],
                ins=[ag0_in.ap().opt()], outs=[ag0_out.ap().opt()])
        toks = []
        t = lastw.get("ag0_in")
        if t:
            toks.append(t)
        for (s, v) in toks:
            if seen["GP"].get(s, 0) < v:
                gp.wait_ge(sems[s][0], v)
                seen["GP"][s] = v
        ins_ = em(gp)
        sems["cAG0"][1] += 1
        ins_.then_inc(sems["cAG0"][0], 1)
        note("ag0_out", ("cAG0", sems["cAG0"][1]))
    GP(_)


    if DEBUG_DUMPS:
        def _(sy):
            OP("SY", sy, ["Bq"], ["dbg_q0"], lambda e: e.dma_start(
                out=dbg["dbg_q0"].ap(), in_=Bq[:, :]))
        SY(_)
    def _(te):
        def em(e):
            last = None
            for tb in range(8):
                for dc in range(NDC):
                    last = e.matmul(
                        PB[:, tb * 8:(tb + 1) * 8],
                        A[:, dc * 1024 + tb * 128:dc * 1024 + (tb + 1) * 128],
                        RWB[:, dc * 8:(dc + 1) * 8],
                        start=(dc == 0), stop=(dc == 15))
            return last
        OP("TE", te, ["A", "RWB"], ["PB"], em)
    TE(_)

    def _(ve):
        for tb in range(8):
            lo = PB[:, tb * 8:(tb + 1) * 8]
            OP("VE", ve, ["PB"], ["STTa"], lambda e, lo=lo: e.reduce_max(
                STT[:, 24:25], lo, axis=AX))
            OP("VE", ve, ["PB", "STTa"], ["STTb"],
               lambda e, lo=lo: e.tensor_scalar(
                   STT[:, 32:40], lo, STT[:, 24:25], None, OP_.subtract))
            OP("VE", ve, ["STTb"], ["STTc"], lambda e: e.tensor_scalar(
                STT[:, 40:48], STT[:, 32:40], 0.0, None, OP_.is_ge))
            OP("VE", ve, ["STTc"], ["STTd"], lambda e: e.tensor_scalar(
                STT[:, 48:56], STT[:, 40:48], -1e30, None, OP_.mult))
            OP("VE", ve, ["PB", "STTd"], ["STTb"],
               lambda e, lo=lo: e.tensor_tensor(
                   STT[:, 32:40], lo, STT[:, 48:56], OP_.add))
            OP("VE", ve, ["STTb"], ["STTa"], lambda e: e.reduce_max(
                STT[:, 25:26], STT[:, 32:40], axis=AX))
            OP("VE", ve, ["PB", "STTa"], ["STTc"],
               lambda e, lo=lo: e.tensor_scalar(
                   STT[:, 40:48], lo, STT[:, 25:26], None, OP_.is_ge))
            OP("VE", ve, ["PB", "STTc"], ["COMB"],
               lambda e, lo=lo, tb=tb: e.tensor_tensor(
                   STT[:, 160 + tb * 8:160 + (tb + 1) * 8], lo,
                   STT[:, 40:48], OP_.mult))
    VE(_)

    def _(te):
        def em(e):
            last = None
            for tb in range(8):
                last = e.transpose(PC[0:8, tb * 128:(tb + 1) * 128],
                                   STT[:, 160 + tb * 8:160 + (tb + 1) * 8],
                                   IDF[:, :])
            return last
        OP("TE", te, ["COMB", "IDF"], ["PC"], em)
    TE(_)

    def _(ve):
        OP("VE", ve, ["PC"], ["CT"], lambda e: e.tensor_copy(
            CT, PC[0:8, 0:1024]))
    VE(_)

    for e_ in range(9):
        shared = (e_ == 8)
        if not shared:
            def _(sy, e_=e_):
                OP("SY", sy, ["CT"], ["RB_C"], lambda e: e.dma_start(
                    out=RB_C, in_=CT[e_:e_ + 1, 0:1024]))
            SY(_)

            def _(te, e_=e_):
                OP("TE", te, ["RB_C", "CONS"], ["PC"], lambda e: mmsteps(
                    e, PC, 0, 1024, [(ONER, RB_C)]))
            TE(_)

            def _(ve, e_=e_):
                OP("VE", ve, ["PC"], ["HCB"], lambda e: e.tensor_copy(
                    HCB[:, 0:1024], PC[:, 0:1024]))
            VE(_)

        for icc in range(8):
            def _(gp, e_=e_, icc=icc):
                if shared:
                    OP("GP", gp, [], ["WT0"], lambda e: e.dma_start(
                        out=WT0[:, 0:2048], in_=sgu_st[0, icc]))
                    OP("GP", gp, [], ["WT1"], lambda e: e.dma_start(
                        out=WT1[:, 0:2048], in_=sgu_st[1, icc]))
                else:
                    OP("GP", gp, [], ["WT0"], lambda e: e.dma_start(
                        out=WT0[:, 0:2048], in_=wgu_st[0, e_, icc]))
                    OP("GP", gp, [], ["WT1"], lambda e: e.dma_start(
                        out=WT1[:, 0:2048], in_=wgu_st[1, e_, icc]))
            GP(_)

            def _(te, e_=e_, icc=icc):
                OP("TE", te, ["WT0", "A"], ["PA"], lambda e: mmsteps(
                    e, PA, 0, 1024,
                    [(WT0[:, dc * 128:(dc + 1) * 128],
                      A[:, dc * 1024:(dc + 1) * 1024])
                     for dc in range(NDC)]))
                OP("TE", te, ["WT1", "A"], ["PB"], lambda e: mmsteps(
                    e, PB, 0, 1024,
                    [(WT1[:, dc * 128:(dc + 1) * 128],
                      A[:, dc * 1024:(dc + 1) * 1024])
                     for dc in range(NDC)]))
            TE(_)

            def _(se, e_=e_, icc=icc, shared=shared):
                OP("SE", se, ["PA", "SVC"], ["SM0"], lambda e: e.activation(
                    SM0[:, 0:1024], PA[:, 0:1024], AF.Silu,
                    bias=(SVC[:, SV_SGB + icc:SV_SGB + icc + 1] if shared
                          else 0.0)))
            SE(_)

            def _(ve, e_=e_, icc=icc, shared=shared):
                if shared:
                    OP("VE", ve, ["PB", "SVC"], ["SM1"],
                       lambda e: e.tensor_scalar(
                           SM1[:, 0:1024], PB[:, 0:1024],
                           SVC[:, SV_SUB + icc:SV_SUB + icc + 1], None,
                           OP_.add))
                else:
                    OP("VE", ve, ["PB", "HCB"], ["SM1"],
                       lambda e: e.tensor_tensor(
                           SM1[:, 0:1024], PB[:, 0:1024], HCB[:, 0:1024],
                           OP_.mult))
                OP("VE", ve, ["SM0", "SM1"], ["Cb"],
                   lambda e: e.tensor_tensor(
                       Cb[:, icc * 1024:(icc + 1) * 1024],
                       SM0[:, 0:1024], SM1[:, 0:1024], OP_.mult))
            VE(_)

        for occ in range(NDC):
            sl = occ % 2
            pn = "PA" if sl == 0 else "PB"
            wn = ["WT0", "WT1"][sl]

            def _(gp, e_=e_, occ=occ, sl=sl, wn=wn, shared=shared):
                src = sd_st[occ] if shared else wd_st[e_, occ]
                OP("GP", gp, [], [wn], lambda e: e.dma_start(
                    out=[WT0, WT1][sl][:, 0:1024], in_=src))
            GP(_)

            def _(te, e_=e_, occ=occ, sl=sl, pn=pn, wn=wn):
                OP("TE", te, [wn, "Cb"], [pn], lambda e: mmsteps(
                    e, PSUM[pn], 0, 1024,
                    [([WT0, WT1][sl][:, icc * 128:(icc + 1) * 128],
                      Cb[:, icc * 1024:(icc + 1) * 1024])
                     for icc in range(8)]))
            TE(_)

            if e_ == 0:
                def _(se, occ=occ, sl=sl, pn=pn):
                    OP("SE", se, [pn], ["Dg"] + DG_FAM[1:],
                       lambda e: e.activation(
                           Dg[:, occ * 1024:(occ + 1) * 1024],
                           PSUM[pn][:, 0:1024], AF.Copy))
                SE(_)
            else:
                def _(ve, occ=occ, sl=sl, pn=pn):
                    def em(e):
                        with nc.allow_low_precision(
                                reason="bf16 moe partial-sum acc, in tol"):
                            return e.tensor_tensor(
                                Dg[:, occ * 1024:(occ + 1) * 1024],
                                Dg[:, occ * 1024:(occ + 1) * 1024],
                                PSUM[pn][:, 0:1024], OP_.add)
                    OP("VE", ve, [pn, "Dg"], ["Dg"], em)
                VE(_)

    def _(se):
        OP("SE", se, ["Dg", "SVC"], ["Dg"], lambda e: [
            e.activation(Dg[:, occ * 1024:(occ + 1) * 1024],
                         Dg[:, occ * 1024:(occ + 1) * 1024], AF.Silu,
                         bias=SVC[:, SV_SDB + occ:SV_SDB + occ + 1])
            for occ in range(NDC)][-1:])
    SE(_)


    if DEBUG_DUMPS:
        def _(sy):
            OP("SY", sy, ["Dg"], ["dbg_g0"], lambda e: e.dma_start(
                out=dbg["dbg_g0"].ap(), in_=Dg[:, :]))
        SY(_)
    for cc in range(NDC):
        sl = cc % 2
        pn = "PA" if sl == 0 else "PB"
        wn = ["WT0", "WT1"][sl]

        def _(gp, cc=cc, sl=sl, wn=wn):
            OP("GP", gp, [], [wn], lambda e: e.dma_start(
                out=[WT0, WT1][sl][:, 0:2048], in_=wg_st[cc]))
        GP(_)

        def _(te, cc=cc, sl=sl, pn=pn, wn=wn):
            OP("TE", te, [wn, "A"], [pn], lambda e: mmsteps(
                e, PSUM[pn], 0, 1024,
                [([WT0, WT1][sl][:, dc * 128:(dc + 1) * 128],
                  A[:, dc * 1024:(dc + 1) * 1024]) for dc in range(NDC)]))
        TE(_)

        def _(se, cc=cc, sl=sl, pn=pn):
            OP("SE", se, [pn], CB_FAM, lambda e: e.activation(
                Cb[:, cc * 1024:(cc + 1) * 1024], PSUM[pn][:, 0:1024],
                AF.Silu))
        SE(_)

    def _(sy):
        OP("SY", sy, [], ["AM", "A", "AKV", "AE0", "AE1", "GS", "RB_A"],
           lambda e: e.dma_start(
               out=AMm, in_=am_t.ap().rearrange("j p c -> p j c")))
    SY(_)

    def emit_attention(layer):
        for h in range(16):
            if layer == 0:
                def _(sy, h=h):
                    def em(e):
                        outl = []
                        for r in range(2):
                            outl.append(e.dma_start(
                                out=AK[:, r * 1024:(r + 1) * 1024],
                                in_=ag0_out[r * 4096 + h * 128:
                                            r * 4096 + (h + 1) * 128, :]))
                        dvq = h // 8
                        for g in range(16):
                            r, tb = OWNER[g], LOCIDX[g]
                            r0 = r * 4096 + 2048 + (tb * 2 + dvq) * 128
                            c0 = (h % 8) * 128
                            outl.append(e.dma_start(
                                out=AVt[:, g * 128:(g + 1) * 128],
                                in_=ag0_out[r0:r0 + 128, c0:c0 + 128]))
                        return outl
                    OP("SY", sy, ["ag0_out"], ["AKV"], em)
                SY(_)

            for j in range(16):
                sl = j % 2
                pn = "PA" if sl == 0 else "PB"
                aen = ["AE0", "AE1"][sl]
                W = WJ[j]

                def _(te, h=h, j=j, sl=sl, pn=pn, W=W, layer=layer):
                    if layer == 0:
                        lh = AK[:, OWNER[j] * 1024 + LOCIDX[j] * 128:
                                OWNER[j] * 1024 + (LOCIDX[j] + 1) * 128]
                        kres = "AKV"
                    else:
                        gr = h // 4
                        lh = Cb[:, gr * 2048 + OWNER[j] * 1024
                                + LOCIDX[j] * 128:
                                gr * 2048 + OWNER[j] * 1024
                                + (LOCIDX[j] + 1) * 128]
                        kres = "Cb"
                    rh = Bq[:, h * 1024 + SOFF[j]:(h + 1) * 1024]
                    OP("TE", te, [kres, "Bq"], [pn], lambda e: mmsteps(
                        e, PSUM[pn], 0, W, [(lh, rh)]))
                TE(_)

                if layer == 0:
                    def _(ve, h=h, j=j, sl=sl, pn=pn, aen=aen, W=W):
                        m = min(256, W)

                        def em(e):
                            i1 = e.tensor_tensor(
                                AE[sl][:, 0:m], PSUM[pn][:, 0:m],
                                AMm[:, j * 256:j * 256 + m], OP_.mult)
                            if W > 256:
                                i1 = e.tensor_copy(AE[sl][:, 256:W],
                                                   PSUM[pn][:, 256:W])
                            return i1
                        OP("VE", ve, [pn, "AM"], [aen], em)
                    VE(_)
                else:
                    def _(se, h=h, j=j, sl=sl, pn=pn, aen=aen, W=W):
                        OP("SE", se, [pn], [aen], lambda e: e.activation(
                            AE[sl][:, 0:W], PSUM[pn][:, 0:W], AF.Exp,
                            scale=INV_SQRT_D))
                    SE(_)

                    def _(ve, h=h, j=j, sl=sl, aen=aen, W=W):
                        m = min(256, W)
                        OP("VE", ve, [aen, "AM"], [aen],
                           lambda e: e.tensor_tensor(
                               AE[sl][:, 0:m], AE[sl][:, 0:m],
                               AMm[:, j * 256:j * 256 + m], OP_.mult))
                    VE(_)

                def _(te, h=h, j=j, sl=sl, aen=aen, W=W, layer=layer):
                    if layer == 0:
                        lh = AVt[:, j * 128:(j + 1) * 128]
                        vres = "AKV"
                    else:
                        lh = Cb[:, 8192 + j * 512 + (h // 4) * 128:
                                8192 + j * 512 + (h // 4 + 1) * 128]
                        vres = "Cb"

                    def em(e):
                        last = None
                        for (a, b_) in col_chunks(SOFF[j], 1024):
                            lastj = 7 if a < 512 else 15
                            last = e.matmul(
                                PC[:, a:b_], lh,
                                AE[sl][:, a - SOFF[j]:b_ - SOFF[j]],
                                start=(j == 0), stop=(j == lastj))
                        if layer == 1:
                            for (a, b_) in col_chunks(SOFF[j], 1024):
                                lastj = 7 if a < 512 else 15
                                last = e.matmul(
                                    PD[0:16, a:b_], ECOLt(h),
                                    AE[sl][:, a - SOFF[j]:b_ - SOFF[j]],
                                    start=(h == 0 and j == 0),
                                    stop=(h == 15 and j == lastj))
                        return last
                    wr = ["PC"] if layer == 0 else ["PC", "PD"]
                    OP("TE", te, [aen, vres, "CONS"], wr, em)
                TE(_)

            def _(ve, h=h):
                OP("VE", ve, ["PC"], ["Bq"], lambda e: e.tensor_copy(
                    Bq[:, h * 1024:(h + 1) * 1024], PC[:, 0:1024]))
            VE(_)
    emit_attention(0)

    if DEBUG_DUMPS:
        def _(sy):
            OP("SY", sy, ["Bq"], ["dbg_y0"], lambda e: e.dma_start(
                out=dbg["dbg_y0"].ap(), in_=Bq[:, :]))
        SY(_)

    for h in range(16):
        sq = [SM0, SM1][h % 2]
        sqn = ["SM0", "SM1"][h % 2]

        def _(se, h=h, sq=sq, sqn=sqn):
            OP("SE", se, ["Bq"], [sqn], lambda e: e.activation(
                sq[:, 0:1024], Bq[:, h * 1024:(h + 1) * 1024], AF.Square))
        SE(_)

        def _(te, h=h, sq=sq, sqn=sqn):
            OP("TE", te, ["Bq", "CONS"], ["PA"], lambda e: mmsteps(
                e, PA, 0, 1024,
                [(ECOLt(h), Bq[:, h * 1024:(h + 1) * 1024])],
                start=(h == 0), stop=(h == 15), rows=16))
            OP("TE", te, [sqn, "CONS"], ["PB"], lambda e: mmsteps(
                e, PB, 0, 1024, [(ECOLt(h), sq[:, 0:1024])],
                start=(h == 0), stop=(h == 15), rows=16))
        TE(_)

    def _(ve):
        OP("VE", ve, ["PA"], ["GS"], lambda e: e.tensor_copy(
            GSA, PA[0:16, 0:1024]))
        OP("VE", ve, ["PB", "GS"], ["GS2"], lambda e: e.tensor_copy(
            GSB, PB[0:16, 0:1024]))
    VE(_)

    def _(gp):
        OP("GP", gp, ["GS"], ["STTa"], lambda e: [
            e.dma_start(
                out=STT[:, hh * 8:(hh + 1) * 8],
                in_=GSA[hh:hh + 1, :].rearrange("o (p i) -> o p i",
                                                p=128, i=8))
            for hh in range(16)])
        OP("GP", gp, ["GS2"], ["STTb"], lambda e: [
            e.dma_start(
                out=STT[:, 128 + hh * 8:128 + (hh + 1) * 8],
                in_=GSB[hh:hh + 1, :].rearrange("o (p i) -> o p i",
                                                p=128, i=8))
            for hh in range(16)])
    GP(_)

    def _(ve):
        OP("VE", ve, ["STTa"], ["STTa"], lambda e: e.tensor_scalar(
            STT[:, 0:128], STT[:, 0:128], 1.0 / 128.0, None, OP_.mult))
        OP("VE", ve, ["STTb"], ["STTb"], lambda e: e.tensor_scalar(
            STT[:, 128:256], STT[:, 128:256], 1.0 / 128.0, None, OP_.mult))
        for qq in range(4):
            OP("VE", ve, ["STTa"], ["AMX"], lambda e, qq=qq: e.tensor_tensor(
                AMX[:, 0:32], STT[:, qq * 32:(qq + 1) * 32],
                STT[:, qq * 32:(qq + 1) * 32], OP_.mult))
            OP("VE", ve, ["STTb", "AMX"], ["STTb"],
               lambda e, qq=qq: e.tensor_tensor(
                   STT[:, 128 + qq * 32:128 + (qq + 1) * 32],
                   STT[:, 128 + qq * 32:128 + (qq + 1) * 32],
                   AMX[:, 0:32], OP_.subtract))
        OP("VE", ve, ["STTb"], ["STTb"], lambda e: e.tensor_scalar(
            STT[:, 128:256], STT[:, 128:256], 1e-5, None, OP_.add))
    VE(_)

    def _(se):
        OP("SE", se, ["STTb"], ["STTb"], lambda e: e.activation(
            STT[:, 128:256], STT[:, 128:256], AF.Sqrt))
    SE(_)

    def _(ve):
        OP("VE", ve, ["STTb"], ["STTb"], lambda e: e.reciprocal(
            STT[:, 128:256], STT[:, 128:256]))
        OP("VE", ve, ["STTa", "STTb"], ["STTa"], lambda e: e.tensor_tensor(
            STT[:, 0:128], STT[:, 0:128], STT[:, 128:256], OP_.mult))
        OP("VE", ve, ["STTa"], ["STTa"], lambda e: e.tensor_scalar(
            STT[:, 0:128], STT[:, 0:128], -1.0, None, OP_.mult))
    VE(_)

    for h in range(16):
        def _(gp, h=h):
            OP("GP", gp, ["STTb"], ["RB_A"], lambda e: e.dma_start(
                out=RB_A, in_=STT[:, 128 + h * 8:128 + (h + 1) * 8]))
        GP(_)

        def _(te, h=h):
            OP("TE", te, ["RB_A", "CONS"], ["PA"], lambda e: mmsteps(
                e, PA, 0, 1024, [(ONER, RB_A)]))
        TE(_)

        def _(gp, h=h):
            OP("GP", gp, ["STTa"], ["RB_A"], lambda e: e.dma_start(
                out=RB_A, in_=STT[:, h * 8:(h + 1) * 8]))
        GP(_)

        def _(te, h=h):
            OP("TE", te, ["RB_A", "CONS"], ["PB"], lambda e: mmsteps(
                e, PB, 0, 1024, [(ONER, RB_A)]))
        TE(_)

        def _(ve, h=h):
            OP("VE", ve, ["Bq", "PA"], ["SM0"], lambda e: e.tensor_tensor(
                SM0[:, 0:1024], Bq[:, h * 1024:(h + 1) * 1024],
                PA[:, 0:1024], OP_.mult))
            OP("VE", ve, ["SM0", "PB"], ["SM1"], lambda e: e.tensor_tensor(
                SM1[:, 0:1024], SM0[:, 0:1024], PB[:, 0:1024], OP_.add))
            OP("VE", ve, ["SM1", "SVC"], ["SM0"], lambda e: e.tensor_scalar(
                SM0[:, 0:1024], SM1[:, 0:1024],
                SVC[:, SV_GNW + h:SV_GNW + h + 1],
                SVC[:, SV_GNB + h:SV_GNB + h + 1], OP_.mult, OP_.add))
            OP("VE", ve, ["SM0", "Cb"], ["Cb"], lambda e: e.tensor_tensor(
                Cb[:, h * 1024:(h + 1) * 1024],
                Cb[:, h * 1024:(h + 1) * 1024], SM0[:, 0:1024], OP_.mult))
        VE(_)


    if DEBUG_DUMPS:
        def _(sy):
            OP("SY", sy, ["Cb"], ["dbg_gated"], lambda e: e.dma_start(
                out=dbg["dbg_gated"].ap(), in_=Cb[:, :]))
        SY(_)
    for occ in range(NDC):
        sl = occ % 2
        pn = "PA" if sl == 0 else "PB"
        wn = ["WT0", "WT1"][sl]
        sqn = ["SM0", "SM1"][sl]

        def _(gp, occ=occ, sl=sl, wn=wn):
            OP("GP", gp, [], [wn], lambda e: e.dma_start(
                out=[WT0, WT1][sl][:, 0:2048], in_=wp_st[occ]))
        GP(_)

        def _(te, occ=occ, sl=sl, pn=pn, wn=wn):
            OP("TE", te, [wn, "Cb"], [pn], lambda e: mmsteps(
                e, PSUM[pn], 0, 1024,
                [([WT0, WT1][sl][:, cc * 128:(cc + 1) * 128],
                  Cb[:, cc * 1024:(cc + 1) * 1024]) for cc in range(NDC)]))
        TE(_)

        def _(ve, occ=occ, sl=sl, pn=pn, sqn=sqn):
            OP("VE", ve, [pn, "Dg"], [sqn], lambda e: e.tensor_tensor(
                [SM0, SM1][sl][:, 0:1024], PSUM[pn][:, 0:1024],
                Dg[:, occ * 1024:(occ + 1) * 1024], OP_.mult))
            OP("VE", ve, [sqn, "X"], ["X"], lambda e: e.tensor_tensor(
                X[:, occ * 1024:(occ + 1) * 1024],
                X[:, occ * 1024:(occ + 1) * 1024],
                [SM0, SM1][sl][:, 0:1024], OP_.add))
        VE(_)


    if DEBUG_DUMPS:
        def _(sy):
            OP("SY", sy, ["X"], ["dbg_x1"], lambda e: e.dma_start(
                out=dbg["dbg_x1"].ap(), in_=X[:, :]))
        SY(_)
    emit_layernorm(RB_D, "RB_D")

    def _(ve):
        OP("VE", ve, ["A"], ["AMX"], lambda e: [
            e.reduce_max(AMX[:, dc:dc + 1],
                         A[:, dc * 1024:(dc + 1) * 1024], axis=AX,
                         apply_absolute_value=True) for dc in range(NDC)][-1:])
        OP("VE", ve, ["AMX"], ["AMX2"], lambda e: e.reduce_max(
            AMX[:, 16:17], AMX[:, 0:16], axis=AX, apply_absolute_value=True))
    VE(_)

    def _(sy):
        OP("SY", sy, ["AMX2"], ["STTb"], lambda e: e.dma_start(
            out=STT[0:1, 128:256], in_=AMX[:, 16:17]))
    SY(_)

    def _(ve):
        OP("VE", ve, ["STTb", "XSR"], ["XSR"], lambda e: e.reduce_max(
            XSR[0:1, 0:1], STT[0:1, 128:256], axis=AX,
            apply_absolute_value=True))
    VE(_)

    def _(sy):
        OP("SY", sy, ["XSR"], ["ar_in"], lambda e: e.dma_start(
            out=ar_in[0:1, :], in_=XSR[0:1, 0:16]))
    SY(_)

    def _(gp):
        t = lastw.get("ar_in")
        if t and seen["GP"].get(t[0], 0) < t[1]:
            gp.wait_ge(sems[t[0]][0], t[1])
            seen["GP"][t[0]] = t[1]
        ins_ = gp.collective_compute(
            "AllReduce", OP_.max,
            replica_groups=[[0, 1, 2, 3, 4, 5, 6, 7]],
            ins=[ar_in.ap().opt()], outs=[ar_out.ap().opt()])
        sems["cAR"][1] += 1
        ins_.then_inc(sems["cAR"][0], 1)
        note("ar_out", ("cAR", sems["cAR"][1]))
    GP(_)

    for cc in range(NDC):
        sl = cc % 2
        pn = "PA" if sl == 0 else "PB"
        wn = ["WT0", "WT1"][sl]

        def _(gp, cc=cc, sl=sl, wn=wn):
            OP("GP", gp, [], [wn], lambda e: e.dma_start(
                out=[WT0, WT1][sl][:, 0:2048], in_=l1q_st[cc]))
        GP(_)

        def _(te, cc=cc, sl=sl, pn=pn, wn=wn):
            OP("TE", te, [wn, "A"], [pn], lambda e: mmsteps(
                e, PSUM[pn], 0, 1024,
                [([WT0, WT1][sl][:, dc * 128:(dc + 1) * 128],
                  A[:, dc * 1024:(dc + 1) * 1024]) for dc in range(NDC)]))
        TE(_)

        def _(se, cc=cc, sl=sl, pn=pn):
            OP("SE", se, [pn, "SVC"], ["Bq"], lambda e: e.activation(
                Bq[:, cc * 1024:(cc + 1) * 1024], PSUM[pn][:, 0:1024],
                AF.Identity, bias=SVC[:, SV_BQ + cc:SV_BQ + cc + 1]))
        SE(_)

    for kc in range(4):
        sl = kc % 2
        pn = "PA" if sl == 0 else "PB"
        wn = ["WT0", "WT1"][sl]
        sqn = ["SM0", "SM1"][sl]

        def _(gp, kc=kc, sl=sl, wn=wn):
            OP("GP", gp, [], [wn], lambda e: e.dma_start(
                out=[WT0, WT1][sl][:, 0:2048], in_=l1k_st[kc]))
        GP(_)

        def _(te, kc=kc, sl=sl, pn=pn, wn=wn):
            OP("TE", te, [wn, "A"], [pn], lambda e: mmsteps(
                e, PSUM[pn], 0, 1024,
                [([WT0, WT1][sl][:, dc * 128:(dc + 1) * 128],
                  A[:, dc * 1024:(dc + 1) * 1024]) for dc in range(NDC)]))
        TE(_)

        def _(se, kc=kc, sl=sl, pn=pn, sqn=sqn):
            OP("SE", se, [pn, "SVC"], [sqn], lambda e: e.activation(
                [SM0, SM1][sl][:, 0:1024], PSUM[pn][:, 0:1024],
                AF.Identity, bias=SVC[:, SV_BK + kc:SV_BK + kc + 1]))
        SE(_)

        def _(sy, kc=kc, sl=sl, sqn=sqn):
            OP("SY", sy, [sqn], ["ag1_in"], lambda e: e.dma_start(
                out=ag1_in[kc * 128:(kc + 1) * 128, :],
                in_=[SM0, SM1][sl][:, 0:1024]))
        SY(_)

    def _(gp):
        OP("GP", gp, [], CB_FAM, lambda e: [
            e.dma_start(out=Cb[:, dc * 512:(dc + 1) * 512], in_=l1v_mov[dc])
            for dc in range(NDC)])
        OP("GP", gp, [], ["RB_D"], lambda e: e.dma_start(
            out=RB_D[0:1, 0:512], in_=bv_row.ap()))
    GP(_)

    def _(te):
        OP("TE", te, ["RB_D", "CONS"], ["PD"], lambda e: mmsteps(
            e, PD, 0, 512, [(ONER, RB_D[0:1, 0:512])]))
    TE(_)

    def _(ve):
        OP("VE", ve, ["PD"], ["BVC"], lambda e: e.tensor_copy(
            BVC[:, 0:512], PD[:, 0:512]))
    VE(_)

    for tb in range(8):
        sl = tb % 2
        pn = "PA" if sl == 0 else "PB"
        sqn = ["SM0", "SM1"][sl]

        def _(te, tb=tb, sl=sl, pn=pn):
            OP("TE", te, ["Cb", "A"], [pn], lambda e: mmsteps(
                e, PSUM[pn], 0, 512,
                [(A[:, dc * 1024 + tb * 128:dc * 1024 + (tb + 1) * 128],
                  Cb[:, dc * 512:(dc + 1) * 512]) for dc in range(NDC)]))
        TE(_)

        def _(ve, tb=tb, sl=sl, pn=pn, sqn=sqn):
            OP("VE", ve, [pn, "BVC"], [sqn], lambda e: e.tensor_tensor(
                [SM0, SM1][sl][:, 0:512], PSUM[pn][:, 0:512],
                BVC[:, 0:512], OP_.add))
        VE(_)

        def _(sy, tb=tb, sl=sl, sqn=sqn):
            r0 = 512 + (tb // 2) * 128
            c0 = (tb % 2) * 512
            OP("SY", sy, [sqn], ["ag1_in"], lambda e: e.dma_start(
                out=ag1_in[r0:r0 + 128, c0:c0 + 512],
                in_=[SM0, SM1][sl][:, 0:512]))
        SY(_)

    def _(gp):
        t = lastw.get("ag1_in")
        if t and seen["GP"].get(t[0], 0) < t[1]:
            gp.wait_ge(sems[t[0]][0], t[1])
            seen["GP"][t[0]] = t[1]
        ins_ = gp.collective_compute(
            "AllGather", OP_.bypass,
            replica_groups=[[0, 1], [2, 3], [4, 5], [6, 7]],
            ins=[ag1_in.ap().opt()], outs=[ag1_out.ap().opt()])
        sems["cAG1"][1] += 1
        ins_.then_inc(sems["cAG1"][0], 1)
        note("ag1_out", ("cAG1", sems["cAG1"][1]))
    GP(_)


    if DEBUG_DUMPS:
        def _(sy):
            OP("SY", sy, ["Bq"], ["dbg_q1"], lambda e: e.dma_start(
                out=dbg["dbg_q1"].ap(), in_=Bq[:, :]))
        SY(_)
    for ncc in range(NDC):
        sl = ncc % 2
        wn = ["WT0", "WT1"][sl]

        def _(sy, ncc=ncc, sl=sl, wn=wn):
            OP("SY", sy, [], [wn], lambda e: e.dma_start(
                out=[WT0, WT1][sl][:, 0:2048], in_=l1w_st[ncc]))
        SY(_)

        def _(ve, ncc=ncc, sl=sl, wn=wn):
            OP("VE", ve, [wn, "AMX2"], ["AMX"], lambda e: e.reduce_max(
                AMX[:, ncc:ncc + 1], [WT0, WT1][sl][:, 0:2048], axis=AX,
                apply_absolute_value=True))
        VE(_)

    def _(ve):
        OP("VE", ve, ["AMX"], ["AMX2"], lambda e: e.reduce_max(
            AMX[:, 17:18], AMX[:, 0:16], axis=AX, apply_absolute_value=True))
    VE(_)

    def _(sy):
        OP("SY", sy, ["AMX2"], ["STTb"], lambda e: e.dma_start(
            out=STT[0:1, 128:256], in_=AMX[:, 17:18]))
        OP("SY", sy, ["ar_out"], ["XSR"], lambda e: e.dma_start(
            out=XSR[0:1, 0:16], in_=ar_out[0:1, :]))
    SY(_)

    def _(ve):
        OP("VE", ve, ["STTb", "XSR"], ["XSR"], lambda e: e.reduce_max(
            XSR[0:1, 1:2], STT[0:1, 128:256], axis=AX,
            apply_absolute_value=True))
        OP("VE", ve, ["XSR"], ["XSR"], lambda e: e.reciprocal(
            XSR[0:1, 2:3], XSR[0:1, 0:1]))
        OP("VE", ve, ["XSR"], ["XSR"], lambda e: e.reciprocal(
            XSR[0:1, 3:4], XSR[0:1, 1:2]))
        OP("VE", ve, ["XSR"], ["XSR"], lambda e: e.tensor_tensor(
            XSR[0:1, 4:5], XSR[0:1, 0:1], XSR[0:1, 1:2], OP_.mult))
    VE(_)

    def _(sy):
        OP("SY", sy, ["XSR"], ["scl_dram"], lambda e: e.dma_start(
            out=scl_dram[0:1, 0:3], in_=XSR[0:1, 2:5]))
        srcap = bass.AP(scl_dram, 0, [[0, 128], [1, 3]])
        OP("SY", sy, ["scl_dram"], ["SCL"], lambda e: e.dma_start(
            out=SCL[:, 0:3], in_=srcap))
    SY(_)

    def _(ve):
        for dc in range(NDC):
            for q4 in range(4):
                c0 = dc * 1024 + q4 * 256
                OP("VE", ve, ["A", "SCL"], ["SM8"],
                   lambda e, c0=c0: e.tensor_scalar(
                       SM8[:, 0:256], A[:, c0:c0 + 256], SCL[:, 0:1], None,
                       OP_.mult))
                OP("VE", ve, ["SM8"], CB_FAM if (dc == 0 and q4 == 0)
                   else ["Cb"],
                   lambda e, c0=c0: e.tensor_copy(Cb[:, c0:c0 + 256],
                                                  SM8[:, 0:256]))
    VE(_)

    for ncc in range(NDC):
        sl = ncc % 2
        pn = "PA" if sl == 0 else "PB"
        wn = ["WT0", "WT1"][sl]
        sqn = ["SM0", "SM1"][sl]

        def _(sy, ncc=ncc, sl=sl, wn=wn):
            OP("SY", sy, [], [wn], lambda e: e.dma_start(
                out=[WT0, WT1][sl][:, 0:2048], in_=l1w_st[ncc]))
        SY(_)

        def _(ve, ncc=ncc, sl=sl, wn=wn):
            for q8 in range(8):
                c0 = q8 * 256
                OP("VE", ve, [wn, "SCL"], ["SM8"],
                   lambda e, c0=c0: e.tensor_scalar(
                       SM8[:, 0:256], [WT0, WT1][sl][:, c0:c0 + 256],
                       SCL[:, 1:2], None, OP_.mult))
                OP("VE", ve, ["SM8"], [wn],
                   lambda e, c0=c0: e.tensor_copy(
                       [WT0, WT1][sl][:, c0:c0 + 256], SM8[:, 0:256]))
        VE(_)

        def _(te, ncc=ncc, sl=sl, pn=pn, wn=wn):
            OP("TE", te, [wn, "Cb"], [pn], lambda e: mmsteps(
                e, PSUM[pn], 0, 1024,
                [([WT0, WT1][sl][:, kcc * 128:(kcc + 1) * 128],
                  Cb[:, kcc * 1024:(kcc + 1) * 1024])
                 for kcc in range(NDC)]))
        TE(_)

        def _(ve, ncc=ncc, sl=sl, pn=pn, sqn=sqn):
            OP("VE", ve, [pn, "SCL"], [sqn], lambda e: e.tensor_scalar(
                [SM0, SM1][sl][:, 0:1024], PSUM[pn][:, 0:1024],
                SCL[:, 2:3], None, OP_.mult))
        VE(_)

        def _(se, ncc=ncc, sl=sl, sqn=sqn):
            OP("SE", se, [sqn, "SVCb"],
               ["Dg"] + DG_FAM[1:] if ncc == 0 else ["Dg"],
               lambda e: e.activation(
                   Dg[:, ncc * 1024:(ncc + 1) * 1024],
                   [SM0, SM1][sl][:, 0:1024], AF.Silu,
                   bias=SVC[:, SV_L1B + ncc:SV_L1B + ncc + 1]))
        SE(_)


    if DEBUG_DUMPS:
        def _(sy):
            OP("SY", sy, ["Dg"], ["dbg_g1"], lambda e: e.dma_start(
                out=dbg["dbg_g1"].ap(), in_=Dg[:, :]))
        SY(_)
    def _(sy):
        def em(e):
            outl = []
            for r in range(2):
                for kc in range(4):
                    outl.append(e.dma_start(
                        out=Cb[:, kc * 2048 + r * 1024:
                               kc * 2048 + (r + 1) * 1024],
                        in_=ag1_out[r * 1024 + kc * 128:
                                    r * 1024 + (kc + 1) * 128, :]))
            for g in range(16):
                r, tb = OWNER[g], LOCIDX[g]
                r0 = r * 1024 + 512 + (tb // 2) * 128
                c0 = (tb % 2) * 512
                outl.append(e.dma_start(
                    out=Cb[:, 8192 + g * 512:8192 + (g + 1) * 512],
                    in_=ag1_out[r0:r0 + 128, c0:c0 + 512]))
            return outl
        OP("SY", sy, ["ag1_out"], CB_FAM, em)
        OP("SY", sy, [], ["AM", "A", "AKV", "AE0", "AE1", "GS", "RB_A"],
           lambda e: e.dma_start(
               out=AMm, in_=am_t.ap().rearrange("j p c -> p j c")))
    SY(_)

    emit_attention(1)

    if DEBUG_DUMPS:
        def _(sy):
            OP("SY", sy, ["Bq"], ["dbg_y1"], lambda e: e.dma_start(
                out=dbg["dbg_y1"].ap(), in_=Bq[:, :]))
        SY(_)

    def _(ve):
        OP("VE", ve, ["PD"], ["GS"], lambda e: e.tensor_copy(
            GSA, PD[0:16, 0:1024]))

        def em(e):
            with nc.allow_low_precision(
                    reason="bf16 softmax denominators, in tol"):
                return e.reciprocal(GSB, GSA)
        OP("VE", ve, ["GS"], ["GS2"], em)
    VE(_)

    for h in range(16):
        def _(sy, h=h):
            OP("SY", sy, ["GS2"], ["RB_A"], lambda e: e.dma_start(
                out=RB_A, in_=GSB[h:h + 1, 0:1024]))
        SY(_)

        def _(te, h=h):
            OP("TE", te, ["RB_A", "CONS"], ["PA"], lambda e: mmsteps(
                e, PA, 0, 1024, [(ONER, RB_A)]))
        TE(_)

        def _(ve, h=h):
            OP("VE", ve, ["Bq", "PA"], ["SM0"], lambda e: e.tensor_tensor(
                SM0[:, 0:1024], Bq[:, h * 1024:(h + 1) * 1024],
                PA[:, 0:1024], OP_.mult))
            OP("VE", ve, ["SM0", "Dg"], ["SM1"], lambda e: e.tensor_tensor(
                SM1[:, 0:1024], SM0[:, 0:1024],
                Dg[:, h * 1024:(h + 1) * 1024], OP_.mult))
            OP("VE", ve, ["SM1", "X"], ["X"], lambda e: e.tensor_tensor(
                X[:, h * 1024:(h + 1) * 1024],
                X[:, h * 1024:(h + 1) * 1024], SM1[:, 0:1024], OP_.add))
        VE(_)

    def _(sy):
        OP("SY", sy, ["X"], ["xout"], lambda e: [
            e.dma_start(out=xout[dc], in_=X[:, dc * 1024:(dc + 1) * 1024])
            for dc in range(NDC)])
        sy.wait_ge(sems["qSY_xout"][0], sems["qSY_xout"][1])
    SY(_)

    blk.__exit__(None, None, None)
    for t in reversed(ctx[:-1]):
        try:
            t.__exit__(None, None, None)
        except Exception:
            pass
    return nc


def _prep_weights(ins):
    f = np.float32
    out = {}
    wqkv = np.asarray(ins["L0_attn_w_qkv"], f)
    wq = wqkv[:, 0:2048].reshape(16, 128, 16, 128).transpose(2, 1, 0, 3)
    wk = wqkv[:, 2048:4096].reshape(16, 128, 16, 128).transpose(2, 1, 0, 3)
    out["wqkv_st"] = np.ascontiguousarray(
        np.concatenate([wq, wk], axis=0).reshape(32, 128, 2048))
    out["wv_mov"] = np.ascontiguousarray(
        wqkv[:, 4096:6144].reshape(16, 128, 2048))
    out["wg_st"] = np.ascontiguousarray(
        np.asarray(ins["L0_attn_w_g"], f).reshape(16, 128, 16, 128)
        .transpose(2, 1, 0, 3).reshape(16, 128, 2048))
    out["wp_st"] = np.ascontiguousarray(
        np.asarray(ins["L0_attn_w_proj"], f).reshape(16, 128, 16, 128)
        .transpose(2, 1, 0, 3).reshape(16, 128, 2048))
    out["rw_st"] = np.ascontiguousarray(
        np.asarray(ins["L0_ffn_router_w"], f).T.reshape(16, 128, 8))
    wgu = np.stack([np.asarray(ins["L0_ffn_wg"], f),
                    np.asarray(ins["L0_ffn_wu"], f)], axis=0)
    out["wgu_st"] = np.ascontiguousarray(
        wgu.reshape(2, 8, 16, 128, 8, 128).transpose(0, 1, 4, 3, 2, 5)
        .reshape(2, 8, 8, 128, 2048))
    out["wd_st"] = np.ascontiguousarray(
        np.asarray(ins["L0_ffn_wd"], f).reshape(8, 8, 128, 16, 128)
        .transpose(0, 3, 2, 1, 4).reshape(8, 16, 128, 1024))
    sgu = np.stack([np.asarray(ins["L0_ffn_sg_w"], f),
                    np.asarray(ins["L0_ffn_su_w"], f)], axis=0)
    out["sgu_st"] = np.ascontiguousarray(
        sgu.reshape(2, 16, 128, 8, 128).transpose(0, 3, 2, 1, 4)
        .reshape(2, 8, 128, 2048))
    out["sd_st"] = np.ascontiguousarray(
        np.asarray(ins["L0_ffn_sd_w"], f).reshape(8, 128, 16, 128)
        .transpose(2, 1, 0, 3).reshape(16, 128, 1024))
    out["l1q_st"] = np.ascontiguousarray(
        np.asarray(ins["L1_attn_Wq"], f).reshape(16, 16, 128, 128)
        .transpose(0, 2, 1, 3).reshape(16, 128, 2048))
    out["l1k_st"] = np.ascontiguousarray(
        np.asarray(ins["L1_attn_Wk"], f).reshape(4, 16, 128, 128)
        .transpose(0, 2, 1, 3).reshape(4, 128, 2048))
    out["l1v_mov"] = np.ascontiguousarray(
        np.asarray(ins["L1_attn_Wv"], f).reshape(4, 16, 128, 128)
        .transpose(1, 2, 0, 3).reshape(16, 128, 512))
    out["bv_row"] = np.ascontiguousarray(
        np.asarray(ins["L1_attn_bv"], f).reshape(1, 512))
    w1 = np.asarray(ins["L1_ffn_weight"])
    out["l1w_st"] = np.ascontiguousarray(
        w1.reshape(16, 128, 16, 128).transpose(0, 3, 2, 1)
        .reshape(16, 128, 2048))
    out["l1b_bf"] = np.ascontiguousarray(
        np.asarray(ins["L1_ffn_bias"]).reshape(16, 128).T)

    sv = np.zeros((128, 128), f)
    sv[:, SV_GNW:SV_GNW + 16] = np.asarray(
        ins["L0_attn_gn_weight"], f).reshape(16, 128).T
    sv[:, SV_GNB:SV_GNB + 16] = np.asarray(
        ins["L0_attn_gn_bias"], f).reshape(16, 128).T
    sv[:, SV_SGB:SV_SGB + 8] = np.asarray(
        ins["L0_ffn_sg_b"], f).reshape(8, 128).T
    sv[:, SV_SUB:SV_SUB + 8] = np.asarray(
        ins["L0_ffn_su_b"], f).reshape(8, 128).T
    sv[:, SV_SDB:SV_SDB + 16] = np.asarray(
        ins["L0_ffn_sd_b"], f).reshape(16, 128).T
    sv[:, SV_BQ:SV_BQ + 16] = np.asarray(
        ins["L1_attn_bq"], f).reshape(16, 128).T
    sv[:, SV_BK:SV_BK + 4] = np.asarray(
        ins["L1_attn_bk"], f).reshape(4, 128).T
    out["smallw"] = sv
    return out


def _xpos_tables(parity):
    f = np.float64
    Cd = 2048
    pos = np.array([ZIG[parity][t // 128] * 128 + (t % 128)
                    for t in range(TPC)], f)
    i2 = np.arange(0, Cd, 2, dtype=f)
    dim_scale = (i2 + 0.4 * Cd) / (1.4 * Cd)
    scale = dim_scale[None, :] ** (pos[:, None] / 512.0)
    inv_freq = 1.0 / 10000.0 ** (i2 / Cd)
    sinu = pos[:, None] * inv_freq[None, :]
    sin = np.repeat(np.sin(sinu) * scale, 2, axis=1)
    cos = np.repeat(np.cos(sinu) * scale, 2, axis=1)
    gam = 1.0 - 2.0 ** (-5.0 - np.arange(16, dtype=f))
    gq = gam[np.arange(Cd) // 128][:, None] ** pos[None, :]
    gk = gam[np.arange(Cd) // 128][:, None] ** (-pos[None, :])
    cq = cos.T * gq * (128.0 ** -0.5)
    sq = sin.T * gq * (128.0 ** -0.5)
    ck = cos.T * gk
    sk = sin.T * gk

    def tile(m):
        return m.reshape(16, 128, TPC).astype(ml_dtypes.bfloat16)
    tq = np.stack([tile(cq), tile(sq)], axis=1)
    tk = np.stack([tile(ck), tile(sk)], axis=1)
    return np.ascontiguousarray(tq), np.ascontiguousarray(tk)


def _mask_tables(parity):
    am = np.zeros((16, 128, 256), np.float32)
    zz = ZIG[parity]
    tri = (np.arange(128)[:, None] <= np.arange(128)[None, :]).astype(
        np.float32)
    for j in range(16):
        L = LPROF[j]
        for p01 in range(min(2, L)):
            gb = zz[8 - L + p01]
            if gb < j:
                blkm = np.zeros((128, 128), np.float32)
            elif gb == j:
                blkm = tri
            else:
                blkm = np.ones((128, 128), np.float32)
            am[j, :, p01 * 128:(p01 + 1) * 128] = blkm
    return np.ascontiguousarray(am.astype(ml_dtypes.bfloat16))


def _consts_arr():
    c = np.zeros((128, 648), np.float32)
    for i in range(64):
        c[2 * i + 1, 2 * i] = -1.0
        c[2 * i, 2 * i + 1] = 1.0
    for h in range(16):
        c[:, CN_ECOL + h * 16 + h] = 1.0
    c[:, CN_ONE] = 1.0
    c[0, CN_ONER:CN_ONER + 128] = 1.0
    return np.ascontiguousarray(c.astype(ml_dtypes.bfloat16))


_NC_CACHE = {}


def kernel(**ins):
    if "nc" not in _NC_CACHE:
        _NC_CACHE["nc"] = build_program()
    nc = _NC_CACHE["nc"]

    w = _prep_weights(ins)
    consts = _consts_arr()
    identf = np.eye(128, dtype=np.float32)
    tabs = [_xpos_tables(0), _xpos_tables(1)]
    ams = [_mask_tables(0), _mask_tables(1)]

    x = np.asarray(ins["L0_attn_x"], np.float32)
    in_maps = []
    for core in range(8):
        b, par = core // 2, core % 2
        xs = np.empty((16, 128, TPC), np.float32)
        for lb in range(8):
            g = ZIG[par][lb]
            blkx = x[b, g * 128:(g + 1) * 128, :]
            xs[:, :, lb * 128:(lb + 1) * 128] = blkx.T.reshape(16, 128, 128)
        m = dict(w)
        m["x0"] = xs
        m["tabq"], m["tabk"] = tabs[par]
        m["am_t"] = ams[par]
        m["consts"] = consts
        m["identf"] = identf
        in_maps.append(m)

    res = run_bass_kernel_spmd(nc, in_maps, core_ids=list(range(8)))

    out = np.empty((4, 2048, 2048), np.float32)
    for core in range(8):
        b, par = core // 2, core % 2
        xo = res.results[core]["xout"]
        for lb in range(8):
            g = ZIG[par][lb]
            out[b, g * 128:(g + 1) * 128, :] = \
                xo[:, :, lb * 128:(lb + 1) * 128].reshape(2048, 128).T
    return out


if __name__ == "__main__":
    nc = build_program()
    print("build ok:", sum(1 for _ in nc.all_instructions()), "instructions")


# version 07
# speedup vs baseline: 1.1719x; 1.1719x over previous
import math
import numpy as np
import ml_dtypes
import concourse.bass as bass
import concourse.mybir as mybir
from concourse.bass_utils import run_bass_kernel_spmd

F32 = mybir.dt.float32
BF16 = mybir.dt.bfloat16
FP8 = mybir.dt.float8e4
AX = mybir.AxisListType.X
OP_ = mybir.AluOpType
AF = mybir.ActivationFunctionType

NDC = 16
TPC = 1024
DEBUG_DUMPS = False
ZIG = [[0, 3, 4, 7, 8, 11, 12, 15], [1, 2, 5, 6, 9, 10, 13, 14]]
OWNER = [0 if g in ZIG[0] else 1 for g in range(16)]
LOCIDX = [ZIG[OWNER[g]].index(g) for g in range(16)]
LEN = [[len([b_ for b_ in ZIG[p] if b_ >= j]) for j in range(16)]
       for p in (0, 1)]
LPROF = [max(LEN[0][j], LEN[1][j]) for j in range(16)]
SOFF = [(8 - LPROF[j]) * 128 for j in range(16)]
WJ = [LPROF[j] * 128 for j in range(16)]
INV_SQRT_D = 1.0 / math.sqrt(128.0)

SV_GNW, SV_GNB, SV_SGB, SV_SUB, SV_SDB, SV_BQ, SV_BK, SV_L1B = \
    0, 16, 32, 40, 48, 64, 80, 84
CN_PSWAP, CN_ECOL, CN_ONE, CN_ONER, CN_RWB = 0, 128, 384, 385, 516


def col_chunks(c0, c1):
    out = []
    c = c0
    while c < c1:
        nxt = min(c1, (c // 512 + 1) * 512)
        out.append((c, nxt))
        c = nxt
    return out


def build_program():
    nc = bass.Bass("TRN2", target_bir_lowering=False)

    def dram_in(name, shape, dt=F32):
        return nc.dram_tensor(name, shape, dt, kind="ExternalInput")

    x0 = dram_in("x0", [NDC, 128, TPC])
    wqkv_st = dram_in("wqkv_st", [32, 128, 2048])
    wv_mov = dram_in("wv_mov", [16, 128, 2048])
    wg_st = dram_in("wg_st", [16, 128, 2048])
    wp_st = dram_in("wp_st", [16, 128, 2048])
    rw_st = dram_in("rw_st", [16, 128, 8])
    wgu_st = dram_in("wgu_st", [2, 8, 8, 128, 2048])
    wd_st = dram_in("wd_st", [8, 16, 128, 1024])
    sgu_st = dram_in("sgu_st", [2, 8, 128, 2048])
    sd_st = dram_in("sd_st", [16, 128, 1024])
    smallw = dram_in("smallw", [128, 128])
    l1q_st = dram_in("l1q_st", [16, 128, 2048])
    l1k_st = dram_in("l1k_st", [4, 128, 2048])
    l1v_mov = dram_in("l1v_mov", [16, 128, 512])
    bv_row = dram_in("bv_row", [1, 512])
    l1w_st = dram_in("l1w_st", [16, 128, 2048], BF16)
    l1b_bf = dram_in("l1b_bf", [128, 16], BF16)
    tabq = dram_in("tabq", [16, 2, 128, TPC], BF16)
    tabk = dram_in("tabk", [16, 2, 128, TPC], BF16)
    am_t = dram_in("am_t", [16, 128, 256], BF16)
    consts = dram_in("consts", [128, 648], BF16)
    identf = dram_in("identf", [128, 128])

    ag0_in = nc.dram_tensor("ag0_in", [4096, 1024], BF16)
    ag0_out = nc.dram_tensor("ag0_out", [8192, 1024], BF16)
    ag1_in = nc.dram_tensor("ag1_in", [1024, 1024], BF16)
    ag1_out = nc.dram_tensor("ag1_out", [2048, 1024], BF16)
    ar_in = nc.dram_tensor("ar_in", [1, 16], F32)
    ar_out = nc.dram_tensor("ar_out", [1, 16], F32)
    scl_dram = nc.dram_tensor("scl_dram", [1, 4], F32)

    xout = nc.dram_tensor("xout", [NDC, 128, TPC], F32, kind="ExternalOutput")
    dbg = {}
    if DEBUG_DUMPS:
        for nm, dt_ in [("dbg_xn1", BF16), ("dbg_q0", BF16), ("dbg_y0", BF16),
                        ("dbg_g0", BF16), ("dbg_gated", BF16),
                        ("dbg_x1", F32), ("dbg_q1", BF16), ("dbg_y1", BF16),
                        ("dbg_g1", BF16)]:
            dbg[nm] = nc.dram_tensor(nm, [128, NDC * 1024], dt_)

    ctx = []

    def sb(name, shape, dt):
        t = nc.sbuf_tensor(name, shape, dt)
        ctx.append(t)
        return t.__enter__()

    def psum(name, shape, dt=F32):
        t = nc.psum_tensor(name, shape, dt)
        ctx.append(t)
        return t.__enter__()

    X = sb("X", [128, NDC * 1024], F32)
    A = sb("A", [128, NDC * 1024], BF16)
    Bq = sb("Bq", [128, NDC * 1024], BF16)
    Cb = sb("Cb", [128, NDC * 1024], BF16)
    Dg = sb("Dg", [128, NDC * 1024], BF16)
    WT0 = sb("WT0", [128, 2048], BF16)
    WT1 = sb("WT1", [128, 2048], BF16)
    SM0 = sb("SM0", [128, 1024], BF16)
    SM1 = sb("SM1", [128, 1024], BF16)
    SVC = sb("SVC", [128, 128], F32)
    CONS = sb("CONS", [128, 648], BF16)
    IDF = sb("IDF", [128, 128], F32)
    STT = sb("STT", [128, 256], F32)
    XSR = sb("XSR", [1, 16], F32)
    AMX = sb("AMX", [128, 32], F32)
    SM8 = sb("SM8", [128, 256], FP8)
    SCL = sb("SCL", [128, 4], F32)

    PA = psum("PA", [128, 1024])
    PB = psum("PB", [128, 1024])
    PC = psum("PC", [128, 1024])
    PD = psum("PD", [128, 1024])
    PSUM = {"PA": PA, "PB": PB, "PC": PC, "PD": PD}

    ONES_COL = CONS[:, CN_ONE:CN_ONE + 1]
    ONER = CONS[0:1, CN_ONER:CN_ONER + 128]
    PSWAP = CONS[:, CN_PSWAP:CN_PSWAP + 128]
    RWB = CONS[:, CN_RWB:CN_RWB + 128]

    def ECOLt(h):
        return CONS[:, CN_ECOL + h * 16:CN_ECOL + (h + 1) * 16]

    TBL = [Dg[:, 0:2048], Dg[:, 2048:4096]]
    T1S = [Dg[:, 4096:5120], Dg[:, 5120:6144]]
    RB_D = Dg[0:1, 6144:7168]
    BVC = Dg[:, 4096:4608]
    HCB = Cb[:, 14 * 1024:15 * 1024]
    RB_C = Cb[0:1, 13 * 1024:13 * 1024 + 1024]
    CT = Cb[0:8, 15 * 1024:16 * 1024]
    AK = A[:, 0:2048]
    AVt = A[:, 2048:4096]
    AE = [A[:, 4096:5120], A[:, 5120:6144]]
    AMm = A[:, 6144:10240]
    GSA = A[0:16, 10240:11264]
    GSB = A[0:16, 11264:12288]
    RB_A = A[0:1, 12288:13312]

    A_FAM = ["A", "AKV", "AE0", "AE1", "AM", "GS", "RB_A"]
    CB_FAM = ["Cb", "HCB", "RB_C", "CT"]
    DG_FAM = ["Dg", "TBL0", "TBL1", "T1S0", "T1S1", "RB_D", "BVC"]

    blk = nc.Block()
    blk.__enter__()
    ctx.append(blk)

    sems = {}

    def sem(name):
        h = nc.semaphore(name)
        ctx.append(h)
        sems[name] = [h.__enter__(), 0]

    for s in ["eT", "eV", "eS", "cAG0", "cAG1", "cAR"]:
        sem(s)

    ENGSEM = {"TE": "eT", "VE": "eV", "SE": "eS"}
    DMAE = {"SY", "GP", "SQ"}

    lastw = {}
    readers = {}
    seen = {k: {} for k in ("TE", "VE", "SE", "SY", "GP", "SQ")}
    PARENT = {"TBL0": "Dg", "TBL1": "Dg", "T1S0": "Dg", "T1S1": "Dg",
              "RB_D": "Dg", "BVC": "Dg", "HCB": "Cb", "RB_C": "Cb",
              "CT": "Cb", "AKV": "A", "AE0": "A", "AE1": "A", "AM": "A",
              "GS": "A", "GS2": "A", "RB_A": "A"}

    def OP(engname, eng, reads, writes, emit, n_dma=1):
        toks = []
        for r in reads:
            t = lastw.get(r)
            if t:
                toks.append(t)
            p = PARENT.get(r)
            if p:
                t = lastw.get(p)
                if t:
                    toks.append(t)
        for w in writes:
            t = lastw.get(w)
            if t:
                toks.append(t)
            toks.extend(readers.get(w, ()))
            p = PARENT.get(w)
            if p:
                t = lastw.get(p)
                if t:
                    toks.append(t)
                toks.extend(readers.get(p, ()))
        agg = {}
        for (s, v) in toks:
            agg[s] = max(agg.get(s, 0), v)
        for s, v in sorted(agg.items()):
            if seen[engname].get(s, 0) < v:
                eng.wait_ge(sems[s][0], v)
                seen[engname][s] = v
        out = emit(eng)
        if engname in DMAE:
            sname = "q%s_%s" % (engname, writes[0])
            if sname not in sems:
                sem(sname)
            step = 16
        else:
            sname = ENGSEM[engname]
            step = 1
        if isinstance(out, (list, tuple)):
            for ins_ in out:
                sems[sname][1] += step
                ins_.then_inc(sems[sname][0], step)
        else:
            sems[sname][1] += step
            out.then_inc(sems[sname][0], step)
        tok = (sname, sems[sname][1])
        for r in reads:
            readers.setdefault(r, []).append(tok)
        for w in writes:
            lastw[w] = tok
            readers[w] = []
        return tok

    def TOK(name, v=None):
        return (name, sems[name][1] if v is None else v)

    def note(res, tok):
        lastw[res] = tok
        readers[res] = []

    def TE(fn):
        blk.tensor(fn)

    def VE(fn):
        blk.vector(fn)

    def SE(fn):
        blk.scalar(fn)

    def GP(fn):
        blk.gpsimd(fn)

    def SY(fn):
        blk.sync(fn)

    def mmsteps(te, P, c0, c1, steps, rbase=None, start=True, stop=True,
                rows=128):
        rb = c0 if rbase is None else rbase
        n = len(steps)
        last = None
        for (a, b_) in col_chunks(c0, c1):
            for i, (lh, rh) in enumerate(steps):
                last = te.matmul(P[0:rows, a:b_], lh, rh[:, a - rb:b_ - rb],
                                 start=(start and i == 0),
                                 stop=(stop and i == n - 1))
        return last

    def _(sy):
        OP("SY", sy, [], ["X"], lambda e: [
            e.dma_start(out=X[:, dc * 1024:(dc + 1) * 1024], in_=x0[dc])
            for dc in range(NDC)])
        OP("SY", sy, [], ["SVC"], lambda e: e.dma_start(
            out=SVC[:, 0:SV_L1B], in_=smallw[0:128, 0:SV_L1B]))
        OP("SY", sy, [], ["CONS"], lambda e: e.dma_start(
            out=CONS[:, 0:CN_RWB], in_=consts[0:128, 0:CN_RWB]))
        OP("SY", sy, [], ["IDF"], lambda e: e.dma_start(
            out=IDF[:, :], in_=identf.ap()))
    SY(_)

    def _(gp):
        OP("GP", gp, [], ["RWB"], lambda e: [
            e.dma_start(out=RWB[:, dc * 8:(dc + 1) * 8], in_=rw_st[dc])
            for dc in range(NDC)])
        OP("GP", gp, [], ["SVCb"], lambda e: e.dma_start(
            out=SVC[:, SV_L1B:SV_L1B + 16], in_=l1b_bf.ap()))
    GP(_)

    def _(ve):
        OP("VE", ve, [], ["XSR"], lambda e: e.memset(XSR[:, :], 0.0))
    VE(_)

    def emit_layernorm(rb_row, rb_res):
        def _(ve):
            OP("VE", ve, ["X"], ["A"] + A_FAM[1:], lambda e: [
                e.tensor_copy(A[:, dc * 1024:(dc + 1) * 1024],
                              X[:, dc * 1024:(dc + 1) * 1024])
                for dc in range(NDC)][-1:])
        VE(_)

        def _(se):
            OP("SE", se, ["A"], CB_FAM, lambda e: [
                e.activation(Cb[:, dc * 1024:(dc + 1) * 1024],
                             A[:, dc * 1024:(dc + 1) * 1024], AF.Square)
                for dc in range(NDC)][-1:])
        SE(_)

        def _(te):
            def em(e):
                last = None
                for tb in range(8):
                    for dc in range(NDC):
                        last = e.matmul(
                            PA[:, tb:tb + 1],
                            A[:, dc * 1024 + tb * 128:
                              dc * 1024 + (tb + 1) * 128],
                            ONES_COL, start=(dc == 0), stop=(dc == 15))
                    for dc in range(NDC):
                        last = e.matmul(
                            PA[:, 512 + tb:512 + tb + 1],
                            Cb[:, dc * 1024 + tb * 128:
                               dc * 1024 + (tb + 1) * 128],
                            ONES_COL, start=(dc == 0), stop=(dc == 15))
                return last
            OP("TE", te, ["A", "Cb", "CONS"], ["PA"], em)
        TE(_)

        def _(ve):
            OP("VE", ve, ["PA"], ["STTa"], lambda e: e.tensor_scalar(
                STT[:, 0:8], PA[:, 0:8], 1.0 / 2048.0, None, OP_.mult))
            OP("VE", ve, ["PA"], ["STTb"], lambda e: e.tensor_scalar(
                STT[:, 8:16], PA[:, 512:520], 1.0 / 2048.0, None, OP_.mult))
            OP("VE", ve, ["STTa"], ["STTc"], lambda e: e.tensor_tensor(
                STT[:, 16:24], STT[:, 0:8], STT[:, 0:8], OP_.mult))
            OP("VE", ve, ["STTb", "STTc"], ["STTb"], lambda e: e.tensor_tensor(
                STT[:, 8:16], STT[:, 8:16], STT[:, 16:24], OP_.subtract))
            OP("VE", ve, ["STTb"], ["STTb"], lambda e: e.tensor_scalar(
                STT[:, 8:16], STT[:, 8:16], 1e-5, None, OP_.add))
        VE(_)

        def _(se):
            OP("SE", se, ["STTb"], ["STTc"], lambda e: e.activation(
                STT[:, 16:24], STT[:, 8:16], AF.Sqrt))
        SE(_)

        def _(ve):
            OP("VE", ve, ["STTc"], ["STTb"], lambda e: e.reciprocal(
                STT[:, 8:16], STT[:, 16:24]))
            OP("VE", ve, ["STTa", "STTb"], ["STTc"], lambda e: e.tensor_tensor(
                STT[:, 16:24], STT[:, 0:8], STT[:, 8:16], OP_.mult))
            OP("VE", ve, ["STTc"], ["STTc"], lambda e: e.tensor_scalar(
                STT[:, 16:24], STT[:, 16:24], -1.0, None, OP_.mult))
        VE(_)

        def _(te):
            def em(e):
                e.transpose(PC[0:8, 0:128], STT[:, 8:16], IDF[:, :])
                return e.transpose(PC[0:8, 128:256], STT[:, 16:24],
                                   IDF[:, :])
            OP("TE", te, ["STTb", "STTc", "IDF"], ["PC"], em)
        TE(_)

        def _(ve):
            OP("VE", ve, ["PC"], ["SM0"], lambda e: e.tensor_copy(
                SM0[0:8, 0:256], PC[0:8, 0:256]))
        VE(_)

        def _(sy):
            OP("SY", sy, ["SM0"], [rb_res], lambda e: e.dma_start(
                out=rb_row, in_=SM0[0:8, 0:128]))
        SY(_)

        def _(te):
            OP("TE", te, [rb_res, "CONS"], ["PA"], lambda e: mmsteps(
                e, PA, 0, 1024, [(ONER, rb_row)]))
        TE(_)

        def _(sy):
            OP("SY", sy, ["SM0"], [rb_res], lambda e: e.dma_start(
                out=rb_row, in_=SM0[0:8, 128:256]))
        SY(_)

        def _(te):
            OP("TE", te, [rb_res, "CONS"], ["PB"], lambda e: mmsteps(
                e, PB, 0, 1024, [(ONER, rb_row)]))
        TE(_)

        def _(ve):
            OP("VE", ve, ["X", "PA"], CB_FAM, lambda e: [
                e.tensor_tensor(Cb[:, dc * 1024:(dc + 1) * 1024],
                                X[:, dc * 1024:(dc + 1) * 1024],
                                PA[:, 0:1024], OP_.mult)
                for dc in range(NDC)][-1:])
            OP("VE", ve, ["Cb", "PB"], ["A"] + A_FAM[1:], lambda e: [
                e.tensor_tensor(A[:, dc * 1024:(dc + 1) * 1024],
                                Cb[:, dc * 1024:(dc + 1) * 1024],
                                PB[:, 0:1024], OP_.add)
                for dc in range(NDC)][-1:])
        VE(_)

    emit_layernorm(RB_D, "RB_D")


    if DEBUG_DUMPS:
        def _(sy):
            OP("SY", sy, ["A"], ["dbg_xn1"], lambda e: e.dma_start(
                out=dbg["dbg_xn1"].ap(), in_=A[:, :]))
        SY(_)
    for cc in range(32):
        sl = cc % 2
        pn = "PA" if sl == 0 else "PB"
        rn = "PC" if sl == 0 else "PD"
        qb = [SM0, SM1][sl]
        qn = ["SM0", "SM1"][sl]
        t1 = T1S[sl]
        tn = ["T1S0", "T1S1"][sl]
        wn = ["WT0", "WT1"][sl]
        tbl = TBL[sl]
        tbn = ["TBL0", "TBL1"][sl]

        def _(gp, cc=cc, sl=sl, wn=wn):
            OP("GP", gp, [], [wn], lambda e: e.dma_start(
                out=[WT0, WT1][sl][:, 0:2048], in_=wqkv_st[cc]))
        GP(_)

        def _(sq, cc=cc, sl=sl, tbl=tbl, tbn=tbn):
            src = tabq if cc < 16 else tabk
            OP("SQ", sq, [], [tbn], lambda e: e.dma_start(
                out=tbl[:, 0:2048],
                in_=src[cc % 16].rearrange("w p t -> p w t")))
        SE(_)

        def _(te, cc=cc, sl=sl, pn=pn, wn=wn):
            OP("TE", te, [wn, "A"], [pn], lambda e: mmsteps(
                e, PSUM[pn], 0, 1024,
                [([WT0, WT1][sl][:, dc * 128:(dc + 1) * 128],
                  A[:, dc * 1024:(dc + 1) * 1024]) for dc in range(NDC)]))
        TE(_)

        def _(ve, cc=cc, sl=sl, pn=pn, qb=qb, qn=qn):
            OP("VE", ve, [pn], [qn], lambda e: e.tensor_copy(
                qb[:, 0:1024], PSUM[pn][:, 0:1024]))
        VE(_)

        def _(te, cc=cc, sl=sl, rn=rn, qb=qb, qn=qn):
            OP("TE", te, [qn, "CONS"], [rn], lambda e: mmsteps(
                e, PSUM[rn], 0, 1024, [(PSWAP, qb[:, 0:1024])]))
        TE(_)

        def _(ve, cc=cc, sl=sl, rn=rn, qb=qb, qn=qn, t1=t1, tn=tn, tbn=tbn):
            tbl_ = TBL[sl]
            OP("VE", ve, [qn, tbn], [tn], lambda e: e.tensor_tensor(
                t1[:, 0:1024], qb[:, 0:1024], tbl_[:, 0:1024], OP_.mult))
            OP("VE", ve, [rn, tbn], [qn], lambda e: e.tensor_tensor(
                qb[:, 0:1024], PSUM[rn][:, 0:1024], tbl_[:, 1024:2048],
                OP_.mult))
            if cc < 16:
                OP("VE", ve, [tn, qn], ["Bq"], lambda e: e.tensor_tensor(
                    Bq[:, cc * 1024:(cc + 1) * 1024], t1[:, 0:1024],
                    qb[:, 0:1024], OP_.add))
            else:
                OP("VE", ve, [tn, qn], [tn], lambda e: e.tensor_tensor(
                    t1[:, 0:1024], t1[:, 0:1024], qb[:, 0:1024], OP_.add))
        VE(_)

        if cc >= 16:
            def _(sy, cc=cc, t1=t1, tn=tn):
                OP("SY", sy, [tn], ["ag0_in"], lambda e: e.dma_start(
                    out=ag0_in[(cc - 16) * 128:(cc - 15) * 128, :],
                    in_=t1[:, 0:1024]))
            SY(_)

    for dvq in range(2):
        def _(gp, dvq=dvq):
            OP("GP", gp, [], CB_FAM, lambda e: [
                e.dma_start(out=Cb[:, dc * 1024:(dc + 1) * 1024],
                            in_=wv_mov[dc, :, dvq * 1024:(dvq + 1) * 1024])
                for dc in range(NDC)])
        GP(_)

        for tb in range(8):
            sl = tb % 2
            pn = "PA" if sl == 0 else "PB"

            def _(te, dvq=dvq, tb=tb, sl=sl, pn=pn):
                OP("TE", te, ["Cb", "A"], [pn], lambda e: mmsteps(
                    e, PSUM[pn], 0, 1024,
                    [(A[:, dc * 1024 + tb * 128:dc * 1024 + (tb + 1) * 128],
                      Cb[:, dc * 1024:(dc + 1) * 1024])
                     for dc in range(NDC)]))
            TE(_)

            def _(ve, dvq=dvq, tb=tb, sl=sl, pn=pn):
                OP("VE", ve, [pn], [["SM0", "SM1"][sl]],
                   lambda e: e.tensor_copy([SM0, SM1][sl][:, 0:1024],
                                           PSUM[pn][:, 0:1024]))
            VE(_)

            def _(sy, dvq=dvq, tb=tb, sl=sl):
                r0 = 2048 + (tb * 2 + dvq) * 128
                OP("SY", sy, [["SM0", "SM1"][sl]], ["ag0_in"],
                   lambda e: e.dma_start(out=ag0_in[r0:r0 + 128, :],
                                         in_=[SM0, SM1][sl][:, 0:1024]))
            SY(_)

    def _(gp):
        def em(e):
            return e.collective_compute(
                "AllGather", OP_.bypass,
                replica_groups=[[0, 1], [2, 3], [4, 5], [6, 7]],
                ins=[ag0_in.ap().opt()], outs=[ag0_out.ap().opt()])
        toks = []
        t = lastw.get("ag0_in")
        if t:
            toks.append(t)
        for (s, v) in toks:
            if seen["GP"].get(s, 0) < v:
                gp.wait_ge(sems[s][0], v)
                seen["GP"][s] = v
        ins_ = em(gp)
        sems["cAG0"][1] += 1
        ins_.then_inc(sems["cAG0"][0], 1)
        note("ag0_out", ("cAG0", sems["cAG0"][1]))
    GP(_)


    if DEBUG_DUMPS:
        def _(sy):
            OP("SY", sy, ["Bq"], ["dbg_q0"], lambda e: e.dma_start(
                out=dbg["dbg_q0"].ap(), in_=Bq[:, :]))
        SY(_)
    def _(te):
        def em(e):
            last = None
            for tb in range(8):
                for dc in range(NDC):
                    last = e.matmul(
                        PB[:, tb * 8:(tb + 1) * 8],
                        A[:, dc * 1024 + tb * 128:dc * 1024 + (tb + 1) * 128],
                        RWB[:, dc * 8:(dc + 1) * 8],
                        start=(dc == 0), stop=(dc == 15))
            return last
        OP("TE", te, ["A", "RWB"], ["PB"], em)
    TE(_)

    def _(ve):
        for tb in range(8):
            lo = PB[:, tb * 8:(tb + 1) * 8]
            OP("VE", ve, ["PB"], ["STTa"], lambda e, lo=lo: e.reduce_max(
                STT[:, 24:25], lo, axis=AX))
            OP("VE", ve, ["PB", "STTa"], ["STTb"],
               lambda e, lo=lo: e.tensor_scalar(
                   STT[:, 32:40], lo, STT[:, 24:25], None, OP_.subtract))
            OP("VE", ve, ["STTb"], ["STTc"], lambda e: e.tensor_scalar(
                STT[:, 40:48], STT[:, 32:40], 0.0, None, OP_.is_ge))
            OP("VE", ve, ["STTc"], ["STTd"], lambda e: e.tensor_scalar(
                STT[:, 48:56], STT[:, 40:48], -1e30, None, OP_.mult))
            OP("VE", ve, ["PB", "STTd"], ["STTb"],
               lambda e, lo=lo: e.tensor_tensor(
                   STT[:, 32:40], lo, STT[:, 48:56], OP_.add))
            OP("VE", ve, ["STTb"], ["STTa"], lambda e: e.reduce_max(
                STT[:, 25:26], STT[:, 32:40], axis=AX))
            OP("VE", ve, ["PB", "STTa"], ["STTc"],
               lambda e, lo=lo: e.tensor_scalar(
                   STT[:, 40:48], lo, STT[:, 25:26], None, OP_.is_ge))
            OP("VE", ve, ["PB", "STTc"], ["COMB"],
               lambda e, lo=lo, tb=tb: e.tensor_tensor(
                   STT[:, 160 + tb * 8:160 + (tb + 1) * 8], lo,
                   STT[:, 40:48], OP_.mult))
    VE(_)

    def _(te):
        def em(e):
            last = None
            for tb in range(8):
                last = e.transpose(PC[0:8, tb * 128:(tb + 1) * 128],
                                   STT[:, 160 + tb * 8:160 + (tb + 1) * 8],
                                   IDF[:, :])
            return last
        OP("TE", te, ["COMB", "IDF"], ["PC"], em)
    TE(_)

    def _(ve):
        OP("VE", ve, ["PC"], ["CT"], lambda e: e.tensor_copy(
            CT, PC[0:8, 0:1024]))
    VE(_)

    for e_ in range(9):
        shared = (e_ == 8)
        if not shared:
            def _(sy, e_=e_):
                OP("SY", sy, ["CT"], ["RB_C"], lambda e: e.dma_start(
                    out=RB_C, in_=CT[e_:e_ + 1, 0:1024]))
            SY(_)

            def _(te, e_=e_):
                OP("TE", te, ["RB_C", "CONS"], ["PC"], lambda e: mmsteps(
                    e, PC, 0, 1024, [(ONER, RB_C)]))
            TE(_)

            def _(ve, e_=e_):
                OP("VE", ve, ["PC"], ["HCB"], lambda e: e.tensor_copy(
                    HCB[:, 0:1024], PC[:, 0:1024]))
            VE(_)

        for icc in range(8):
            def _(gp, e_=e_, icc=icc):
                if shared:
                    OP("GP", gp, [], ["WT0"], lambda e: e.dma_start(
                        out=WT0[:, 0:2048], in_=sgu_st[0, icc]))
                    OP("GP", gp, [], ["WT1"], lambda e: e.dma_start(
                        out=WT1[:, 0:2048], in_=sgu_st[1, icc]))
                else:
                    OP("GP", gp, [], ["WT0"], lambda e: e.dma_start(
                        out=WT0[:, 0:2048], in_=wgu_st[0, e_, icc]))
                    OP("GP", gp, [], ["WT1"], lambda e: e.dma_start(
                        out=WT1[:, 0:2048], in_=wgu_st[1, e_, icc]))
            GP(_)

            def _(te, e_=e_, icc=icc):
                OP("TE", te, ["WT0", "A"], ["PA"], lambda e: mmsteps(
                    e, PA, 0, 1024,
                    [(WT0[:, dc * 128:(dc + 1) * 128],
                      A[:, dc * 1024:(dc + 1) * 1024])
                     for dc in range(NDC)]))
                OP("TE", te, ["WT1", "A"], ["PB"], lambda e: mmsteps(
                    e, PB, 0, 1024,
                    [(WT1[:, dc * 128:(dc + 1) * 128],
                      A[:, dc * 1024:(dc + 1) * 1024])
                     for dc in range(NDC)]))
            TE(_)

            def _(se, e_=e_, icc=icc, shared=shared):
                OP("SE", se, ["PA", "SVC"], ["SM0"], lambda e: e.activation(
                    SM0[:, 0:1024], PA[:, 0:1024], AF.Silu,
                    bias=(SVC[:, SV_SGB + icc:SV_SGB + icc + 1] if shared
                          else 0.0)))
            SE(_)

            def _(ve, e_=e_, icc=icc, shared=shared):
                if shared:
                    OP("VE", ve, ["PB", "SVC"], ["SM1"],
                       lambda e: e.tensor_scalar(
                           SM1[:, 0:1024], PB[:, 0:1024],
                           SVC[:, SV_SUB + icc:SV_SUB + icc + 1], None,
                           OP_.add))
                else:
                    OP("VE", ve, ["PB", "HCB"], ["SM1"],
                       lambda e: e.tensor_tensor(
                           SM1[:, 0:1024], PB[:, 0:1024], HCB[:, 0:1024],
                           OP_.mult))
                OP("VE", ve, ["SM0", "SM1"], ["Cb"],
                   lambda e: e.tensor_tensor(
                       Cb[:, icc * 1024:(icc + 1) * 1024],
                       SM0[:, 0:1024], SM1[:, 0:1024], OP_.mult))
            VE(_)

        for occ in range(NDC):
            sl = occ % 2
            pn = "PA" if sl == 0 else "PB"
            wn = ["WT0", "WT1"][sl]

            def _(gp, e_=e_, occ=occ, sl=sl, wn=wn, shared=shared):
                src = sd_st[occ] if shared else wd_st[e_, occ]
                OP("GP", gp, [], [wn], lambda e: e.dma_start(
                    out=[WT0, WT1][sl][:, 0:1024], in_=src))
            GP(_)

            def _(te, e_=e_, occ=occ, sl=sl, pn=pn, wn=wn):
                OP("TE", te, [wn, "Cb"], [pn], lambda e: mmsteps(
                    e, PSUM[pn], 0, 1024,
                    [([WT0, WT1][sl][:, icc * 128:(icc + 1) * 128],
                      Cb[:, icc * 1024:(icc + 1) * 1024])
                     for icc in range(8)]))
            TE(_)

            if e_ == 0:
                def _(se, occ=occ, sl=sl, pn=pn):
                    OP("SE", se, [pn], ["Dg"] + DG_FAM[1:],
                       lambda e: e.activation(
                           Dg[:, occ * 1024:(occ + 1) * 1024],
                           PSUM[pn][:, 0:1024], AF.Copy))
                SE(_)
            else:
                def _(ve, occ=occ, sl=sl, pn=pn):
                    def em(e):
                        with nc.allow_low_precision(
                                reason="bf16 moe partial-sum acc, in tol"):
                            return e.tensor_tensor(
                                Dg[:, occ * 1024:(occ + 1) * 1024],
                                Dg[:, occ * 1024:(occ + 1) * 1024],
                                PSUM[pn][:, 0:1024], OP_.add)
                    OP("VE", ve, [pn, "Dg"], ["Dg"], em)
                VE(_)

    def _(se):
        OP("SE", se, ["Dg", "SVC"], ["Dg"], lambda e: [
            e.activation(Dg[:, occ * 1024:(occ + 1) * 1024],
                         Dg[:, occ * 1024:(occ + 1) * 1024], AF.Silu,
                         bias=SVC[:, SV_SDB + occ:SV_SDB + occ + 1])
            for occ in range(NDC)][-1:])
    SE(_)


    if DEBUG_DUMPS:
        def _(sy):
            OP("SY", sy, ["Dg"], ["dbg_g0"], lambda e: e.dma_start(
                out=dbg["dbg_g0"].ap(), in_=Dg[:, :]))
        SY(_)
    for cc in range(NDC):
        sl = cc % 2
        pn = "PA" if sl == 0 else "PB"
        wn = ["WT0", "WT1"][sl]

        def _(gp, cc=cc, sl=sl, wn=wn):
            OP("GP", gp, [], [wn], lambda e: e.dma_start(
                out=[WT0, WT1][sl][:, 0:2048], in_=wg_st[cc]))
        GP(_)

        def _(te, cc=cc, sl=sl, pn=pn, wn=wn):
            OP("TE", te, [wn, "A"], [pn], lambda e: mmsteps(
                e, PSUM[pn], 0, 1024,
                [([WT0, WT1][sl][:, dc * 128:(dc + 1) * 128],
                  A[:, dc * 1024:(dc + 1) * 1024]) for dc in range(NDC)]))
        TE(_)

        def _(se, cc=cc, sl=sl, pn=pn):
            OP("SE", se, [pn], CB_FAM, lambda e: e.activation(
                Cb[:, cc * 1024:(cc + 1) * 1024], PSUM[pn][:, 0:1024],
                AF.Silu))
        SE(_)

    def _(sy):
        OP("SY", sy, [], ["AM", "A", "AKV", "AE0", "AE1", "GS", "RB_A"],
           lambda e: e.dma_start(
               out=AMm, in_=am_t.ap().rearrange("j p c -> p j c")))
    SY(_)

    def emit_attention(layer):
        for h in range(16):
            if layer == 0:
                def _(sy, h=h):
                    def em(e):
                        outl = []
                        for r in range(2):
                            outl.append(e.dma_start(
                                out=AK[:, r * 1024:(r + 1) * 1024],
                                in_=ag0_out[r * 4096 + h * 128:
                                            r * 4096 + (h + 1) * 128, :]))
                        dvq = h // 8
                        for g in range(16):
                            r, tb = OWNER[g], LOCIDX[g]
                            r0 = r * 4096 + 2048 + (tb * 2 + dvq) * 128
                            c0 = (h % 8) * 128
                            outl.append(e.dma_start(
                                out=AVt[:, g * 128:(g + 1) * 128],
                                in_=ag0_out[r0:r0 + 128, c0:c0 + 128]))
                        return outl
                    OP("SY", sy, ["ag0_out"], ["AKV"], em)
                SY(_)

            def emit_S(j):
                sl = j % 2
                pn = "PA" if sl == 0 else "PB"
                W = WJ[j]

                def _(te, h=h, j=j, sl=sl, pn=pn, W=W, layer=layer):
                    if layer == 0:
                        lh = AK[:, OWNER[j] * 1024 + LOCIDX[j] * 128:
                                OWNER[j] * 1024 + (LOCIDX[j] + 1) * 128]
                        kres = "AKV"
                    else:
                        gr = h // 4
                        lh = Cb[:, gr * 2048 + OWNER[j] * 1024
                                + LOCIDX[j] * 128:
                                gr * 2048 + OWNER[j] * 1024
                                + (LOCIDX[j] + 1) * 128]
                        kres = "Cb"
                    rh = Bq[:, h * 1024 + SOFF[j]:(h + 1) * 1024]
                    OP("TE", te, [kres, "Bq"], [pn], lambda e: mmsteps(
                        e, PSUM[pn], 0, W, [(lh, rh)]))
                TE(_)

            def emit_evict_av(j):
                sl = j % 2
                pn = "PA" if sl == 0 else "PB"
                aen = ["AE0", "AE1"][sl]
                W = WJ[j]
                if layer == 0:
                    def _(ve, h=h, j=j, sl=sl, pn=pn, aen=aen, W=W):
                        m = min(256, W)

                        def em(e):
                            i1 = e.tensor_tensor(
                                AE[sl][:, 0:m], PSUM[pn][:, 0:m],
                                AMm[:, j * 256:j * 256 + m], OP_.mult)
                            if W > 256:
                                i1 = e.tensor_copy(AE[sl][:, 256:W],
                                                   PSUM[pn][:, 256:W])
                            return i1
                        OP("VE", ve, [pn, "AM"], [aen], em)
                    VE(_)
                else:
                    def _(se, h=h, j=j, sl=sl, pn=pn, aen=aen, W=W):
                        OP("SE", se, [pn], [aen], lambda e: e.activation(
                            AE[sl][:, 0:W], PSUM[pn][:, 0:W], AF.Exp,
                            scale=INV_SQRT_D))
                    SE(_)

                    def _(ve, h=h, j=j, sl=sl, aen=aen, W=W):
                        m = min(256, W)
                        OP("VE", ve, [aen, "AM"], [aen],
                           lambda e: e.tensor_tensor(
                               AE[sl][:, 0:m], AE[sl][:, 0:m],
                               AMm[:, j * 256:j * 256 + m], OP_.mult))
                    VE(_)

                def _(te, h=h, j=j, sl=sl, aen=aen, W=W, layer=layer):
                    if layer == 0:
                        lh = AVt[:, j * 128:(j + 1) * 128]
                        vres = "AKV"
                    else:
                        lh = Cb[:, 8192 + j * 512 + (h // 4) * 128:
                                8192 + j * 512 + (h // 4 + 1) * 128]
                        vres = "Cb"

                    def em(e):
                        last = None
                        for (a, b_) in col_chunks(SOFF[j], 1024):
                            lastj = 7 if a < 512 else 15
                            last = e.matmul(
                                PC[:, a:b_], lh,
                                AE[sl][:, a - SOFF[j]:b_ - SOFF[j]],
                                start=(j == 0), stop=(j == lastj))
                        if layer == 1:
                            for (a, b_) in col_chunks(SOFF[j], 1024):
                                lastj = 7 if a < 512 else 15
                                last = e.matmul(
                                    PD[0:16, a:b_], ECOLt(h),
                                    AE[sl][:, a - SOFF[j]:b_ - SOFF[j]],
                                    start=(h == 0 and j == 0),
                                    stop=(h == 15 and j == lastj))
                        return last
                    wr = ["PC"] if layer == 0 else ["PC", "PD"]
                    OP("TE", te, [aen, vres, "CONS"], wr, em)
                TE(_)

            for j in range(16):
                emit_S(j)
                if j >= 1:
                    emit_evict_av(j - 1)
            emit_evict_av(15)

            def _(ve, h=h):
                OP("VE", ve, ["PC"], ["Bq"], lambda e: e.tensor_copy(
                    Bq[:, h * 1024:(h + 1) * 1024], PC[:, 0:1024]))
            VE(_)
    emit_attention(0)

    if DEBUG_DUMPS:
        def _(sy):
            OP("SY", sy, ["Bq"], ["dbg_y0"], lambda e: e.dma_start(
                out=dbg["dbg_y0"].ap(), in_=Bq[:, :]))
        SY(_)

    for h in range(16):
        sq = [SM0, SM1][h % 2]
        sqn = ["SM0", "SM1"][h % 2]

        def _(se, h=h, sq=sq, sqn=sqn):
            OP("SE", se, ["Bq"], [sqn], lambda e: e.activation(
                sq[:, 0:1024], Bq[:, h * 1024:(h + 1) * 1024], AF.Square))
        SE(_)

        def _(te, h=h, sq=sq, sqn=sqn):
            OP("TE", te, ["Bq", "CONS"], ["PA"], lambda e: mmsteps(
                e, PA, 0, 1024,
                [(ECOLt(h), Bq[:, h * 1024:(h + 1) * 1024])],
                start=(h == 0), stop=(h == 15), rows=16))
            OP("TE", te, [sqn, "CONS"], ["PB"], lambda e: mmsteps(
                e, PB, 0, 1024, [(ECOLt(h), sq[:, 0:1024])],
                start=(h == 0), stop=(h == 15), rows=16))
        TE(_)

    def _(ve):
        OP("VE", ve, ["PA"], ["GS"], lambda e: e.tensor_copy(
            GSA, PA[0:16, 0:1024]))
        OP("VE", ve, ["PB", "GS"], ["GS2"], lambda e: e.tensor_copy(
            GSB, PB[0:16, 0:1024]))
    VE(_)

    def _(gp):
        OP("GP", gp, ["GS"], ["STTa"], lambda e: [
            e.dma_start(
                out=STT[:, hh * 8:(hh + 1) * 8],
                in_=GSA[hh:hh + 1, :].rearrange("o (p i) -> o p i",
                                                p=128, i=8))
            for hh in range(16)])
        OP("GP", gp, ["GS2"], ["STTb"], lambda e: [
            e.dma_start(
                out=STT[:, 128 + hh * 8:128 + (hh + 1) * 8],
                in_=GSB[hh:hh + 1, :].rearrange("o (p i) -> o p i",
                                                p=128, i=8))
            for hh in range(16)])
    GP(_)

    def _(ve):
        OP("VE", ve, ["STTa"], ["STTa"], lambda e: e.tensor_scalar(
            STT[:, 0:128], STT[:, 0:128], 1.0 / 128.0, None, OP_.mult))
        OP("VE", ve, ["STTb"], ["STTb"], lambda e: e.tensor_scalar(
            STT[:, 128:256], STT[:, 128:256], 1.0 / 128.0, None, OP_.mult))
        for qq in range(4):
            OP("VE", ve, ["STTa"], ["AMX"], lambda e, qq=qq: e.tensor_tensor(
                AMX[:, 0:32], STT[:, qq * 32:(qq + 1) * 32],
                STT[:, qq * 32:(qq + 1) * 32], OP_.mult))
            OP("VE", ve, ["STTb", "AMX"], ["STTb"],
               lambda e, qq=qq: e.tensor_tensor(
                   STT[:, 128 + qq * 32:128 + (qq + 1) * 32],
                   STT[:, 128 + qq * 32:128 + (qq + 1) * 32],
                   AMX[:, 0:32], OP_.subtract))
        OP("VE", ve, ["STTb"], ["STTb"], lambda e: e.tensor_scalar(
            STT[:, 128:256], STT[:, 128:256], 1e-5, None, OP_.add))
    VE(_)

    def _(se):
        OP("SE", se, ["STTb"], ["STTb"], lambda e: e.activation(
            STT[:, 128:256], STT[:, 128:256], AF.Sqrt))
    SE(_)

    def _(ve):
        OP("VE", ve, ["STTb"], ["STTb"], lambda e: e.reciprocal(
            STT[:, 128:256], STT[:, 128:256]))
        OP("VE", ve, ["STTa", "STTb"], ["STTa"], lambda e: e.tensor_tensor(
            STT[:, 0:128], STT[:, 0:128], STT[:, 128:256], OP_.mult))
        OP("VE", ve, ["STTa"], ["STTa"], lambda e: e.tensor_scalar(
            STT[:, 0:128], STT[:, 0:128], -1.0, None, OP_.mult))
    VE(_)

    for h in range(16):
        def _(gp, h=h):
            OP("GP", gp, ["STTb"], ["RB_A"], lambda e: e.dma_start(
                out=RB_A, in_=STT[:, 128 + h * 8:128 + (h + 1) * 8]))
        GP(_)

        def _(te, h=h):
            OP("TE", te, ["RB_A", "CONS"], ["PA"], lambda e: mmsteps(
                e, PA, 0, 1024, [(ONER, RB_A)]))
        TE(_)

        def _(gp, h=h):
            OP("GP", gp, ["STTa"], ["RB_A"], lambda e: e.dma_start(
                out=RB_A, in_=STT[:, h * 8:(h + 1) * 8]))
        GP(_)

        def _(te, h=h):
            OP("TE", te, ["RB_A", "CONS"], ["PB"], lambda e: mmsteps(
                e, PB, 0, 1024, [(ONER, RB_A)]))
        TE(_)

        def _(ve, h=h):
            OP("VE", ve, ["Bq", "PA"], ["SM0"], lambda e: e.tensor_tensor(
                SM0[:, 0:1024], Bq[:, h * 1024:(h + 1) * 1024],
                PA[:, 0:1024], OP_.mult))
            OP("VE", ve, ["SM0", "PB"], ["SM1"], lambda e: e.tensor_tensor(
                SM1[:, 0:1024], SM0[:, 0:1024], PB[:, 0:1024], OP_.add))
            OP("VE", ve, ["SM1", "SVC"], ["SM0"], lambda e: e.tensor_scalar(
                SM0[:, 0:1024], SM1[:, 0:1024],
                SVC[:, SV_GNW + h:SV_GNW + h + 1],
                SVC[:, SV_GNB + h:SV_GNB + h + 1], OP_.mult, OP_.add))
            OP("VE", ve, ["SM0", "Cb"], ["Cb"], lambda e: e.tensor_tensor(
                Cb[:, h * 1024:(h + 1) * 1024],
                Cb[:, h * 1024:(h + 1) * 1024], SM0[:, 0:1024], OP_.mult))
        VE(_)


    if DEBUG_DUMPS:
        def _(sy):
            OP("SY", sy, ["Cb"], ["dbg_gated"], lambda e: e.dma_start(
                out=dbg["dbg_gated"].ap(), in_=Cb[:, :]))
        SY(_)
    for occ in range(NDC):
        sl = occ % 2
        pn = "PA" if sl == 0 else "PB"
        wn = ["WT0", "WT1"][sl]
        sqn = ["SM0", "SM1"][sl]

        def _(gp, occ=occ, sl=sl, wn=wn):
            OP("GP", gp, [], [wn], lambda e: e.dma_start(
                out=[WT0, WT1][sl][:, 0:2048], in_=wp_st[occ]))
        GP(_)

        def _(te, occ=occ, sl=sl, pn=pn, wn=wn):
            OP("TE", te, [wn, "Cb"], [pn], lambda e: mmsteps(
                e, PSUM[pn], 0, 1024,
                [([WT0, WT1][sl][:, cc * 128:(cc + 1) * 128],
                  Cb[:, cc * 1024:(cc + 1) * 1024]) for cc in range(NDC)]))
        TE(_)

        def _(ve, occ=occ, sl=sl, pn=pn, sqn=sqn):
            OP("VE", ve, [pn, "Dg"], [sqn], lambda e: e.tensor_tensor(
                [SM0, SM1][sl][:, 0:1024], PSUM[pn][:, 0:1024],
                Dg[:, occ * 1024:(occ + 1) * 1024], OP_.mult))
            OP("VE", ve, [sqn, "X"], ["X"], lambda e: e.tensor_tensor(
                X[:, occ * 1024:(occ + 1) * 1024],
                X[:, occ * 1024:(occ + 1) * 1024],
                [SM0, SM1][sl][:, 0:1024], OP_.add))
        VE(_)


    if DEBUG_DUMPS:
        def _(sy):
            OP("SY", sy, ["X"], ["dbg_x1"], lambda e: e.dma_start(
                out=dbg["dbg_x1"].ap(), in_=X[:, :]))
        SY(_)
    emit_layernorm(RB_D, "RB_D")

    def _(ve):
        OP("VE", ve, ["A"], ["AMX"], lambda e: [
            e.reduce_max(AMX[:, dc:dc + 1],
                         A[:, dc * 1024:(dc + 1) * 1024], axis=AX,
                         apply_absolute_value=True) for dc in range(NDC)][-1:])
        OP("VE", ve, ["AMX"], ["AMX2"], lambda e: e.reduce_max(
            AMX[:, 16:17], AMX[:, 0:16], axis=AX, apply_absolute_value=True))
    VE(_)

    def _(sy):
        OP("SY", sy, ["AMX2"], ["STTb"], lambda e: e.dma_start(
            out=STT[0:1, 128:256], in_=AMX[:, 16:17]))
    SY(_)

    def _(ve):
        OP("VE", ve, ["STTb", "XSR"], ["XSR"], lambda e: e.reduce_max(
            XSR[0:1, 0:1], STT[0:1, 128:256], axis=AX,
            apply_absolute_value=True))
    VE(_)

    def _(sy):
        OP("SY", sy, ["XSR"], ["ar_in"], lambda e: e.dma_start(
            out=ar_in[0:1, :], in_=XSR[0:1, 0:16]))
    SY(_)

    def _(gp):
        t = lastw.get("ar_in")
        if t and seen["GP"].get(t[0], 0) < t[1]:
            gp.wait_ge(sems[t[0]][0], t[1])
            seen["GP"][t[0]] = t[1]
        ins_ = gp.collective_compute(
            "AllReduce", OP_.max,
            replica_groups=[[0, 1, 2, 3, 4, 5, 6, 7]],
            ins=[ar_in.ap().opt()], outs=[ar_out.ap().opt()])
        sems["cAR"][1] += 1
        ins_.then_inc(sems["cAR"][0], 1)
        note("ar_out", ("cAR", sems["cAR"][1]))
    GP(_)

    for cc in range(NDC):
        sl = cc % 2
        pn = "PA" if sl == 0 else "PB"
        wn = ["WT0", "WT1"][sl]

        def _(gp, cc=cc, sl=sl, wn=wn):
            OP("GP", gp, [], [wn], lambda e: e.dma_start(
                out=[WT0, WT1][sl][:, 0:2048], in_=l1q_st[cc]))
        GP(_)

        def _(te, cc=cc, sl=sl, pn=pn, wn=wn):
            OP("TE", te, [wn, "A"], [pn], lambda e: mmsteps(
                e, PSUM[pn], 0, 1024,
                [([WT0, WT1][sl][:, dc * 128:(dc + 1) * 128],
                  A[:, dc * 1024:(dc + 1) * 1024]) for dc in range(NDC)]))
        TE(_)

        def _(se, cc=cc, sl=sl, pn=pn):
            OP("SE", se, [pn, "SVC"], ["Bq"], lambda e: e.activation(
                Bq[:, cc * 1024:(cc + 1) * 1024], PSUM[pn][:, 0:1024],
                AF.Identity, bias=SVC[:, SV_BQ + cc:SV_BQ + cc + 1]))
        SE(_)

    for kc in range(4):
        sl = kc % 2
        pn = "PA" if sl == 0 else "PB"
        wn = ["WT0", "WT1"][sl]
        sqn = ["SM0", "SM1"][sl]

        def _(gp, kc=kc, sl=sl, wn=wn):
            OP("GP", gp, [], [wn], lambda e: e.dma_start(
                out=[WT0, WT1][sl][:, 0:2048], in_=l1k_st[kc]))
        GP(_)

        def _(te, kc=kc, sl=sl, pn=pn, wn=wn):
            OP("TE", te, [wn, "A"], [pn], lambda e: mmsteps(
                e, PSUM[pn], 0, 1024,
                [([WT0, WT1][sl][:, dc * 128:(dc + 1) * 128],
                  A[:, dc * 1024:(dc + 1) * 1024]) for dc in range(NDC)]))
        TE(_)

        def _(se, kc=kc, sl=sl, pn=pn, sqn=sqn):
            OP("SE", se, [pn, "SVC"], [sqn], lambda e: e.activation(
                [SM0, SM1][sl][:, 0:1024], PSUM[pn][:, 0:1024],
                AF.Identity, bias=SVC[:, SV_BK + kc:SV_BK + kc + 1]))
        SE(_)

        def _(sy, kc=kc, sl=sl, sqn=sqn):
            OP("SY", sy, [sqn], ["ag1_in"], lambda e: e.dma_start(
                out=ag1_in[kc * 128:(kc + 1) * 128, :],
                in_=[SM0, SM1][sl][:, 0:1024]))
        SY(_)

    def _(gp):
        OP("GP", gp, [], CB_FAM, lambda e: [
            e.dma_start(out=Cb[:, dc * 512:(dc + 1) * 512], in_=l1v_mov[dc])
            for dc in range(NDC)])
        OP("GP", gp, [], ["RB_D"], lambda e: e.dma_start(
            out=RB_D[0:1, 0:512], in_=bv_row.ap()))
    GP(_)

    def _(te):
        OP("TE", te, ["RB_D", "CONS"], ["PD"], lambda e: mmsteps(
            e, PD, 0, 512, [(ONER, RB_D[0:1, 0:512])]))
    TE(_)

    def _(ve):
        OP("VE", ve, ["PD"], ["BVC"], lambda e: e.tensor_copy(
            BVC[:, 0:512], PD[:, 0:512]))
    VE(_)

    for tb in range(8):
        sl = tb % 2
        pn = "PA" if sl == 0 else "PB"
        sqn = ["SM0", "SM1"][sl]

        def _(te, tb=tb, sl=sl, pn=pn):
            OP("TE", te, ["Cb", "A"], [pn], lambda e: mmsteps(
                e, PSUM[pn], 0, 512,
                [(A[:, dc * 1024 + tb * 128:dc * 1024 + (tb + 1) * 128],
                  Cb[:, dc * 512:(dc + 1) * 512]) for dc in range(NDC)]))
        TE(_)

        def _(ve, tb=tb, sl=sl, pn=pn, sqn=sqn):
            OP("VE", ve, [pn, "BVC"], [sqn], lambda e: e.tensor_tensor(
                [SM0, SM1][sl][:, 0:512], PSUM[pn][:, 0:512],
                BVC[:, 0:512], OP_.add))
        VE(_)

        def _(sy, tb=tb, sl=sl, sqn=sqn):
            r0 = 512 + (tb // 2) * 128
            c0 = (tb % 2) * 512
            OP("SY", sy, [sqn], ["ag1_in"], lambda e: e.dma_start(
                out=ag1_in[r0:r0 + 128, c0:c0 + 512],
                in_=[SM0, SM1][sl][:, 0:512]))
        SY(_)

    def _(gp):
        t = lastw.get("ag1_in")
        if t and seen["GP"].get(t[0], 0) < t[1]:
            gp.wait_ge(sems[t[0]][0], t[1])
            seen["GP"][t[0]] = t[1]
        ins_ = gp.collective_compute(
            "AllGather", OP_.bypass,
            replica_groups=[[0, 1], [2, 3], [4, 5], [6, 7]],
            ins=[ag1_in.ap().opt()], outs=[ag1_out.ap().opt()])
        sems["cAG1"][1] += 1
        ins_.then_inc(sems["cAG1"][0], 1)
        note("ag1_out", ("cAG1", sems["cAG1"][1]))
    GP(_)


    if DEBUG_DUMPS:
        def _(sy):
            OP("SY", sy, ["Bq"], ["dbg_q1"], lambda e: e.dma_start(
                out=dbg["dbg_q1"].ap(), in_=Bq[:, :]))
        SY(_)
    for ncc in range(NDC):
        sl = ncc % 2
        wn = ["WT0", "WT1"][sl]

        def _(sy, ncc=ncc, sl=sl, wn=wn):
            OP("SY", sy, [], [wn], lambda e: e.dma_start(
                out=[WT0, WT1][sl][:, 0:2048], in_=l1w_st[ncc]))
        SY(_)

        def _(ve, ncc=ncc, sl=sl, wn=wn):
            OP("VE", ve, [wn, "AMX2"], ["AMX"], lambda e: e.reduce_max(
                AMX[:, ncc:ncc + 1], [WT0, WT1][sl][:, 0:2048], axis=AX,
                apply_absolute_value=True))
        VE(_)

    def _(ve):
        OP("VE", ve, ["AMX"], ["AMX2"], lambda e: e.reduce_max(
            AMX[:, 17:18], AMX[:, 0:16], axis=AX, apply_absolute_value=True))
    VE(_)

    def _(sy):
        OP("SY", sy, ["AMX2"], ["STTb"], lambda e: e.dma_start(
            out=STT[0:1, 128:256], in_=AMX[:, 17:18]))
        OP("SY", sy, ["ar_out"], ["XSR"], lambda e: e.dma_start(
            out=XSR[0:1, 0:16], in_=ar_out[0:1, :]))
    SY(_)

    def _(ve):
        OP("VE", ve, ["STTb", "XSR"], ["XSR"], lambda e: e.reduce_max(
            XSR[0:1, 1:2], STT[0:1, 128:256], axis=AX,
            apply_absolute_value=True))
        OP("VE", ve, ["XSR"], ["XSR"], lambda e: e.reciprocal(
            XSR[0:1, 2:3], XSR[0:1, 0:1]))
        OP("VE", ve, ["XSR"], ["XSR"], lambda e: e.reciprocal(
            XSR[0:1, 3:4], XSR[0:1, 1:2]))
        OP("VE", ve, ["XSR"], ["XSR"], lambda e: e.tensor_tensor(
            XSR[0:1, 4:5], XSR[0:1, 0:1], XSR[0:1, 1:2], OP_.mult))
    VE(_)

    def _(sy):
        OP("SY", sy, ["XSR"], ["scl_dram"], lambda e: e.dma_start(
            out=scl_dram[0:1, 0:3], in_=XSR[0:1, 2:5]))
        srcap = bass.AP(scl_dram, 0, [[0, 128], [1, 3]])
        OP("SY", sy, ["scl_dram"], ["SCL"], lambda e: e.dma_start(
            out=SCL[:, 0:3], in_=srcap))
    SY(_)

    def _(ve):
        for dc in range(NDC):
            for q4 in range(4):
                c0 = dc * 1024 + q4 * 256
                OP("VE", ve, ["A", "SCL"], ["SM8"],
                   lambda e, c0=c0: e.tensor_scalar(
                       SM8[:, 0:256], A[:, c0:c0 + 256], SCL[:, 0:1], None,
                       OP_.mult))
                OP("VE", ve, ["SM8"], CB_FAM if (dc == 0 and q4 == 0)
                   else ["Cb"],
                   lambda e, c0=c0: e.tensor_copy(Cb[:, c0:c0 + 256],
                                                  SM8[:, 0:256]))
    VE(_)

    for ncc in range(NDC):
        sl = ncc % 2
        pn = "PA" if sl == 0 else "PB"
        wn = ["WT0", "WT1"][sl]
        sqn = ["SM0", "SM1"][sl]

        def _(sy, ncc=ncc, sl=sl, wn=wn):
            OP("SY", sy, [], [wn], lambda e: e.dma_start(
                out=[WT0, WT1][sl][:, 0:2048], in_=l1w_st[ncc]))
        SY(_)

        def _(ve, ncc=ncc, sl=sl, wn=wn):
            for q8 in range(8):
                c0 = q8 * 256
                OP("VE", ve, [wn, "SCL"], ["SM8"],
                   lambda e, c0=c0: e.tensor_scalar(
                       SM8[:, 0:256], [WT0, WT1][sl][:, c0:c0 + 256],
                       SCL[:, 1:2], None, OP_.mult))
                OP("VE", ve, ["SM8"], [wn],
                   lambda e, c0=c0: e.tensor_copy(
                       [WT0, WT1][sl][:, c0:c0 + 256], SM8[:, 0:256]))
        VE(_)

        def _(te, ncc=ncc, sl=sl, pn=pn, wn=wn):
            OP("TE", te, [wn, "Cb"], [pn], lambda e: mmsteps(
                e, PSUM[pn], 0, 1024,
                [([WT0, WT1][sl][:, kcc * 128:(kcc + 1) * 128],
                  Cb[:, kcc * 1024:(kcc + 1) * 1024])
                 for kcc in range(NDC)]))
        TE(_)

        def _(ve, ncc=ncc, sl=sl, pn=pn, sqn=sqn):
            OP("VE", ve, [pn, "SCL"], [sqn], lambda e: e.tensor_scalar(
                [SM0, SM1][sl][:, 0:1024], PSUM[pn][:, 0:1024],
                SCL[:, 2:3], None, OP_.mult))
        VE(_)

        def _(se, ncc=ncc, sl=sl, sqn=sqn):
            OP("SE", se, [sqn, "SVCb"],
               ["Dg"] + DG_FAM[1:] if ncc == 0 else ["Dg"],
               lambda e: e.activation(
                   Dg[:, ncc * 1024:(ncc + 1) * 1024],
                   [SM0, SM1][sl][:, 0:1024], AF.Silu,
                   bias=SVC[:, SV_L1B + ncc:SV_L1B + ncc + 1]))
        SE(_)


    if DEBUG_DUMPS:
        def _(sy):
            OP("SY", sy, ["Dg"], ["dbg_g1"], lambda e: e.dma_start(
                out=dbg["dbg_g1"].ap(), in_=Dg[:, :]))
        SY(_)
    def _(sy):
        def em(e):
            outl = []
            for r in range(2):
                for kc in range(4):
                    outl.append(e.dma_start(
                        out=Cb[:, kc * 2048 + r * 1024:
                               kc * 2048 + (r + 1) * 1024],
                        in_=ag1_out[r * 1024 + kc * 128:
                                    r * 1024 + (kc + 1) * 128, :]))
            for g in range(16):
                r, tb = OWNER[g], LOCIDX[g]
                r0 = r * 1024 + 512 + (tb // 2) * 128
                c0 = (tb % 2) * 512
                outl.append(e.dma_start(
                    out=Cb[:, 8192 + g * 512:8192 + (g + 1) * 512],
                    in_=ag1_out[r0:r0 + 128, c0:c0 + 512]))
            return outl
        OP("SY", sy, ["ag1_out"], CB_FAM, em)
        OP("SY", sy, [], ["AM", "A", "AKV", "AE0", "AE1", "GS", "RB_A"],
           lambda e: e.dma_start(
               out=AMm, in_=am_t.ap().rearrange("j p c -> p j c")))
    SY(_)

    emit_attention(1)

    if DEBUG_DUMPS:
        def _(sy):
            OP("SY", sy, ["Bq"], ["dbg_y1"], lambda e: e.dma_start(
                out=dbg["dbg_y1"].ap(), in_=Bq[:, :]))
        SY(_)

    def _(ve):
        OP("VE", ve, ["PD"], ["GS"], lambda e: e.tensor_copy(
            GSA, PD[0:16, 0:1024]))

        def em(e):
            with nc.allow_low_precision(
                    reason="bf16 softmax denominators, in tol"):
                return e.reciprocal(GSB, GSA)
        OP("VE", ve, ["GS"], ["GS2"], em)
    VE(_)

    for h in range(16):
        def _(sy, h=h):
            OP("SY", sy, ["GS2"], ["RB_A"], lambda e: e.dma_start(
                out=RB_A, in_=GSB[h:h + 1, 0:1024]))
        SY(_)

        def _(te, h=h):
            OP("TE", te, ["RB_A", "CONS"], ["PA"], lambda e: mmsteps(
                e, PA, 0, 1024, [(ONER, RB_A)]))
        TE(_)

        def _(ve, h=h):
            OP("VE", ve, ["Bq", "PA"], ["SM0"], lambda e: e.tensor_tensor(
                SM0[:, 0:1024], Bq[:, h * 1024:(h + 1) * 1024],
                PA[:, 0:1024], OP_.mult))
            OP("VE", ve, ["SM0", "Dg"], ["SM1"], lambda e: e.tensor_tensor(
                SM1[:, 0:1024], SM0[:, 0:1024],
                Dg[:, h * 1024:(h + 1) * 1024], OP_.mult))
            OP("VE", ve, ["SM1", "X"], ["X"], lambda e: e.tensor_tensor(
                X[:, h * 1024:(h + 1) * 1024],
                X[:, h * 1024:(h + 1) * 1024], SM1[:, 0:1024], OP_.add))
        VE(_)

    def _(sy):
        OP("SY", sy, ["X"], ["xout"], lambda e: [
            e.dma_start(out=xout[dc], in_=X[:, dc * 1024:(dc + 1) * 1024])
            for dc in range(NDC)])
        sy.wait_ge(sems["qSY_xout"][0], sems["qSY_xout"][1])
    SY(_)

    blk.__exit__(None, None, None)
    for t in reversed(ctx[:-1]):
        try:
            t.__exit__(None, None, None)
        except Exception:
            pass
    return nc


def _prep_weights(ins):
    f = np.float32
    out = {}
    wqkv = np.asarray(ins["L0_attn_w_qkv"], f)
    wq = wqkv[:, 0:2048].reshape(16, 128, 16, 128).transpose(2, 1, 0, 3)
    wk = wqkv[:, 2048:4096].reshape(16, 128, 16, 128).transpose(2, 1, 0, 3)
    out["wqkv_st"] = np.ascontiguousarray(
        np.concatenate([wq, wk], axis=0).reshape(32, 128, 2048))
    out["wv_mov"] = np.ascontiguousarray(
        wqkv[:, 4096:6144].reshape(16, 128, 2048))
    out["wg_st"] = np.ascontiguousarray(
        np.asarray(ins["L0_attn_w_g"], f).reshape(16, 128, 16, 128)
        .transpose(2, 1, 0, 3).reshape(16, 128, 2048))
    out["wp_st"] = np.ascontiguousarray(
        np.asarray(ins["L0_attn_w_proj"], f).reshape(16, 128, 16, 128)
        .transpose(2, 1, 0, 3).reshape(16, 128, 2048))
    out["rw_st"] = np.ascontiguousarray(
        np.asarray(ins["L0_ffn_router_w"], f).T.reshape(16, 128, 8))
    wgu = np.stack([np.asarray(ins["L0_ffn_wg"], f),
                    np.asarray(ins["L0_ffn_wu"], f)], axis=0)
    out["wgu_st"] = np.ascontiguousarray(
        wgu.reshape(2, 8, 16, 128, 8, 128).transpose(0, 1, 4, 3, 2, 5)
        .reshape(2, 8, 8, 128, 2048))
    out["wd_st"] = np.ascontiguousarray(
        np.asarray(ins["L0_ffn_wd"], f).reshape(8, 8, 128, 16, 128)
        .transpose(0, 3, 2, 1, 4).reshape(8, 16, 128, 1024))
    sgu = np.stack([np.asarray(ins["L0_ffn_sg_w"], f),
                    np.asarray(ins["L0_ffn_su_w"], f)], axis=0)
    out["sgu_st"] = np.ascontiguousarray(
        sgu.reshape(2, 16, 128, 8, 128).transpose(0, 3, 2, 1, 4)
        .reshape(2, 8, 128, 2048))
    out["sd_st"] = np.ascontiguousarray(
        np.asarray(ins["L0_ffn_sd_w"], f).reshape(8, 128, 16, 128)
        .transpose(2, 1, 0, 3).reshape(16, 128, 1024))
    out["l1q_st"] = np.ascontiguousarray(
        np.asarray(ins["L1_attn_Wq"], f).reshape(16, 16, 128, 128)
        .transpose(0, 2, 1, 3).reshape(16, 128, 2048))
    out["l1k_st"] = np.ascontiguousarray(
        np.asarray(ins["L1_attn_Wk"], f).reshape(4, 16, 128, 128)
        .transpose(0, 2, 1, 3).reshape(4, 128, 2048))
    out["l1v_mov"] = np.ascontiguousarray(
        np.asarray(ins["L1_attn_Wv"], f).reshape(4, 16, 128, 128)
        .transpose(1, 2, 0, 3).reshape(16, 128, 512))
    out["bv_row"] = np.ascontiguousarray(
        np.asarray(ins["L1_attn_bv"], f).reshape(1, 512))
    w1 = np.asarray(ins["L1_ffn_weight"])
    out["l1w_st"] = np.ascontiguousarray(
        w1.reshape(16, 128, 16, 128).transpose(0, 3, 2, 1)
        .reshape(16, 128, 2048))
    out["l1b_bf"] = np.ascontiguousarray(
        np.asarray(ins["L1_ffn_bias"]).reshape(16, 128).T)

    sv = np.zeros((128, 128), f)
    sv[:, SV_GNW:SV_GNW + 16] = np.asarray(
        ins["L0_attn_gn_weight"], f).reshape(16, 128).T
    sv[:, SV_GNB:SV_GNB + 16] = np.asarray(
        ins["L0_attn_gn_bias"], f).reshape(16, 128).T
    sv[:, SV_SGB:SV_SGB + 8] = np.asarray(
        ins["L0_ffn_sg_b"], f).reshape(8, 128).T
    sv[:, SV_SUB:SV_SUB + 8] = np.asarray(
        ins["L0_ffn_su_b"], f).reshape(8, 128).T
    sv[:, SV_SDB:SV_SDB + 16] = np.asarray(
        ins["L0_ffn_sd_b"], f).reshape(16, 128).T
    sv[:, SV_BQ:SV_BQ + 16] = np.asarray(
        ins["L1_attn_bq"], f).reshape(16, 128).T
    sv[:, SV_BK:SV_BK + 4] = np.asarray(
        ins["L1_attn_bk"], f).reshape(4, 128).T
    out["smallw"] = sv
    return out


def _xpos_tables(parity):
    f = np.float64
    Cd = 2048
    pos = np.array([ZIG[parity][t // 128] * 128 + (t % 128)
                    for t in range(TPC)], f)
    i2 = np.arange(0, Cd, 2, dtype=f)
    dim_scale = (i2 + 0.4 * Cd) / (1.4 * Cd)
    scale = dim_scale[None, :] ** (pos[:, None] / 512.0)
    inv_freq = 1.0 / 10000.0 ** (i2 / Cd)
    sinu = pos[:, None] * inv_freq[None, :]
    sin = np.repeat(np.sin(sinu) * scale, 2, axis=1)
    cos = np.repeat(np.cos(sinu) * scale, 2, axis=1)
    gam = 1.0 - 2.0 ** (-5.0 - np.arange(16, dtype=f))
    gq = gam[np.arange(Cd) // 128][:, None] ** pos[None, :]
    gk = gam[np.arange(Cd) // 128][:, None] ** (-pos[None, :])
    cq = cos.T * gq * (128.0 ** -0.5)
    sq = sin.T * gq * (128.0 ** -0.5)
    ck = cos.T * gk
    sk = sin.T * gk

    def tile(m):
        return m.reshape(16, 128, TPC).astype(ml_dtypes.bfloat16)
    tq = np.stack([tile(cq), tile(sq)], axis=1)
    tk = np.stack([tile(ck), tile(sk)], axis=1)
    return np.ascontiguousarray(tq), np.ascontiguousarray(tk)


def _mask_tables(parity):
    am = np.zeros((16, 128, 256), np.float32)
    zz = ZIG[parity]
    tri = (np.arange(128)[:, None] <= np.arange(128)[None, :]).astype(
        np.float32)
    for j in range(16):
        L = LPROF[j]
        for p01 in range(min(2, L)):
            gb = zz[8 - L + p01]
            if gb < j:
                blkm = np.zeros((128, 128), np.float32)
            elif gb == j:
                blkm = tri
            else:
                blkm = np.ones((128, 128), np.float32)
            am[j, :, p01 * 128:(p01 + 1) * 128] = blkm
    return np.ascontiguousarray(am.astype(ml_dtypes.bfloat16))


def _consts_arr():
    c = np.zeros((128, 648), np.float32)
    for i in range(64):
        c[2 * i + 1, 2 * i] = -1.0
        c[2 * i, 2 * i + 1] = 1.0
    for h in range(16):
        c[:, CN_ECOL + h * 16 + h] = 1.0
    c[:, CN_ONE] = 1.0
    c[0, CN_ONER:CN_ONER + 128] = 1.0
    return np.ascontiguousarray(c.astype(ml_dtypes.bfloat16))


_NC_CACHE = {}


def kernel(**ins):
    if "nc" not in _NC_CACHE:
        _NC_CACHE["nc"] = build_program()
    nc = _NC_CACHE["nc"]

    w = _prep_weights(ins)
    consts = _consts_arr()
    identf = np.eye(128, dtype=np.float32)
    tabs = [_xpos_tables(0), _xpos_tables(1)]
    ams = [_mask_tables(0), _mask_tables(1)]

    x = np.asarray(ins["L0_attn_x"], np.float32)
    in_maps = []
    for core in range(8):
        b, par = core // 2, core % 2
        xs = np.empty((16, 128, TPC), np.float32)
        for lb in range(8):
            g = ZIG[par][lb]
            blkx = x[b, g * 128:(g + 1) * 128, :]
            xs[:, :, lb * 128:(lb + 1) * 128] = blkx.T.reshape(16, 128, 128)
        m = dict(w)
        m["x0"] = xs
        m["tabq"], m["tabk"] = tabs[par]
        m["am_t"] = ams[par]
        m["consts"] = consts
        m["identf"] = identf
        in_maps.append(m)

    res = run_bass_kernel_spmd(nc, in_maps, core_ids=list(range(8)))

    out = np.empty((4, 2048, 2048), np.float32)
    for core in range(8):
        b, par = core // 2, core % 2
        xo = res.results[core]["xout"]
        for lb in range(8):
            g = ZIG[par][lb]
            out[b, g * 128:(g + 1) * 128, :] = \
                xo[:, :, lb * 128:(lb + 1) * 128].reshape(2048, 128).T
    return out


if __name__ == "__main__":
    nc = build_program()
    print("build ok:", sum(1 for _ in nc.all_instructions()), "instructions")


# version 08
# speedup vs baseline: 1.2006x; 1.0245x over previous
import math
import numpy as np
import ml_dtypes
import concourse.bass as bass
import concourse.mybir as mybir
from concourse.bass_utils import run_bass_kernel_spmd

F32 = mybir.dt.float32
BF16 = mybir.dt.bfloat16
FP8 = mybir.dt.float8e4
AX = mybir.AxisListType.X
OP_ = mybir.AluOpType
AF = mybir.ActivationFunctionType

NDC = 16
TPC = 1024
DEBUG_DUMPS = False
ZIG = [[0, 3, 4, 7, 8, 11, 12, 15], [1, 2, 5, 6, 9, 10, 13, 14]]
OWNER = [0 if g in ZIG[0] else 1 for g in range(16)]
LOCIDX = [ZIG[OWNER[g]].index(g) for g in range(16)]
LEN = [[len([b_ for b_ in ZIG[p] if b_ >= j]) for j in range(16)]
       for p in (0, 1)]
LPROF = [max(LEN[0][j], LEN[1][j]) for j in range(16)]
SOFF = [(8 - LPROF[j]) * 128 for j in range(16)]
WJ = [LPROF[j] * 128 for j in range(16)]
INV_SQRT_D = 1.0 / math.sqrt(128.0)

SV_GNW, SV_GNB, SV_SGB, SV_SUB, SV_SDB, SV_BQ, SV_BK, SV_L1B = \
    0, 16, 32, 40, 48, 64, 80, 84
CN_PSWAP, CN_ECOL, CN_ONE, CN_ONER, CN_RWB = 0, 128, 384, 385, 516


def col_chunks(c0, c1):
    out = []
    c = c0
    while c < c1:
        nxt = min(c1, (c // 512 + 1) * 512)
        out.append((c, nxt))
        c = nxt
    return out


def build_program():
    nc = bass.Bass("TRN2", target_bir_lowering=False)

    def dram_in(name, shape, dt=F32):
        return nc.dram_tensor(name, shape, dt, kind="ExternalInput")

    x0 = dram_in("x0", [NDC, 128, TPC])
    wqkv_st = dram_in("wqkv_st", [32, 128, 2048])
    wv_mov = dram_in("wv_mov", [16, 128, 2048])
    wg_st = dram_in("wg_st", [16, 128, 2048])
    wp_st = dram_in("wp_st", [16, 128, 2048])
    rw_st = dram_in("rw_st", [16, 128, 8])
    wgu_st = dram_in("wgu_st", [2, 8, 8, 128, 2048])
    wd_st = dram_in("wd_st", [8, 16, 128, 1024])
    sgu_st = dram_in("sgu_st", [2, 8, 128, 2048])
    sd_st = dram_in("sd_st", [16, 128, 1024])
    smallw = dram_in("smallw", [128, 128])
    l1q_st = dram_in("l1q_st", [16, 128, 2048])
    l1k_st = dram_in("l1k_st", [4, 128, 2048])
    l1v_mov = dram_in("l1v_mov", [16, 128, 512])
    bv_row = dram_in("bv_row", [1, 512])
    l1w_st = dram_in("l1w_st", [16, 128, 2048], BF16)
    l1b_bf = dram_in("l1b_bf", [128, 16], BF16)
    tabq = dram_in("tabq", [16, 2, 128, TPC], BF16)
    tabk = dram_in("tabk", [16, 2, 128, TPC], BF16)
    am_t = dram_in("am_t", [16, 128, 256], BF16)
    consts = dram_in("consts", [128, 648], BF16)
    identf = dram_in("identf", [128, 128])

    ag0_in = nc.dram_tensor("ag0_in", [4096, 1024], BF16)
    ag0_out = nc.dram_tensor("ag0_out", [8192, 1024], BF16)
    ag1_in = nc.dram_tensor("ag1_in", [1024, 1024], BF16)
    ag1_out = nc.dram_tensor("ag1_out", [2048, 1024], BF16)
    ar_in = nc.dram_tensor("ar_in", [1, 16], F32)
    ar_out = nc.dram_tensor("ar_out", [1, 16], F32)
    scl_dram = nc.dram_tensor("scl_dram", [1, 4], F32)

    xout = nc.dram_tensor("xout", [NDC, 128, TPC], F32, kind="ExternalOutput")
    dbg = {}
    if DEBUG_DUMPS:
        for nm, dt_ in [("dbg_xn1", BF16), ("dbg_q0", BF16), ("dbg_y0", BF16),
                        ("dbg_g0", BF16), ("dbg_gated", BF16),
                        ("dbg_x1", F32), ("dbg_q1", BF16), ("dbg_y1", BF16),
                        ("dbg_g1", BF16)]:
            dbg[nm] = nc.dram_tensor(nm, [128, NDC * 1024], dt_)

    ctx = []

    def sb(name, shape, dt):
        t = nc.sbuf_tensor(name, shape, dt)
        ctx.append(t)
        return t.__enter__()

    def psum(name, shape, dt=F32):
        t = nc.psum_tensor(name, shape, dt)
        ctx.append(t)
        return t.__enter__()

    X = sb("X", [128, NDC * 1024], F32)
    A = sb("A", [128, NDC * 1024], BF16)
    Bq = sb("Bq", [128, NDC * 1024], BF16)
    Cb = sb("Cb", [128, NDC * 1024], BF16)
    Dg = sb("Dg", [128, NDC * 1024], BF16)
    WT0 = sb("WT0", [128, 2048], BF16)
    WT1 = sb("WT1", [128, 2048], BF16)
    SM0 = sb("SM0", [128, 1024], BF16)
    SM1 = sb("SM1", [128, 1024], BF16)
    SVC = sb("SVC", [128, 128], F32)
    CONS = sb("CONS", [128, 648], BF16)
    IDF = sb("IDF", [128, 128], F32)
    STT = sb("STT", [128, 256], F32)
    XSR = sb("XSR", [1, 16], F32)
    AMX = sb("AMX", [128, 32], F32)
    SM8 = sb("SM8", [128, 256], FP8)
    SCL = sb("SCL", [128, 4], F32)

    PA = psum("PA", [128, 1024])
    PB = psum("PB", [128, 1024])
    PC = psum("PC", [128, 1024])
    PD = psum("PD", [128, 1024])
    PSUM = {"PA": PA, "PB": PB, "PC": PC, "PD": PD}

    ONES_COL = CONS[:, CN_ONE:CN_ONE + 1]
    ONER = CONS[0:1, CN_ONER:CN_ONER + 128]
    PSWAP = CONS[:, CN_PSWAP:CN_PSWAP + 128]
    RWB = CONS[:, CN_RWB:CN_RWB + 128]

    def ECOLt(h):
        return CONS[:, CN_ECOL + h * 16:CN_ECOL + (h + 1) * 16]

    TBL = [Dg[:, 0:2048], Dg[:, 2048:4096]]
    T1S = [Dg[:, 4096:5120], Dg[:, 5120:6144]]
    RB_D = Dg[0:1, 6144:7168]
    VW = Dg[:, 0:8192]
    BVC = Dg[:, 8192:8704]
    RB_D2 = Dg[0:1, 8704:9728]
    HCB = Cb[:, 14 * 1024:15 * 1024]
    RB_C = Cb[0:1, 13 * 1024:13 * 1024 + 1024]
    CT = Cb[0:8, 15 * 1024:16 * 1024]
    AK = A[:, 0:2048]
    AVt = A[:, 2048:4096]
    AE = [A[:, 4096:5120], A[:, 5120:6144]]
    AMm = A[:, 6144:10240]
    GSA = A[0:16, 10240:11264]
    GSB = A[0:16, 11264:12288]
    RB_As = [A[0:1, 12288:14336], A[0:1, 14336:16384]]

    A_FAM = ["A", "AKV", "AE0", "AE1", "AM", "GS", "RB_A0", "RB_A1"]
    CB_FAM = ["Cb", "HCB", "RB_C", "CT"]
    DG_FAM = ["Dg", "TBL0", "TBL1", "T1S0", "T1S1", "RB_D", "BVC", "VW", "RB_D2"]

    blk = nc.Block()
    blk.__enter__()
    ctx.append(blk)

    sems = {}

    def sem(name):
        h = nc.semaphore(name)
        ctx.append(h)
        sems[name] = [h.__enter__(), 0]

    for s in ["eT", "eV", "eS", "cAG0", "cAG1", "cAR"]:
        sem(s)

    ENGSEM = {"TE": "eT", "VE": "eV", "SE": "eS"}
    DMAE = {"SY", "GP", "SQ"}

    lastw = {}
    readers = {}
    seen = {k: {} for k in ("TE", "VE", "SE", "SY", "GP", "SQ")}
    PARENT = {"TBL0": "Dg", "TBL1": "Dg", "T1S0": "Dg", "T1S1": "Dg",
              "RB_D": "Dg", "BVC": "Dg", "VW": "Dg", "RB_D2": "Dg", "HCB": "Cb", "RB_C": "Cb",
              "CT": "Cb", "AKV": "A", "AE0": "A", "AE1": "A", "AM": "A",
              "GS": "A", "GS2": "A", "RB_A0": "A", "RB_A1": "A"}

    def OP(engname, eng, reads, writes, emit, n_dma=1):
        toks = []
        for r in reads:
            t = lastw.get(r)
            if t:
                toks.append(t)
            p = PARENT.get(r)
            if p:
                t = lastw.get(p)
                if t:
                    toks.append(t)
        for w in writes:
            t = lastw.get(w)
            if t:
                toks.append(t)
            toks.extend(readers.get(w, ()))
            p = PARENT.get(w)
            if p:
                t = lastw.get(p)
                if t:
                    toks.append(t)
                toks.extend(readers.get(p, ()))
        agg = {}
        for (s, v) in toks:
            agg[s] = max(agg.get(s, 0), v)
        for s, v in sorted(agg.items()):
            if seen[engname].get(s, 0) < v:
                eng.wait_ge(sems[s][0], v)
                seen[engname][s] = v
        out = emit(eng)
        if engname in DMAE:
            sname = "q%s_%s" % (engname, writes[0])
            if sname not in sems:
                sem(sname)
            step = 16
        else:
            sname = ENGSEM[engname]
            step = 1
        if isinstance(out, (list, tuple)):
            for ins_ in out:
                sems[sname][1] += step
                ins_.then_inc(sems[sname][0], step)
        else:
            sems[sname][1] += step
            out.then_inc(sems[sname][0], step)
        tok = (sname, sems[sname][1])
        for r in reads:
            readers.setdefault(r, []).append(tok)
        for w in writes:
            lastw[w] = tok
            readers[w] = []
        return tok

    def TOK(name, v=None):
        return (name, sems[name][1] if v is None else v)

    def note(res, tok):
        lastw[res] = tok
        readers[res] = []

    def TE(fn):
        blk.tensor(fn)

    def VE(fn):
        blk.vector(fn)

    def SE(fn):
        blk.scalar(fn)

    def GP(fn):
        blk.gpsimd(fn)

    def SY(fn):
        blk.sync(fn)

    def mmsteps(te, P, c0, c1, steps, rbase=None, start=True, stop=True,
                rows=128):
        rb = c0 if rbase is None else rbase
        n = len(steps)
        last = None
        for (a, b_) in col_chunks(c0, c1):
            for i, (lh, rh) in enumerate(steps):
                last = te.matmul(P[0:rows, a:b_], lh, rh[:, a - rb:b_ - rb],
                                 start=(start and i == 0),
                                 stop=(stop and i == n - 1))
        return last

    def _(sy):
        OP("SY", sy, [], ["X"], lambda e: [
            e.dma_start(out=X[:, dc * 1024:(dc + 1) * 1024], in_=x0[dc])
            for dc in range(NDC)])
        OP("SY", sy, [], ["SVC"], lambda e: e.dma_start(
            out=SVC[:, 0:SV_L1B], in_=smallw[0:128, 0:SV_L1B]))
        OP("SY", sy, [], ["CONS"], lambda e: e.dma_start(
            out=CONS[:, 0:CN_RWB], in_=consts[0:128, 0:CN_RWB]))
        OP("SY", sy, [], ["IDF"], lambda e: e.dma_start(
            out=IDF[:, :], in_=identf.ap()))
    SY(_)

    def _(gp):
        OP("GP", gp, [], ["RWB"], lambda e: [
            e.dma_start(out=RWB[:, dc * 8:(dc + 1) * 8], in_=rw_st[dc])
            for dc in range(NDC)])
        OP("GP", gp, [], ["SVCb"], lambda e: e.dma_start(
            out=SVC[:, SV_L1B:SV_L1B + 16], in_=l1b_bf.ap()))
    GP(_)

    def _(ve):
        OP("VE", ve, [], ["XSR"], lambda e: e.memset(XSR[:, :], 0.0))
    VE(_)

    def emit_layernorm(rb_row, rb_res):
        def _(ve):
            OP("VE", ve, ["X"], ["A"] + A_FAM[1:], lambda e: [
                e.tensor_copy(A[:, dc * 1024:(dc + 1) * 1024],
                              X[:, dc * 1024:(dc + 1) * 1024])
                for dc in range(NDC)][-1:])
        VE(_)

        def _(se):
            OP("SE", se, ["A"], CB_FAM, lambda e: [
                e.activation(Cb[:, dc * 1024:(dc + 1) * 1024],
                             A[:, dc * 1024:(dc + 1) * 1024], AF.Square)
                for dc in range(NDC)][-1:])
        SE(_)

        def _(te):
            def em(e):
                last = None
                for tb in range(8):
                    for dc in range(NDC):
                        last = e.matmul(
                            PA[:, tb:tb + 1],
                            A[:, dc * 1024 + tb * 128:
                              dc * 1024 + (tb + 1) * 128],
                            ONES_COL, start=(dc == 0), stop=(dc == 15))
                    for dc in range(NDC):
                        last = e.matmul(
                            PA[:, 512 + tb:512 + tb + 1],
                            Cb[:, dc * 1024 + tb * 128:
                               dc * 1024 + (tb + 1) * 128],
                            ONES_COL, start=(dc == 0), stop=(dc == 15))
                return last
            OP("TE", te, ["A", "Cb", "CONS"], ["PA"], em)
        TE(_)

        def _(ve):
            OP("VE", ve, ["PA"], ["STTa"], lambda e: e.tensor_scalar(
                STT[:, 0:8], PA[:, 0:8], 1.0 / 2048.0, None, OP_.mult))
            OP("VE", ve, ["PA"], ["STTb"], lambda e: e.tensor_scalar(
                STT[:, 8:16], PA[:, 512:520], 1.0 / 2048.0, None, OP_.mult))
            OP("VE", ve, ["STTa"], ["STTc"], lambda e: e.tensor_tensor(
                STT[:, 16:24], STT[:, 0:8], STT[:, 0:8], OP_.mult))
            OP("VE", ve, ["STTb", "STTc"], ["STTb"], lambda e: e.tensor_tensor(
                STT[:, 8:16], STT[:, 8:16], STT[:, 16:24], OP_.subtract))
            OP("VE", ve, ["STTb"], ["STTb"], lambda e: e.tensor_scalar(
                STT[:, 8:16], STT[:, 8:16], 1e-5, None, OP_.add))
        VE(_)

        def _(se):
            OP("SE", se, ["STTb"], ["STTc"], lambda e: e.activation(
                STT[:, 16:24], STT[:, 8:16], AF.Sqrt))
        SE(_)

        def _(ve):
            OP("VE", ve, ["STTc"], ["STTb"], lambda e: e.reciprocal(
                STT[:, 8:16], STT[:, 16:24]))
            OP("VE", ve, ["STTa", "STTb"], ["STTc"], lambda e: e.tensor_tensor(
                STT[:, 16:24], STT[:, 0:8], STT[:, 8:16], OP_.mult))
            OP("VE", ve, ["STTc"], ["STTc"], lambda e: e.tensor_scalar(
                STT[:, 16:24], STT[:, 16:24], -1.0, None, OP_.mult))
        VE(_)

        def _(te):
            def em(e):
                e.transpose(PC[0:8, 0:128], STT[:, 8:16], IDF[:, :])
                return e.transpose(PC[0:8, 128:256], STT[:, 16:24],
                                   IDF[:, :])
            OP("TE", te, ["STTb", "STTc", "IDF"], ["PC"], em)
        TE(_)

        def _(ve):
            OP("VE", ve, ["PC"], ["SM0"], lambda e: e.tensor_copy(
                SM0[0:8, 0:256], PC[0:8, 0:256]))
        VE(_)

        def _(sy):
            OP("SY", sy, ["SM0"], [rb_res], lambda e: e.dma_start(
                out=rb_row, in_=SM0[0:8, 0:128]))
        SY(_)

        def _(te):
            OP("TE", te, [rb_res, "CONS"], ["PA"], lambda e: mmsteps(
                e, PA, 0, 1024, [(ONER, rb_row)]))
        TE(_)

        def _(sy):
            OP("SY", sy, ["SM0"], [rb_res], lambda e: e.dma_start(
                out=rb_row, in_=SM0[0:8, 128:256]))
        SY(_)

        def _(te):
            OP("TE", te, [rb_res, "CONS"], ["PB"], lambda e: mmsteps(
                e, PB, 0, 1024, [(ONER, rb_row)]))
        TE(_)

        def _(ve):
            OP("VE", ve, ["X", "PA"], CB_FAM, lambda e: [
                e.tensor_tensor(Cb[:, dc * 1024:(dc + 1) * 1024],
                                X[:, dc * 1024:(dc + 1) * 1024],
                                PA[:, 0:1024], OP_.mult)
                for dc in range(NDC)][-1:])
            OP("VE", ve, ["Cb", "PB"], ["A"] + A_FAM[1:], lambda e: [
                e.tensor_tensor(A[:, dc * 1024:(dc + 1) * 1024],
                                Cb[:, dc * 1024:(dc + 1) * 1024],
                                PB[:, 0:1024], OP_.add)
                for dc in range(NDC)][-1:])
        VE(_)

    emit_layernorm(RB_D, "RB_D")


    if DEBUG_DUMPS:
        def _(sy):
            OP("SY", sy, ["A"], ["dbg_xn1"], lambda e: e.dma_start(
                out=dbg["dbg_xn1"].ap(), in_=A[:, :]))
        SY(_)
    for cc in range(32):
        sl = cc % 2
        pn = "PA" if sl == 0 else "PB"
        rn = "PC" if sl == 0 else "PD"
        qb = [SM0, SM1][sl]
        qn = ["SM0", "SM1"][sl]
        t1 = T1S[sl]
        tn = ["T1S0", "T1S1"][sl]
        wn = ["WT0", "WT1"][sl]
        tbl = TBL[sl]
        tbn = ["TBL0", "TBL1"][sl]

        def _(gp, cc=cc, sl=sl, wn=wn):
            OP("GP", gp, [], [wn], lambda e: e.dma_start(
                out=[WT0, WT1][sl][:, 0:2048], in_=wqkv_st[cc]))
        GP(_)

        def _(sq, cc=cc, sl=sl, tbl=tbl, tbn=tbn):
            src = tabq if cc < 16 else tabk
            OP("SQ", sq, [], [tbn], lambda e: e.dma_start(
                out=tbl[:, 0:2048],
                in_=src[cc % 16].rearrange("w p t -> p w t")))
        SE(_)

        def _(te, cc=cc, sl=sl, pn=pn, wn=wn):
            OP("TE", te, [wn, "A"], [pn], lambda e: mmsteps(
                e, PSUM[pn], 0, 1024,
                [([WT0, WT1][sl][:, dc * 128:(dc + 1) * 128],
                  A[:, dc * 1024:(dc + 1) * 1024]) for dc in range(NDC)]))
        TE(_)

        def _(ve, cc=cc, sl=sl, pn=pn, qb=qb, qn=qn):
            OP("VE", ve, [pn], [qn], lambda e: e.tensor_copy(
                qb[:, 0:1024], PSUM[pn][:, 0:1024]))
        VE(_)

        def _(te, cc=cc, sl=sl, rn=rn, qb=qb, qn=qn):
            OP("TE", te, [qn, "CONS"], [rn], lambda e: mmsteps(
                e, PSUM[rn], 0, 1024, [(PSWAP, qb[:, 0:1024])]))
        TE(_)

        def _(ve, cc=cc, sl=sl, rn=rn, qb=qb, qn=qn, t1=t1, tn=tn, tbn=tbn):
            tbl_ = TBL[sl]
            OP("VE", ve, [qn, tbn], [tn], lambda e: e.tensor_tensor(
                t1[:, 0:1024], qb[:, 0:1024], tbl_[:, 0:1024], OP_.mult))
            OP("VE", ve, [rn, tbn], [qn], lambda e: e.tensor_tensor(
                qb[:, 0:1024], PSUM[rn][:, 0:1024], tbl_[:, 1024:2048],
                OP_.mult))
            if cc < 16:
                OP("VE", ve, [tn, qn], ["Bq"], lambda e: e.tensor_tensor(
                    Bq[:, cc * 1024:(cc + 1) * 1024], t1[:, 0:1024],
                    qb[:, 0:1024], OP_.add))
            else:
                OP("VE", ve, [tn, qn], [tn], lambda e: e.tensor_tensor(
                    t1[:, 0:1024], t1[:, 0:1024], qb[:, 0:1024], OP_.add))
        VE(_)

        if cc >= 16:
            def _(sy, cc=cc, t1=t1, tn=tn):
                OP("SY", sy, [tn], ["ag0_in"], lambda e: e.dma_start(
                    out=ag0_in[(cc - 16) * 128:(cc - 15) * 128, :],
                    in_=t1[:, 0:1024]))
            SY(_)

    for dvq in range(2):
        def _(gp, dvq=dvq):
            OP("GP", gp, [], CB_FAM, lambda e: [
                e.dma_start(out=Cb[:, dc * 1024:(dc + 1) * 1024],
                            in_=wv_mov[dc, :, dvq * 1024:(dvq + 1) * 1024])
                for dc in range(NDC)])
        GP(_)

        for tb in range(8):
            sl = tb % 2
            pn = "PA" if sl == 0 else "PB"

            def _(te, dvq=dvq, tb=tb, sl=sl, pn=pn):
                OP("TE", te, ["Cb", "A"], [pn], lambda e: mmsteps(
                    e, PSUM[pn], 0, 1024,
                    [(A[:, dc * 1024 + tb * 128:dc * 1024 + (tb + 1) * 128],
                      Cb[:, dc * 1024:(dc + 1) * 1024])
                     for dc in range(NDC)]))
            TE(_)

            def _(ve, dvq=dvq, tb=tb, sl=sl, pn=pn):
                OP("VE", ve, [pn], [["SM0", "SM1"][sl]],
                   lambda e: e.tensor_copy([SM0, SM1][sl][:, 0:1024],
                                           PSUM[pn][:, 0:1024]))
            VE(_)

            def _(sy, dvq=dvq, tb=tb, sl=sl):
                r0 = 2048 + (tb * 2 + dvq) * 128
                OP("SY", sy, [["SM0", "SM1"][sl]], ["ag0_in"],
                   lambda e: e.dma_start(out=ag0_in[r0:r0 + 128, :],
                                         in_=[SM0, SM1][sl][:, 0:1024]))
            SY(_)

    def _(gp):
        def em(e):
            return e.collective_compute(
                "AllGather", OP_.bypass,
                replica_groups=[[0, 1], [2, 3], [4, 5], [6, 7]],
                ins=[ag0_in.ap().opt()], outs=[ag0_out.ap().opt()])
        toks = []
        t = lastw.get("ag0_in")
        if t:
            toks.append(t)
        for (s, v) in toks:
            if seen["GP"].get(s, 0) < v:
                gp.wait_ge(sems[s][0], v)
                seen["GP"][s] = v
        ins_ = em(gp)
        sems["cAG0"][1] += 1
        ins_.then_inc(sems["cAG0"][0], 1)
        note("ag0_out", ("cAG0", sems["cAG0"][1]))
    GP(_)


    if DEBUG_DUMPS:
        def _(sy):
            OP("SY", sy, ["Bq"], ["dbg_q0"], lambda e: e.dma_start(
                out=dbg["dbg_q0"].ap(), in_=Bq[:, :]))
        SY(_)
    def _(te):
        def em(e):
            last = None
            for tb in range(8):
                for dc in range(NDC):
                    last = e.matmul(
                        PB[:, tb * 8:(tb + 1) * 8],
                        A[:, dc * 1024 + tb * 128:dc * 1024 + (tb + 1) * 128],
                        RWB[:, dc * 8:(dc + 1) * 8],
                        start=(dc == 0), stop=(dc == 15))
            return last
        OP("TE", te, ["A", "RWB"], ["PB"], em)
    TE(_)

    def _(ve):
        for tb in range(8):
            lo = PB[:, tb * 8:(tb + 1) * 8]
            OP("VE", ve, ["PB"], ["STTa"], lambda e, lo=lo: e.reduce_max(
                STT[:, 24:25], lo, axis=AX))
            OP("VE", ve, ["PB", "STTa"], ["STTb"],
               lambda e, lo=lo: e.tensor_scalar(
                   STT[:, 32:40], lo, STT[:, 24:25], None, OP_.subtract))
            OP("VE", ve, ["STTb"], ["STTc"], lambda e: e.tensor_scalar(
                STT[:, 40:48], STT[:, 32:40], 0.0, None, OP_.is_ge))
            OP("VE", ve, ["STTc"], ["STTd"], lambda e: e.tensor_scalar(
                STT[:, 48:56], STT[:, 40:48], -1e30, None, OP_.mult))
            OP("VE", ve, ["PB", "STTd"], ["STTb"],
               lambda e, lo=lo: e.tensor_tensor(
                   STT[:, 32:40], lo, STT[:, 48:56], OP_.add))
            OP("VE", ve, ["STTb"], ["STTa"], lambda e: e.reduce_max(
                STT[:, 25:26], STT[:, 32:40], axis=AX))
            OP("VE", ve, ["PB", "STTa"], ["STTc"],
               lambda e, lo=lo: e.tensor_scalar(
                   STT[:, 40:48], lo, STT[:, 25:26], None, OP_.is_ge))
            OP("VE", ve, ["PB", "STTc"], ["COMB"],
               lambda e, lo=lo, tb=tb: e.tensor_tensor(
                   STT[:, 160 + tb * 8:160 + (tb + 1) * 8], lo,
                   STT[:, 40:48], OP_.mult))
    VE(_)

    def _(te):
        def em(e):
            last = None
            for tb in range(8):
                last = e.transpose(PC[0:8, tb * 128:(tb + 1) * 128],
                                   STT[:, 160 + tb * 8:160 + (tb + 1) * 8],
                                   IDF[:, :])
            return last
        OP("TE", te, ["COMB", "IDF"], ["PC"], em)
    TE(_)

    def _(ve):
        OP("VE", ve, ["PC"], ["CT"], lambda e: e.tensor_copy(
            CT, PC[0:8, 0:1024]))
    VE(_)

    for e_ in range(9):
        shared = (e_ == 8)
        if not shared:
            def _(sy, e_=e_):
                OP("SY", sy, ["CT"], ["RB_C"], lambda e: e.dma_start(
                    out=RB_C, in_=CT[e_:e_ + 1, 0:1024]))
            SY(_)

            def _(te, e_=e_):
                OP("TE", te, ["RB_C", "CONS"], ["PC"], lambda e: mmsteps(
                    e, PC, 0, 1024, [(ONER, RB_C)]))
            TE(_)

            def _(ve, e_=e_):
                OP("VE", ve, ["PC"], ["HCB"], lambda e: e.tensor_copy(
                    HCB[:, 0:1024], PC[:, 0:1024]))
            VE(_)

        for icc in range(8):
            def _(gp, e_=e_, icc=icc):
                if shared:
                    OP("GP", gp, [], ["WT0"], lambda e: e.dma_start(
                        out=WT0[:, 0:2048], in_=sgu_st[0, icc]))
                    OP("GP", gp, [], ["WT1"], lambda e: e.dma_start(
                        out=WT1[:, 0:2048], in_=sgu_st[1, icc]))
                else:
                    OP("GP", gp, [], ["WT0"], lambda e: e.dma_start(
                        out=WT0[:, 0:2048], in_=wgu_st[0, e_, icc]))
                    OP("GP", gp, [], ["WT1"], lambda e: e.dma_start(
                        out=WT1[:, 0:2048], in_=wgu_st[1, e_, icc]))
            GP(_)

            def _(te, e_=e_, icc=icc):
                OP("TE", te, ["WT0", "A"], ["PA"], lambda e: mmsteps(
                    e, PA, 0, 1024,
                    [(WT0[:, dc * 128:(dc + 1) * 128],
                      A[:, dc * 1024:(dc + 1) * 1024])
                     for dc in range(NDC)]))
                OP("TE", te, ["WT1", "A"], ["PB"], lambda e: mmsteps(
                    e, PB, 0, 1024,
                    [(WT1[:, dc * 128:(dc + 1) * 128],
                      A[:, dc * 1024:(dc + 1) * 1024])
                     for dc in range(NDC)]))
            TE(_)

            def _(se, e_=e_, icc=icc, shared=shared):
                OP("SE", se, ["PA", "SVC"], ["SM0"], lambda e: e.activation(
                    SM0[:, 0:1024], PA[:, 0:1024], AF.Silu,
                    bias=(SVC[:, SV_SGB + icc:SV_SGB + icc + 1] if shared
                          else 0.0)))
            SE(_)

            def _(ve, e_=e_, icc=icc, shared=shared):
                if shared:
                    OP("VE", ve, ["PB", "SVC"], ["SM1"],
                       lambda e: e.tensor_scalar(
                           SM1[:, 0:1024], PB[:, 0:1024],
                           SVC[:, SV_SUB + icc:SV_SUB + icc + 1], None,
                           OP_.add))
                else:
                    OP("VE", ve, ["PB", "HCB"], ["SM1"],
                       lambda e: e.tensor_tensor(
                           SM1[:, 0:1024], PB[:, 0:1024], HCB[:, 0:1024],
                           OP_.mult))
                OP("VE", ve, ["SM0", "SM1"], ["Cb"],
                   lambda e: e.tensor_tensor(
                       Cb[:, icc * 1024:(icc + 1) * 1024],
                       SM0[:, 0:1024], SM1[:, 0:1024], OP_.mult))
            VE(_)

        for occ in range(NDC):
            sl = occ % 2
            pn = "PA" if sl == 0 else "PB"
            wn = ["WT0", "WT1"][sl]

            def _(gp, e_=e_, occ=occ, sl=sl, wn=wn, shared=shared):
                src = sd_st[occ] if shared else wd_st[e_, occ]
                OP("GP", gp, [], [wn], lambda e: e.dma_start(
                    out=[WT0, WT1][sl][:, 0:1024], in_=src))
            GP(_)

            def _(te, e_=e_, occ=occ, sl=sl, pn=pn, wn=wn):
                OP("TE", te, [wn, "Cb"], [pn], lambda e: mmsteps(
                    e, PSUM[pn], 0, 1024,
                    [([WT0, WT1][sl][:, icc * 128:(icc + 1) * 128],
                      Cb[:, icc * 1024:(icc + 1) * 1024])
                     for icc in range(8)]))
            TE(_)

            if e_ == 0:
                def _(se, occ=occ, sl=sl, pn=pn):
                    OP("SE", se, [pn], ["Dg"] + DG_FAM[1:],
                       lambda e: e.activation(
                           Dg[:, occ * 1024:(occ + 1) * 1024],
                           PSUM[pn][:, 0:1024], AF.Copy))
                SE(_)
            else:
                def _(ve, occ=occ, sl=sl, pn=pn):
                    def em(e):
                        with nc.allow_low_precision(
                                reason="bf16 moe partial-sum acc, in tol"):
                            return e.tensor_tensor(
                                Dg[:, occ * 1024:(occ + 1) * 1024],
                                Dg[:, occ * 1024:(occ + 1) * 1024],
                                PSUM[pn][:, 0:1024], OP_.add)
                    OP("VE", ve, [pn, "Dg"], ["Dg"], em)
                VE(_)

    def _(se):
        OP("SE", se, ["Dg", "SVC"], ["Dg"], lambda e: [
            e.activation(Dg[:, occ * 1024:(occ + 1) * 1024],
                         Dg[:, occ * 1024:(occ + 1) * 1024], AF.Silu,
                         bias=SVC[:, SV_SDB + occ:SV_SDB + occ + 1])
            for occ in range(NDC)][-1:])
    SE(_)


    if DEBUG_DUMPS:
        def _(sy):
            OP("SY", sy, ["Dg"], ["dbg_g0"], lambda e: e.dma_start(
                out=dbg["dbg_g0"].ap(), in_=Dg[:, :]))
        SY(_)
    for cc in range(NDC):
        sl = cc % 2
        pn = "PA" if sl == 0 else "PB"
        wn = ["WT0", "WT1"][sl]

        def _(gp, cc=cc, sl=sl, wn=wn):
            OP("GP", gp, [], [wn], lambda e: e.dma_start(
                out=[WT0, WT1][sl][:, 0:2048], in_=wg_st[cc]))
        GP(_)

        def _(te, cc=cc, sl=sl, pn=pn, wn=wn):
            OP("TE", te, [wn, "A"], [pn], lambda e: mmsteps(
                e, PSUM[pn], 0, 1024,
                [([WT0, WT1][sl][:, dc * 128:(dc + 1) * 128],
                  A[:, dc * 1024:(dc + 1) * 1024]) for dc in range(NDC)]))
        TE(_)

        def _(se, cc=cc, sl=sl, pn=pn):
            OP("SE", se, [pn], CB_FAM, lambda e: e.activation(
                Cb[:, cc * 1024:(cc + 1) * 1024], PSUM[pn][:, 0:1024],
                AF.Silu))
        SE(_)

    def _(sy):
        OP("SY", sy, [], ["AM", "A", "AKV", "AE0", "AE1", "GS", "RB_A0", "RB_A1"],
           lambda e: e.dma_start(
               out=AMm, in_=am_t.ap().rearrange("j p c -> p j c")))
    SY(_)

    def emit_attention(layer):
        for h in range(16):
            if layer == 0:
                def _(sy, h=h):
                    def em(e):
                        outl = []
                        for r in range(2):
                            outl.append(e.dma_start(
                                out=AK[:, r * 1024:(r + 1) * 1024],
                                in_=ag0_out[r * 4096 + h * 128:
                                            r * 4096 + (h + 1) * 128, :]))
                        dvq = h // 8
                        for g in range(16):
                            r, tb = OWNER[g], LOCIDX[g]
                            r0 = r * 4096 + 2048 + (tb * 2 + dvq) * 128
                            c0 = (h % 8) * 128
                            outl.append(e.dma_start(
                                out=AVt[:, g * 128:(g + 1) * 128],
                                in_=ag0_out[r0:r0 + 128, c0:c0 + 128]))
                        return outl
                    OP("SY", sy, ["ag0_out"], ["AKV"], em)
                SY(_)

            def emit_S(j):
                sl = j % 2
                pn = "PA" if sl == 0 else "PB"
                W = WJ[j]

                def _(te, h=h, j=j, sl=sl, pn=pn, W=W, layer=layer):
                    if layer == 0:
                        lh = AK[:, OWNER[j] * 1024 + LOCIDX[j] * 128:
                                OWNER[j] * 1024 + (LOCIDX[j] + 1) * 128]
                        kres = "AKV"
                    else:
                        gr = h // 4
                        lh = Cb[:, gr * 2048 + OWNER[j] * 1024
                                + LOCIDX[j] * 128:
                                gr * 2048 + OWNER[j] * 1024
                                + (LOCIDX[j] + 1) * 128]
                        kres = "Cb"
                    rh = Bq[:, h * 1024 + SOFF[j]:(h + 1) * 1024]
                    OP("TE", te, [kres, "Bq"], [pn], lambda e: mmsteps(
                        e, PSUM[pn], 0, W, [(lh, rh)]))
                TE(_)

            def emit_evict_av(j):
                sl = j % 2
                pn = "PA" if sl == 0 else "PB"
                aen = ["AE0", "AE1"][sl]
                W = WJ[j]
                if layer == 0:
                    def _(ve, h=h, j=j, sl=sl, pn=pn, aen=aen, W=W):
                        m = min(256, W)

                        def em(e):
                            i1 = e.tensor_tensor(
                                AE[sl][:, 0:m], PSUM[pn][:, 0:m],
                                AMm[:, j * 256:j * 256 + m], OP_.mult)
                            if W > 256:
                                i1 = e.tensor_copy(AE[sl][:, 256:W],
                                                   PSUM[pn][:, 256:W])
                            return i1
                        OP("VE", ve, [pn, "AM"], [aen], em)
                    VE(_)
                else:
                    def _(se, h=h, j=j, sl=sl, pn=pn, aen=aen, W=W):
                        OP("SE", se, [pn], [aen], lambda e: e.activation(
                            AE[sl][:, 0:W], PSUM[pn][:, 0:W], AF.Exp,
                            scale=INV_SQRT_D))
                    SE(_)

                    def _(ve, h=h, j=j, sl=sl, aen=aen, W=W):
                        m = min(256, W)
                        OP("VE", ve, [aen, "AM"], [aen],
                           lambda e: e.tensor_tensor(
                               AE[sl][:, 0:m], AE[sl][:, 0:m],
                               AMm[:, j * 256:j * 256 + m], OP_.mult))
                    VE(_)

                def _(te, h=h, j=j, sl=sl, aen=aen, W=W, layer=layer):
                    if layer == 0:
                        lh = AVt[:, j * 128:(j + 1) * 128]
                        vres = "AKV"
                    else:
                        lh = Cb[:, 8192 + j * 512 + (h // 4) * 128:
                                8192 + j * 512 + (h // 4 + 1) * 128]
                        vres = "Cb"

                    def em(e):
                        last = None
                        for (a, b_) in col_chunks(SOFF[j], 1024):
                            lastj = 7 if a < 512 else 15
                            last = e.matmul(
                                PC[:, a:b_], lh,
                                AE[sl][:, a - SOFF[j]:b_ - SOFF[j]],
                                start=(j == 0), stop=(j == lastj))
                        if layer == 1:
                            for (a, b_) in col_chunks(SOFF[j], 1024):
                                lastj = 7 if a < 512 else 15
                                last = e.matmul(
                                    PD[0:16, a:b_], ECOLt(h),
                                    AE[sl][:, a - SOFF[j]:b_ - SOFF[j]],
                                    start=(h == 0 and j == 0),
                                    stop=(h == 15 and j == lastj))
                        return last
                    wr = ["PC"] if layer == 0 else ["PC", "PD"]
                    OP("TE", te, [aen, vres, "CONS"], wr, em)
                TE(_)

            for j in range(16):
                emit_S(j)
                if j >= 1:
                    emit_evict_av(j - 1)
            emit_evict_av(15)

            def _(ve, h=h):
                OP("VE", ve, ["PC"], ["Bq"], lambda e: e.tensor_copy(
                    Bq[:, h * 1024:(h + 1) * 1024], PC[:, 0:1024]))
            VE(_)
    emit_attention(0)

    if DEBUG_DUMPS:
        def _(sy):
            OP("SY", sy, ["Bq"], ["dbg_y0"], lambda e: e.dma_start(
                out=dbg["dbg_y0"].ap(), in_=Bq[:, :]))
        SY(_)

    for h in range(16):
        sq = [SM0, SM1][h % 2]
        sqn = ["SM0", "SM1"][h % 2]

        def _(se, h=h, sq=sq, sqn=sqn):
            OP("SE", se, ["Bq"], [sqn], lambda e: e.activation(
                sq[:, 0:1024], Bq[:, h * 1024:(h + 1) * 1024], AF.Square))
        SE(_)

        def _(te, h=h, sq=sq, sqn=sqn):
            OP("TE", te, ["Bq", "CONS"], ["PA"], lambda e: mmsteps(
                e, PA, 0, 1024,
                [(ECOLt(h), Bq[:, h * 1024:(h + 1) * 1024])],
                start=(h == 0), stop=(h == 15), rows=16))
            OP("TE", te, [sqn, "CONS"], ["PB"], lambda e: mmsteps(
                e, PB, 0, 1024, [(ECOLt(h), sq[:, 0:1024])],
                start=(h == 0), stop=(h == 15), rows=16))
        TE(_)

    def _(ve):
        OP("VE", ve, ["PA"], ["GS"], lambda e: e.tensor_copy(
            GSA, PA[0:16, 0:1024]))
        OP("VE", ve, ["PB", "GS"], ["GS2"], lambda e: e.tensor_copy(
            GSB, PB[0:16, 0:1024]))
    VE(_)

    def _(gp):
        OP("GP", gp, ["GS"], ["STTa"], lambda e: [
            e.dma_start(
                out=STT[:, hh * 8:(hh + 1) * 8],
                in_=GSA[hh:hh + 1, :].rearrange("o (p i) -> o p i",
                                                p=128, i=8))
            for hh in range(16)])
        OP("GP", gp, ["GS2"], ["STTb"], lambda e: [
            e.dma_start(
                out=STT[:, 128 + hh * 8:128 + (hh + 1) * 8],
                in_=GSB[hh:hh + 1, :].rearrange("o (p i) -> o p i",
                                                p=128, i=8))
            for hh in range(16)])
    GP(_)

    def _(ve):
        OP("VE", ve, ["STTa"], ["STTa"], lambda e: e.tensor_scalar(
            STT[:, 0:128], STT[:, 0:128], 1.0 / 128.0, None, OP_.mult))
        OP("VE", ve, ["STTb"], ["STTb"], lambda e: e.tensor_scalar(
            STT[:, 128:256], STT[:, 128:256], 1.0 / 128.0, None, OP_.mult))
        for qq in range(4):
            OP("VE", ve, ["STTa"], ["AMX"], lambda e, qq=qq: e.tensor_tensor(
                AMX[:, 0:32], STT[:, qq * 32:(qq + 1) * 32],
                STT[:, qq * 32:(qq + 1) * 32], OP_.mult))
            OP("VE", ve, ["STTb", "AMX"], ["STTb"],
               lambda e, qq=qq: e.tensor_tensor(
                   STT[:, 128 + qq * 32:128 + (qq + 1) * 32],
                   STT[:, 128 + qq * 32:128 + (qq + 1) * 32],
                   AMX[:, 0:32], OP_.subtract))
        OP("VE", ve, ["STTb"], ["STTb"], lambda e: e.tensor_scalar(
            STT[:, 128:256], STT[:, 128:256], 1e-5, None, OP_.add))
    VE(_)

    def _(se):
        OP("SE", se, ["STTb"], ["STTb"], lambda e: e.activation(
            STT[:, 128:256], STT[:, 128:256], AF.Sqrt))
    SE(_)

    def _(ve):
        OP("VE", ve, ["STTb"], ["STTb"], lambda e: e.reciprocal(
            STT[:, 128:256], STT[:, 128:256]))
        OP("VE", ve, ["STTa", "STTb"], ["STTa"], lambda e: e.tensor_tensor(
            STT[:, 0:128], STT[:, 0:128], STT[:, 128:256], OP_.mult))
        OP("VE", ve, ["STTa"], ["STTa"], lambda e: e.tensor_scalar(
            STT[:, 0:128], STT[:, 0:128], -1.0, None, OP_.mult))
    VE(_)

    for h in range(16):
        rbs = RB_As[h % 2]
        rbn = ["RB_A0", "RB_A1"][h % 2]

        def _(gp, h=h, rbs=rbs, rbn=rbn):
            OP("GP", gp, ["STTa", "STTb"], [rbn], lambda e: [
                e.dma_start(out=rbs[0:1, 0:1024],
                            in_=STT[:, 128 + h * 8:128 + (h + 1) * 8]),
                e.dma_start(out=rbs[0:1, 1024:2048],
                            in_=STT[:, h * 8:(h + 1) * 8])])
        GP(_)

        def _(te, h=h, rbs=rbs, rbn=rbn):
            OP("TE", te, [rbn, "CONS"], ["PA"], lambda e: mmsteps(
                e, PA, 0, 1024, [(ONER, rbs[0:1, 0:1024])]))
            OP("TE", te, [rbn, "CONS"], ["PB"], lambda e: mmsteps(
                e, PB, 0, 1024, [(ONER, rbs[0:1, 1024:2048])]))
        TE(_)

        def _(ve, h=h):
            OP("VE", ve, ["Bq", "PA"], ["SM0"], lambda e: e.tensor_tensor(
                SM0[:, 0:1024], Bq[:, h * 1024:(h + 1) * 1024],
                PA[:, 0:1024], OP_.mult))
            OP("VE", ve, ["SM0", "PB"], ["SM1"], lambda e: e.tensor_tensor(
                SM1[:, 0:1024], SM0[:, 0:1024], PB[:, 0:1024], OP_.add))
            OP("VE", ve, ["SM1", "SVC"], ["SM0"], lambda e: e.tensor_scalar(
                SM0[:, 0:1024], SM1[:, 0:1024],
                SVC[:, SV_GNW + h:SV_GNW + h + 1],
                SVC[:, SV_GNB + h:SV_GNB + h + 1], OP_.mult, OP_.add))
            OP("VE", ve, ["SM0", "Cb"], ["Cb"], lambda e: e.tensor_tensor(
                Cb[:, h * 1024:(h + 1) * 1024],
                Cb[:, h * 1024:(h + 1) * 1024], SM0[:, 0:1024], OP_.mult))
        VE(_)


    if DEBUG_DUMPS:
        def _(sy):
            OP("SY", sy, ["Cb"], ["dbg_gated"], lambda e: e.dma_start(
                out=dbg["dbg_gated"].ap(), in_=Cb[:, :]))
        SY(_)
    for occ in range(NDC):
        sl = occ % 2
        pn = "PA" if sl == 0 else "PB"
        wn = ["WT0", "WT1"][sl]
        sqn = ["SM0", "SM1"][sl]

        def _(gp, occ=occ, sl=sl, wn=wn):
            OP("GP", gp, [], [wn], lambda e: e.dma_start(
                out=[WT0, WT1][sl][:, 0:2048], in_=wp_st[occ]))
        GP(_)

        def _(te, occ=occ, sl=sl, pn=pn, wn=wn):
            OP("TE", te, [wn, "Cb"], [pn], lambda e: mmsteps(
                e, PSUM[pn], 0, 1024,
                [([WT0, WT1][sl][:, cc * 128:(cc + 1) * 128],
                  Cb[:, cc * 1024:(cc + 1) * 1024]) for cc in range(NDC)]))
        TE(_)

        def _(ve, occ=occ, sl=sl, pn=pn, sqn=sqn):
            OP("VE", ve, [pn, "Dg"], [sqn], lambda e: e.tensor_tensor(
                [SM0, SM1][sl][:, 0:1024], PSUM[pn][:, 0:1024],
                Dg[:, occ * 1024:(occ + 1) * 1024], OP_.mult))
            OP("VE", ve, [sqn, "X"], ["X"], lambda e: e.tensor_tensor(
                X[:, occ * 1024:(occ + 1) * 1024],
                X[:, occ * 1024:(occ + 1) * 1024],
                [SM0, SM1][sl][:, 0:1024], OP_.add))
        VE(_)


    if DEBUG_DUMPS:
        def _(sy):
            OP("SY", sy, ["X"], ["dbg_x1"], lambda e: e.dma_start(
                out=dbg["dbg_x1"].ap(), in_=X[:, :]))
        SY(_)
    emit_layernorm(RB_D, "RB_D")

    def _(ve):
        OP("VE", ve, ["A"], ["AMX"], lambda e: [
            e.reduce_max(AMX[:, dc:dc + 1],
                         A[:, dc * 1024:(dc + 1) * 1024], axis=AX,
                         apply_absolute_value=True) for dc in range(NDC)][-1:])
        OP("VE", ve, ["AMX"], ["AMX2"], lambda e: e.reduce_max(
            AMX[:, 16:17], AMX[:, 0:16], axis=AX, apply_absolute_value=True))
    VE(_)

    def _(sy):
        OP("SY", sy, ["AMX2"], ["STTb"], lambda e: e.dma_start(
            out=STT[0:1, 128:256], in_=AMX[:, 16:17]))
    SY(_)

    def _(ve):
        OP("VE", ve, ["STTb", "XSR"], ["XSR"], lambda e: e.reduce_max(
            XSR[0:1, 0:1], STT[0:1, 128:256], axis=AX,
            apply_absolute_value=True))
    VE(_)

    def _(sy):
        OP("SY", sy, ["XSR"], ["ar_in"], lambda e: e.dma_start(
            out=ar_in[0:1, :], in_=XSR[0:1, 0:16]))
    SY(_)

    def _(gp):
        t = lastw.get("ar_in")
        if t and seen["GP"].get(t[0], 0) < t[1]:
            gp.wait_ge(sems[t[0]][0], t[1])
            seen["GP"][t[0]] = t[1]
        ins_ = gp.collective_compute(
            "AllReduce", OP_.max,
            replica_groups=[[0, 1, 2, 3, 4, 5, 6, 7]],
            ins=[ar_in.ap().opt()], outs=[ar_out.ap().opt()])
        sems["cAR"][1] += 1
        ins_.then_inc(sems["cAR"][0], 1)
        note("ar_out", ("cAR", sems["cAR"][1]))
    GP(_)

    def _(sy):
        OP("SY", sy, ["ar_out"], ["XSR"], lambda e: e.dma_start(
            out=XSR[0:1, 0:16], in_=ar_out[0:1, :]))
    SY(_)

    def _(ve):
        OP("VE", ve, ["XSR"], ["XSR"], lambda e: e.reciprocal(
            XSR[0:1, 2:3], XSR[0:1, 0:1]))
    VE(_)

    def _(sy):
        OP("SY", sy, ["XSR"], ["scl_dram"], lambda e: e.dma_start(
            out=scl_dram[0:1, 0:1], in_=XSR[0:1, 2:3]))
        srcap0 = bass.AP(scl_dram, 0, [[0, 128], [1, 1]])
        OP("SY", sy, ["scl_dram"], ["SCL"], lambda e: e.dma_start(
            out=SCL[:, 0:1], in_=srcap0))
    SY(_)

    def _(ve):
        for dc in range(NDC):
            for q4 in range(4):
                c0 = dc * 1024 + q4 * 256
                OP("VE", ve, ["A", "SCL"], ["SM8"],
                   lambda e, c0=c0: e.tensor_scalar(
                       SM8[:, 0:256], A[:, c0:c0 + 256], SCL[:, 0:1], None,
                       OP_.mult))
                OP("VE", ve, ["SM8"], CB_FAM if (dc == 0 and q4 == 0)
                   else ["Cb"],
                   lambda e, c0=c0: e.tensor_copy(Cb[:, c0:c0 + 256],
                                                  SM8[:, 0:256]))
    VE(_)

    for cc in range(NDC):
        sl = cc % 2
        pn = "PA" if sl == 0 else "PB"
        wn = ["WT0", "WT1"][sl]

        def _(gp, cc=cc, sl=sl, wn=wn):
            OP("GP", gp, [], [wn], lambda e: e.dma_start(
                out=[WT0, WT1][sl][:, 0:2048], in_=l1q_st[cc]))
        GP(_)

        def _(te, cc=cc, sl=sl, pn=pn, wn=wn):
            OP("TE", te, [wn, "A"], [pn], lambda e: mmsteps(
                e, PSUM[pn], 0, 1024,
                [([WT0, WT1][sl][:, dc * 128:(dc + 1) * 128],
                  A[:, dc * 1024:(dc + 1) * 1024]) for dc in range(NDC)]))
        TE(_)

        def _(se, cc=cc, sl=sl, pn=pn):
            OP("SE", se, [pn, "SVC"], ["Bq"], lambda e: e.activation(
                Bq[:, cc * 1024:(cc + 1) * 1024], PSUM[pn][:, 0:1024],
                AF.Identity, bias=SVC[:, SV_BQ + cc:SV_BQ + cc + 1]))
        SE(_)

    for kc in range(4):
        sl = kc % 2
        pn = "PA" if sl == 0 else "PB"
        wn = ["WT0", "WT1"][sl]
        sqn = ["SM0", "SM1"][sl]

        def _(gp, kc=kc, sl=sl, wn=wn):
            OP("GP", gp, [], [wn], lambda e: e.dma_start(
                out=[WT0, WT1][sl][:, 0:2048], in_=l1k_st[kc]))
        GP(_)

        def _(te, kc=kc, sl=sl, pn=pn, wn=wn):
            OP("TE", te, [wn, "A"], [pn], lambda e: mmsteps(
                e, PSUM[pn], 0, 1024,
                [([WT0, WT1][sl][:, dc * 128:(dc + 1) * 128],
                  A[:, dc * 1024:(dc + 1) * 1024]) for dc in range(NDC)]))
        TE(_)

        def _(se, kc=kc, sl=sl, pn=pn, sqn=sqn):
            OP("SE", se, [pn, "SVC"], [sqn], lambda e: e.activation(
                [SM0, SM1][sl][:, 0:1024], PSUM[pn][:, 0:1024],
                AF.Identity, bias=SVC[:, SV_BK + kc:SV_BK + kc + 1]))
        SE(_)

        def _(sy, kc=kc, sl=sl, sqn=sqn):
            OP("SY", sy, [sqn], ["ag1_in"], lambda e: e.dma_start(
                out=ag1_in[kc * 128:(kc + 1) * 128, :],
                in_=[SM0, SM1][sl][:, 0:1024]))
        SY(_)

    def _(gp):
        OP("GP", gp, [],
           ["VW", "RB_D", "TBL0", "TBL1", "T1S0", "T1S1"], lambda e: [
               e.dma_start(out=VW[:, dc * 512:(dc + 1) * 512],
                           in_=l1v_mov[dc]) for dc in range(NDC)])
        OP("GP", gp, [], ["RB_D2"], lambda e: e.dma_start(
            out=RB_D2[0:1, 0:512], in_=bv_row.ap()))
    GP(_)

    def _(te):
        OP("TE", te, ["RB_D2", "CONS"], ["PD"], lambda e: mmsteps(
            e, PD, 0, 512, [(ONER, RB_D2[0:1, 0:512])]))
    TE(_)

    def _(ve):
        OP("VE", ve, ["PD"], ["BVC"], lambda e: e.tensor_copy(
            BVC[:, 0:512], PD[:, 0:512]))
    VE(_)

    for tb in range(8):
        sl = tb % 2
        pn = "PA" if sl == 0 else "PB"
        sqn = ["SM0", "SM1"][sl]

        def _(te, tb=tb, sl=sl, pn=pn):
            OP("TE", te, ["VW", "A"], [pn], lambda e: mmsteps(
                e, PSUM[pn], 0, 512,
                [(A[:, dc * 1024 + tb * 128:dc * 1024 + (tb + 1) * 128],
                  VW[:, dc * 512:(dc + 1) * 512]) for dc in range(NDC)]))
        TE(_)

        def _(ve, tb=tb, sl=sl, pn=pn, sqn=sqn):
            OP("VE", ve, [pn, "BVC"], [sqn], lambda e: e.tensor_tensor(
                [SM0, SM1][sl][:, 0:512], PSUM[pn][:, 0:512],
                BVC[:, 0:512], OP_.add))
        VE(_)

        def _(sy, tb=tb, sl=sl, sqn=sqn):
            r0 = 512 + (tb // 2) * 128
            c0 = (tb % 2) * 512
            OP("SY", sy, [sqn], ["ag1_in"], lambda e: e.dma_start(
                out=ag1_in[r0:r0 + 128, c0:c0 + 512],
                in_=[SM0, SM1][sl][:, 0:512]))
        SY(_)

    def _(gp):
        t = lastw.get("ag1_in")
        if t and seen["GP"].get(t[0], 0) < t[1]:
            gp.wait_ge(sems[t[0]][0], t[1])
            seen["GP"][t[0]] = t[1]
        ins_ = gp.collective_compute(
            "AllGather", OP_.bypass,
            replica_groups=[[0, 1], [2, 3], [4, 5], [6, 7]],
            ins=[ag1_in.ap().opt()], outs=[ag1_out.ap().opt()])
        sems["cAG1"][1] += 1
        ins_.then_inc(sems["cAG1"][0], 1)
        note("ag1_out", ("cAG1", sems["cAG1"][1]))
    GP(_)


    if DEBUG_DUMPS:
        def _(sy):
            OP("SY", sy, ["Bq"], ["dbg_q1"], lambda e: e.dma_start(
                out=dbg["dbg_q1"].ap(), in_=Bq[:, :]))
        SY(_)
    for ncc in range(NDC):
        sl = ncc % 2
        wn = ["WT0", "WT1"][sl]

        def _(sy, ncc=ncc, sl=sl, wn=wn):
            OP("SY", sy, [], [wn], lambda e: e.dma_start(
                out=[WT0, WT1][sl][:, 0:2048], in_=l1w_st[ncc]))
        SY(_)

        def _(ve, ncc=ncc, sl=sl, wn=wn):
            OP("VE", ve, [wn, "AMX2"], ["AMX"], lambda e: e.reduce_max(
                AMX[:, ncc:ncc + 1], [WT0, WT1][sl][:, 0:2048], axis=AX,
                apply_absolute_value=True))
        VE(_)

    def _(ve):
        OP("VE", ve, ["AMX"], ["AMX2"], lambda e: e.reduce_max(
            AMX[:, 17:18], AMX[:, 0:16], axis=AX, apply_absolute_value=True))
    VE(_)

    def _(sy):
        OP("SY", sy, ["AMX2"], ["STTb"], lambda e: e.dma_start(
            out=STT[0:1, 128:256], in_=AMX[:, 17:18]))
    SY(_)

    def _(ve):
        OP("VE", ve, ["STTb", "XSR"], ["XSR"], lambda e: e.reduce_max(
            XSR[0:1, 1:2], STT[0:1, 128:256], axis=AX,
            apply_absolute_value=True))
        OP("VE", ve, ["XSR"], ["XSR"], lambda e: e.reciprocal(
            XSR[0:1, 3:4], XSR[0:1, 1:2]))
        OP("VE", ve, ["XSR"], ["XSR"], lambda e: e.tensor_tensor(
            XSR[0:1, 4:5], XSR[0:1, 0:1], XSR[0:1, 1:2], OP_.mult))
    VE(_)

    def _(sy):
        OP("SY", sy, ["XSR"], ["scl_dram"], lambda e: e.dma_start(
            out=scl_dram[0:1, 1:3], in_=XSR[0:1, 3:5]))
        srcap = bass.AP(scl_dram, 1, [[0, 128], [1, 2]])
        OP("SY", sy, ["scl_dram"], ["SCL"], lambda e: e.dma_start(
            out=SCL[:, 1:3], in_=srcap))
    SY(_)

    for ncc in range(NDC):
        sl = ncc % 2
        pn = "PA" if sl == 0 else "PB"
        wn = ["WT0", "WT1"][sl]
        sqn = ["SM0", "SM1"][sl]

        def _(sy, ncc=ncc, sl=sl, wn=wn):
            OP("SY", sy, [], [wn], lambda e: e.dma_start(
                out=[WT0, WT1][sl][:, 0:2048], in_=l1w_st[ncc]))
        SY(_)

        def _(ve, ncc=ncc, sl=sl, wn=wn):
            for q8 in range(8):
                c0 = q8 * 256
                OP("VE", ve, [wn, "SCL"], ["SM8"],
                   lambda e, c0=c0: e.tensor_scalar(
                       SM8[:, 0:256], [WT0, WT1][sl][:, c0:c0 + 256],
                       SCL[:, 1:2], None, OP_.mult))
                OP("VE", ve, ["SM8"], [wn],
                   lambda e, c0=c0: e.tensor_copy(
                       [WT0, WT1][sl][:, c0:c0 + 256], SM8[:, 0:256]))
        VE(_)

        def _(te, ncc=ncc, sl=sl, pn=pn, wn=wn):
            OP("TE", te, [wn, "Cb"], [pn], lambda e: mmsteps(
                e, PSUM[pn], 0, 1024,
                [([WT0, WT1][sl][:, kcc * 128:(kcc + 1) * 128],
                  Cb[:, kcc * 1024:(kcc + 1) * 1024])
                 for kcc in range(NDC)]))
        TE(_)

        def _(ve, ncc=ncc, sl=sl, pn=pn, sqn=sqn):
            OP("VE", ve, [pn, "SCL"], [sqn], lambda e: e.tensor_scalar(
                [SM0, SM1][sl][:, 0:1024], PSUM[pn][:, 0:1024],
                SCL[:, 2:3], None, OP_.mult))
        VE(_)

        def _(se, ncc=ncc, sl=sl, sqn=sqn):
            OP("SE", se, [sqn, "SVCb"],
               ["Dg"] + DG_FAM[1:] if ncc == 0 else ["Dg"],
               lambda e: e.activation(
                   Dg[:, ncc * 1024:(ncc + 1) * 1024],
                   [SM0, SM1][sl][:, 0:1024], AF.Silu,
                   bias=SVC[:, SV_L1B + ncc:SV_L1B + ncc + 1]))
        SE(_)


    if DEBUG_DUMPS:
        def _(sy):
            OP("SY", sy, ["Dg"], ["dbg_g1"], lambda e: e.dma_start(
                out=dbg["dbg_g1"].ap(), in_=Dg[:, :]))
        SY(_)
    def _(sy):
        def em(e):
            outl = []
            for r in range(2):
                for kc in range(4):
                    outl.append(e.dma_start(
                        out=Cb[:, kc * 2048 + r * 1024:
                               kc * 2048 + (r + 1) * 1024],
                        in_=ag1_out[r * 1024 + kc * 128:
                                    r * 1024 + (kc + 1) * 128, :]))
            for g in range(16):
                r, tb = OWNER[g], LOCIDX[g]
                r0 = r * 1024 + 512 + (tb // 2) * 128
                c0 = (tb % 2) * 512
                outl.append(e.dma_start(
                    out=Cb[:, 8192 + g * 512:8192 + (g + 1) * 512],
                    in_=ag1_out[r0:r0 + 128, c0:c0 + 512]))
            return outl
        OP("SY", sy, ["ag1_out"], CB_FAM, em)
        OP("SY", sy, [], ["AM", "A", "AKV", "AE0", "AE1", "GS", "RB_A0", "RB_A1"],
           lambda e: e.dma_start(
               out=AMm, in_=am_t.ap().rearrange("j p c -> p j c")))
    SY(_)

    emit_attention(1)

    if DEBUG_DUMPS:
        def _(sy):
            OP("SY", sy, ["Bq"], ["dbg_y1"], lambda e: e.dma_start(
                out=dbg["dbg_y1"].ap(), in_=Bq[:, :]))
        SY(_)

    def _(ve):
        OP("VE", ve, ["PD"], ["GS"], lambda e: e.tensor_copy(
            GSA, PD[0:16, 0:1024]))

        def em(e):
            with nc.allow_low_precision(
                    reason="bf16 softmax denominators, in tol"):
                return e.reciprocal(GSB, GSA)
        OP("VE", ve, ["GS"], ["GS2"], em)
    VE(_)

    for h in range(16):
        rbs = RB_As[h % 2]
        rbn = ["RB_A0", "RB_A1"][h % 2]

        def _(sy, h=h, rbs=rbs, rbn=rbn):
            OP("SY", sy, ["GS2"], [rbn], lambda e: e.dma_start(
                out=rbs[0:1, 0:1024], in_=GSB[h:h + 1, 0:1024]))
        SY(_)

        def _(te, h=h, rbs=rbs, rbn=rbn):
            OP("TE", te, [rbn, "CONS"], ["PA"], lambda e: mmsteps(
                e, PA, 0, 1024, [(ONER, rbs[0:1, 0:1024])]))
        TE(_)

        def _(ve, h=h):
            OP("VE", ve, ["Bq", "PA"], ["SM0"], lambda e: e.tensor_tensor(
                SM0[:, 0:1024], Bq[:, h * 1024:(h + 1) * 1024],
                PA[:, 0:1024], OP_.mult))
            OP("VE", ve, ["SM0", "Dg"], ["SM1"], lambda e: e.tensor_tensor(
                SM1[:, 0:1024], SM0[:, 0:1024],
                Dg[:, h * 1024:(h + 1) * 1024], OP_.mult))
            OP("VE", ve, ["SM1", "X"], ["X"], lambda e: e.tensor_tensor(
                X[:, h * 1024:(h + 1) * 1024],
                X[:, h * 1024:(h + 1) * 1024], SM1[:, 0:1024], OP_.add))
        VE(_)

    def _(sy):
        OP("SY", sy, ["X"], ["xout"], lambda e: [
            e.dma_start(out=xout[dc], in_=X[:, dc * 1024:(dc + 1) * 1024])
            for dc in range(NDC)])
        sy.wait_ge(sems["qSY_xout"][0], sems["qSY_xout"][1])
    SY(_)

    blk.__exit__(None, None, None)
    for t in reversed(ctx[:-1]):
        try:
            t.__exit__(None, None, None)
        except Exception:
            pass
    return nc


def _prep_weights(ins):
    f = np.float32
    out = {}
    wqkv = np.asarray(ins["L0_attn_w_qkv"], f)
    wq = wqkv[:, 0:2048].reshape(16, 128, 16, 128).transpose(2, 1, 0, 3)
    wk = wqkv[:, 2048:4096].reshape(16, 128, 16, 128).transpose(2, 1, 0, 3)
    out["wqkv_st"] = np.ascontiguousarray(
        np.concatenate([wq, wk], axis=0).reshape(32, 128, 2048))
    out["wv_mov"] = np.ascontiguousarray(
        wqkv[:, 4096:6144].reshape(16, 128, 2048))
    out["wg_st"] = np.ascontiguousarray(
        np.asarray(ins["L0_attn_w_g"], f).reshape(16, 128, 16, 128)
        .transpose(2, 1, 0, 3).reshape(16, 128, 2048))
    out["wp_st"] = np.ascontiguousarray(
        np.asarray(ins["L0_attn_w_proj"], f).reshape(16, 128, 16, 128)
        .transpose(2, 1, 0, 3).reshape(16, 128, 2048))
    out["rw_st"] = np.ascontiguousarray(
        np.asarray(ins["L0_ffn_router_w"], f).T.reshape(16, 128, 8))
    wgu = np.stack([np.asarray(ins["L0_ffn_wg"], f),
                    np.asarray(ins["L0_ffn_wu"], f)], axis=0)
    out["wgu_st"] = np.ascontiguousarray(
        wgu.reshape(2, 8, 16, 128, 8, 128).transpose(0, 1, 4, 3, 2, 5)
        .reshape(2, 8, 8, 128, 2048))
    out["wd_st"] = np.ascontiguousarray(
        np.asarray(ins["L0_ffn_wd"], f).reshape(8, 8, 128, 16, 128)
        .transpose(0, 3, 2, 1, 4).reshape(8, 16, 128, 1024))
    sgu = np.stack([np.asarray(ins["L0_ffn_sg_w"], f),
                    np.asarray(ins["L0_ffn_su_w"], f)], axis=0)
    out["sgu_st"] = np.ascontiguousarray(
        sgu.reshape(2, 16, 128, 8, 128).transpose(0, 3, 2, 1, 4)
        .reshape(2, 8, 128, 2048))
    out["sd_st"] = np.ascontiguousarray(
        np.asarray(ins["L0_ffn_sd_w"], f).reshape(8, 128, 16, 128)
        .transpose(2, 1, 0, 3).reshape(16, 128, 1024))
    out["l1q_st"] = np.ascontiguousarray(
        np.asarray(ins["L1_attn_Wq"], f).reshape(16, 16, 128, 128)
        .transpose(0, 2, 1, 3).reshape(16, 128, 2048))
    out["l1k_st"] = np.ascontiguousarray(
        np.asarray(ins["L1_attn_Wk"], f).reshape(4, 16, 128, 128)
        .transpose(0, 2, 1, 3).reshape(4, 128, 2048))
    out["l1v_mov"] = np.ascontiguousarray(
        np.asarray(ins["L1_attn_Wv"], f).reshape(4, 16, 128, 128)
        .transpose(1, 2, 0, 3).reshape(16, 128, 512))
    out["bv_row"] = np.ascontiguousarray(
        np.asarray(ins["L1_attn_bv"], f).reshape(1, 512))
    w1 = np.asarray(ins["L1_ffn_weight"])
    out["l1w_st"] = np.ascontiguousarray(
        w1.reshape(16, 128, 16, 128).transpose(0, 3, 2, 1)
        .reshape(16, 128, 2048))
    out["l1b_bf"] = np.ascontiguousarray(
        np.asarray(ins["L1_ffn_bias"]).reshape(16, 128).T)

    sv = np.zeros((128, 128), f)
    sv[:, SV_GNW:SV_GNW + 16] = np.asarray(
        ins["L0_attn_gn_weight"], f).reshape(16, 128).T
    sv[:, SV_GNB:SV_GNB + 16] = np.asarray(
        ins["L0_attn_gn_bias"], f).reshape(16, 128).T
    sv[:, SV_SGB:SV_SGB + 8] = np.asarray(
        ins["L0_ffn_sg_b"], f).reshape(8, 128).T
    sv[:, SV_SUB:SV_SUB + 8] = np.asarray(
        ins["L0_ffn_su_b"], f).reshape(8, 128).T
    sv[:, SV_SDB:SV_SDB + 16] = np.asarray(
        ins["L0_ffn_sd_b"], f).reshape(16, 128).T
    sv[:, SV_BQ:SV_BQ + 16] = np.asarray(
        ins["L1_attn_bq"], f).reshape(16, 128).T
    sv[:, SV_BK:SV_BK + 4] = np.asarray(
        ins["L1_attn_bk"], f).reshape(4, 128).T
    out["smallw"] = sv
    return out


def _xpos_tables(parity):
    f = np.float64
    Cd = 2048
    pos = np.array([ZIG[parity][t // 128] * 128 + (t % 128)
                    for t in range(TPC)], f)
    i2 = np.arange(0, Cd, 2, dtype=f)
    dim_scale = (i2 + 0.4 * Cd) / (1.4 * Cd)
    scale = dim_scale[None, :] ** (pos[:, None] / 512.0)
    inv_freq = 1.0 / 10000.0 ** (i2 / Cd)
    sinu = pos[:, None] * inv_freq[None, :]
    sin = np.repeat(np.sin(sinu) * scale, 2, axis=1)
    cos = np.repeat(np.cos(sinu) * scale, 2, axis=1)
    gam = 1.0 - 2.0 ** (-5.0 - np.arange(16, dtype=f))
    gq = gam[np.arange(Cd) // 128][:, None] ** pos[None, :]
    gk = gam[np.arange(Cd) // 128][:, None] ** (-pos[None, :])
    cq = cos.T * gq * (128.0 ** -0.5)
    sq = sin.T * gq * (128.0 ** -0.5)
    ck = cos.T * gk
    sk = sin.T * gk

    def tile(m):
        return m.reshape(16, 128, TPC).astype(ml_dtypes.bfloat16)
    tq = np.stack([tile(cq), tile(sq)], axis=1)
    tk = np.stack([tile(ck), tile(sk)], axis=1)
    return np.ascontiguousarray(tq), np.ascontiguousarray(tk)


def _mask_tables(parity):
    am = np.zeros((16, 128, 256), np.float32)
    zz = ZIG[parity]
    tri = (np.arange(128)[:, None] <= np.arange(128)[None, :]).astype(
        np.float32)
    for j in range(16):
        L = LPROF[j]
        for p01 in range(min(2, L)):
            gb = zz[8 - L + p01]
            if gb < j:
                blkm = np.zeros((128, 128), np.float32)
            elif gb == j:
                blkm = tri
            else:
                blkm = np.ones((128, 128), np.float32)
            am[j, :, p01 * 128:(p01 + 1) * 128] = blkm
    return np.ascontiguousarray(am.astype(ml_dtypes.bfloat16))


def _consts_arr():
    c = np.zeros((128, 648), np.float32)
    for i in range(64):
        c[2 * i + 1, 2 * i] = -1.0
        c[2 * i, 2 * i + 1] = 1.0
    for h in range(16):
        c[:, CN_ECOL + h * 16 + h] = 1.0
    c[:, CN_ONE] = 1.0
    c[0, CN_ONER:CN_ONER + 128] = 1.0
    return np.ascontiguousarray(c.astype(ml_dtypes.bfloat16))


_NC_CACHE = {}


def kernel(**ins):
    if "nc" not in _NC_CACHE:
        _NC_CACHE["nc"] = build_program()
    nc = _NC_CACHE["nc"]

    w = _prep_weights(ins)
    consts = _consts_arr()
    identf = np.eye(128, dtype=np.float32)
    tabs = [_xpos_tables(0), _xpos_tables(1)]
    ams = [_mask_tables(0), _mask_tables(1)]

    x = np.asarray(ins["L0_attn_x"], np.float32)
    in_maps = []
    for core in range(8):
        b, par = core // 2, core % 2
        xs = np.empty((16, 128, TPC), np.float32)
        for lb in range(8):
            g = ZIG[par][lb]
            blkx = x[b, g * 128:(g + 1) * 128, :]
            xs[:, :, lb * 128:(lb + 1) * 128] = blkx.T.reshape(16, 128, 128)
        m = dict(w)
        m["x0"] = xs
        m["tabq"], m["tabk"] = tabs[par]
        m["am_t"] = ams[par]
        m["consts"] = consts
        m["identf"] = identf
        in_maps.append(m)

    res = run_bass_kernel_spmd(nc, in_maps, core_ids=list(range(8)))

    out = np.empty((4, 2048, 2048), np.float32)
    for core in range(8):
        b, par = core // 2, core % 2
        xo = res.results[core]["xout"]
        for lb in range(8):
            g = ZIG[par][lb]
            out[b, g * 128:(g + 1) * 128, :] = \
                xo[:, :, lb * 128:(lb + 1) * 128].reshape(2048, 128).T
    return out


if __name__ == "__main__":
    nc = build_program()
    print("build ok:", sum(1 for _ in nc.all_instructions()), "instructions")


# version 09 (final)
# speedup vs baseline: 1.2022x; 1.0013x over previous
import math
import numpy as np
import ml_dtypes
import concourse.bass as bass
import concourse.mybir as mybir
from concourse.bass_utils import run_bass_kernel_spmd

F32 = mybir.dt.float32
BF16 = mybir.dt.bfloat16
FP8 = mybir.dt.float8e4
AX = mybir.AxisListType.X
OP_ = mybir.AluOpType
AF = mybir.ActivationFunctionType

NDC = 16
TPC = 1024
DEBUG_DUMPS = False
ZIG = [[0, 3, 4, 7, 8, 11, 12, 15], [1, 2, 5, 6, 9, 10, 13, 14]]
OWNER = [0 if g in ZIG[0] else 1 for g in range(16)]
LOCIDX = [ZIG[OWNER[g]].index(g) for g in range(16)]
LEN = [[len([b_ for b_ in ZIG[p] if b_ >= j]) for j in range(16)]
       for p in (0, 1)]
LPROF = [max(LEN[0][j], LEN[1][j]) for j in range(16)]
SOFF = [(8 - LPROF[j]) * 128 for j in range(16)]
WJ = [LPROF[j] * 128 for j in range(16)]
INV_SQRT_D = 1.0 / math.sqrt(128.0)

SV_GNW, SV_GNB, SV_SGB, SV_SUB, SV_SDB, SV_BQ, SV_BK, SV_L1B = \
    0, 16, 32, 40, 48, 64, 80, 84
CN_PSWAP, CN_ECOL, CN_ONE, CN_ONER, CN_RWB = 0, 128, 384, 385, 516


def col_chunks(c0, c1):
    out = []
    c = c0
    while c < c1:
        nxt = min(c1, (c // 512 + 1) * 512)
        out.append((c, nxt))
        c = nxt
    return out


def build_program():
    nc = bass.Bass("TRN2", target_bir_lowering=False)

    def dram_in(name, shape, dt=F32):
        return nc.dram_tensor(name, shape, dt, kind="ExternalInput")

    x0 = dram_in("x0", [NDC, 128, TPC])
    wqkv_st = dram_in("wqkv_st", [32, 128, 2048])
    wv_mov = dram_in("wv_mov", [16, 128, 2048])
    wg_st = dram_in("wg_st", [16, 128, 2048])
    wp_st = dram_in("wp_st", [16, 128, 2048])
    rw_st = dram_in("rw_st", [16, 128, 8])
    wgu_st = dram_in("wgu_st", [2, 8, 8, 128, 2048])
    wd_st = dram_in("wd_st", [8, 16, 128, 1024])
    sgu_st = dram_in("sgu_st", [2, 8, 128, 2048])
    sd_st = dram_in("sd_st", [16, 128, 1024])
    smallw = dram_in("smallw", [128, 128])
    l1q_st = dram_in("l1q_st", [16, 128, 2048])
    l1k_st = dram_in("l1k_st", [4, 128, 2048])
    l1v_mov = dram_in("l1v_mov", [16, 128, 512])
    bv_row = dram_in("bv_row", [1, 512])
    l1w_st = dram_in("l1w_st", [16, 128, 2048], BF16)
    l1b_bf = dram_in("l1b_bf", [128, 16], BF16)
    tabq = dram_in("tabq", [16, 2, 128, TPC], BF16)
    tabk = dram_in("tabk", [16, 2, 128, TPC], BF16)
    am_t = dram_in("am_t", [16, 128, 256], BF16)
    consts = dram_in("consts", [128, 648], BF16)
    identf = dram_in("identf", [128, 128])

    ag0_in = nc.dram_tensor("ag0_in", [4096, 1024], BF16)
    ag0_out = nc.dram_tensor("ag0_out", [8192, 1024], BF16)
    ag1_in = nc.dram_tensor("ag1_in", [1024, 1024], BF16)
    ag1_out = nc.dram_tensor("ag1_out", [2048, 1024], BF16)
    ar_in = nc.dram_tensor("ar_in", [1, 16], F32)
    ar_out = nc.dram_tensor("ar_out", [1, 16], F32)
    scl_dram = nc.dram_tensor("scl_dram", [1, 4], F32)

    xout = nc.dram_tensor("xout", [NDC, 128, TPC], F32, kind="ExternalOutput")
    dbg = {}
    if DEBUG_DUMPS:
        for nm, dt_ in [("dbg_xn1", BF16), ("dbg_q0", BF16), ("dbg_y0", BF16),
                        ("dbg_g0", BF16), ("dbg_gated", BF16),
                        ("dbg_x1", F32), ("dbg_q1", BF16), ("dbg_y1", BF16),
                        ("dbg_g1", BF16)]:
            dbg[nm] = nc.dram_tensor(nm, [128, NDC * 1024], dt_)

    ctx = []

    def sb(name, shape, dt):
        t = nc.sbuf_tensor(name, shape, dt)
        ctx.append(t)
        return t.__enter__()

    def psum(name, shape, dt=F32):
        t = nc.psum_tensor(name, shape, dt)
        ctx.append(t)
        return t.__enter__()

    X = sb("X", [128, NDC * 1024], F32)
    A = sb("A", [128, NDC * 1024], BF16)
    Bq = sb("Bq", [128, NDC * 1024], BF16)
    Cb = sb("Cb", [128, NDC * 1024], BF16)
    Dg = sb("Dg", [128, NDC * 1024], BF16)
    WT0 = sb("WT0", [128, 2048], BF16)
    WT1 = sb("WT1", [128, 2048], BF16)
    SM0 = sb("SM0", [128, 1024], BF16)
    SM1 = sb("SM1", [128, 1024], BF16)
    SVC = sb("SVC", [128, 128], F32)
    CONS = sb("CONS", [128, 648], BF16)
    IDF = sb("IDF", [128, 128], F32)
    STT = sb("STT", [128, 256], F32)
    XSR = sb("XSR", [1, 16], F32)
    AMX = sb("AMX", [128, 32], F32)
    SM8 = sb("SM8", [128, 256], FP8)
    SCL = sb("SCL", [128, 4], F32)

    PA = psum("PA", [128, 1024])
    PB = psum("PB", [128, 1024])
    PC = psum("PC", [128, 1024])
    PD = psum("PD", [128, 1024])
    PSUM = {"PA": PA, "PB": PB, "PC": PC, "PD": PD}

    ONES_COL = CONS[:, CN_ONE:CN_ONE + 1]
    ONER = CONS[0:1, CN_ONER:CN_ONER + 128]
    PSWAP = CONS[:, CN_PSWAP:CN_PSWAP + 128]
    RWB = CONS[:, CN_RWB:CN_RWB + 128]

    def ECOLt(h):
        return CONS[:, CN_ECOL + h * 16:CN_ECOL + (h + 1) * 16]

    TBL = [Dg[:, 0:2048], Dg[:, 2048:4096]]
    T1S = [Dg[:, 4096:5120], Dg[:, 5120:6144]]
    RB_D = Dg[0:1, 6144:7168]
    VW = Dg[:, 0:8192]
    BVC = Dg[:, 8192:8704]
    RB_D2 = Dg[0:1, 8704:9728]
    HCB = Cb[:, 14 * 1024:15 * 1024]
    RB_C = Cb[0:1, 13 * 1024:13 * 1024 + 1024]
    CT = Cb[0:8, 15 * 1024:16 * 1024]
    AK = A[:, 0:2048]
    AVt = A[:, 2048:4096]
    AE = [A[:, 4096:5120], A[:, 5120:6144]]
    AMm = A[:, 6144:10240]
    GSA = A[0:16, 10240:11264]
    GSB = A[0:16, 11264:12288]
    RB_As = [A[0:1, 12288:14336], A[0:1, 14336:16384]]

    A_FAM = ["A", "AKV", "AE0", "AE1", "AM", "GS", "RB_A0", "RB_A1"]
    CB_FAM = ["Cb", "HCB", "RB_C", "CT"]
    DG_FAM = ["Dg", "TBL0", "TBL1", "T1S0", "T1S1", "RB_D", "BVC", "VW", "RB_D2"]

    blk = nc.Block()
    blk.__enter__()
    ctx.append(blk)

    sems = {}

    def sem(name):
        h = nc.semaphore(name)
        ctx.append(h)
        sems[name] = [h.__enter__(), 0]

    for s in ["eT", "eV", "eS", "cAG0", "cAG1", "cAR"]:
        sem(s)

    ENGSEM = {"TE": "eT", "VE": "eV", "SE": "eS"}
    DMAE = {"SY", "GP", "SQ"}

    lastw = {}
    readers = {}
    seen = {k: {} for k in ("TE", "VE", "SE", "SY", "GP", "SQ")}
    PARENT = {"TBL0": "Dg", "TBL1": "Dg", "T1S0": "Dg", "T1S1": "Dg",
              "RB_D": "Dg", "BVC": "Dg", "VW": "Dg", "RB_D2": "Dg", "HCB": "Cb", "RB_C": "Cb",
              "CT": "Cb", "AKV": "A", "AE0": "A", "AE1": "A", "AM": "A",
              "GS": "A", "GS2": "A", "RB_A0": "A", "RB_A1": "A"}

    def OP(engname, eng, reads, writes, emit, n_dma=1):
        toks = []
        for r in reads:
            t = lastw.get(r)
            if t:
                toks.append(t)
            p = PARENT.get(r)
            if p:
                t = lastw.get(p)
                if t:
                    toks.append(t)
        for w in writes:
            t = lastw.get(w)
            if t:
                toks.append(t)
            toks.extend(readers.get(w, ()))
            p = PARENT.get(w)
            if p:
                t = lastw.get(p)
                if t:
                    toks.append(t)
                toks.extend(readers.get(p, ()))
        agg = {}
        for (s, v) in toks:
            agg[s] = max(agg.get(s, 0), v)
        for s, v in sorted(agg.items()):
            if seen[engname].get(s, 0) < v:
                eng.wait_ge(sems[s][0], v)
                seen[engname][s] = v
        out = emit(eng)
        if engname in DMAE:
            sname = "q%s_%s" % (engname, writes[0])
            if sname not in sems:
                sem(sname)
            step = 16
        else:
            sname = ENGSEM[engname]
            step = 1
        if isinstance(out, (list, tuple)):
            for ins_ in out:
                sems[sname][1] += step
                ins_.then_inc(sems[sname][0], step)
        else:
            sems[sname][1] += step
            out.then_inc(sems[sname][0], step)
        tok = (sname, sems[sname][1])
        for r in reads:
            readers.setdefault(r, []).append(tok)
        for w in writes:
            lastw[w] = tok
            readers[w] = []
        return tok

    def TOK(name, v=None):
        return (name, sems[name][1] if v is None else v)

    def note(res, tok):
        lastw[res] = tok
        readers[res] = []

    def TE(fn):
        blk.tensor(fn)

    def VE(fn):
        blk.vector(fn)

    def SE(fn):
        blk.scalar(fn)

    def GP(fn):
        blk.gpsimd(fn)

    def SY(fn):
        blk.sync(fn)

    def mmsteps(te, P, c0, c1, steps, rbase=None, start=True, stop=True,
                rows=128):
        rb = c0 if rbase is None else rbase
        n = len(steps)
        last = None
        for (a, b_) in col_chunks(c0, c1):
            for i, (lh, rh) in enumerate(steps):
                last = te.matmul(P[0:rows, a:b_], lh, rh[:, a - rb:b_ - rb],
                                 start=(start and i == 0),
                                 stop=(stop and i == n - 1))
        return last

    def _(sy):
        OP("SY", sy, [], ["X"], lambda e: [
            e.dma_start(out=X[:, dc * 1024:(dc + 1) * 1024], in_=x0[dc])
            for dc in range(NDC)])
        OP("SY", sy, [], ["SVC"], lambda e: e.dma_start(
            out=SVC[:, 0:SV_L1B], in_=smallw[0:128, 0:SV_L1B]))
        OP("SY", sy, [], ["CONS"], lambda e: e.dma_start(
            out=CONS[:, 0:CN_RWB], in_=consts[0:128, 0:CN_RWB]))
        OP("SY", sy, [], ["IDF"], lambda e: e.dma_start(
            out=IDF[:, :], in_=identf.ap()))
    SY(_)

    def _(gp):
        OP("GP", gp, [], ["RWB"], lambda e: [
            e.dma_start(out=RWB[:, dc * 8:(dc + 1) * 8], in_=rw_st[dc])
            for dc in range(NDC)])
        OP("GP", gp, [], ["SVCb"], lambda e: e.dma_start(
            out=SVC[:, SV_L1B:SV_L1B + 16], in_=l1b_bf.ap()))
    GP(_)

    def _(ve):
        OP("VE", ve, [], ["XSR"], lambda e: e.memset(XSR[:, :], 0.0))
    VE(_)

    def emit_layernorm(rb_row, rb_res):
        def _(ve):
            OP("VE", ve, ["X"], ["A"] + A_FAM[1:], lambda e: [
                e.tensor_copy(A[:, dc * 1024:(dc + 1) * 1024],
                              X[:, dc * 1024:(dc + 1) * 1024])
                for dc in range(NDC)][-1:])
        VE(_)

        def _(se):
            OP("SE", se, ["A"], CB_FAM, lambda e: [
                e.activation(Cb[:, dc * 1024:(dc + 1) * 1024],
                             A[:, dc * 1024:(dc + 1) * 1024], AF.Square)
                for dc in range(NDC)][-1:])
        SE(_)

        def _(te):
            def em(e):
                last = None
                for tb in range(8):
                    for dc in range(NDC):
                        last = e.matmul(
                            PA[:, tb:tb + 1],
                            A[:, dc * 1024 + tb * 128:
                              dc * 1024 + (tb + 1) * 128],
                            ONES_COL, start=(dc == 0), stop=(dc == 15))
                    for dc in range(NDC):
                        last = e.matmul(
                            PA[:, 512 + tb:512 + tb + 1],
                            Cb[:, dc * 1024 + tb * 128:
                               dc * 1024 + (tb + 1) * 128],
                            ONES_COL, start=(dc == 0), stop=(dc == 15))
                return last
            OP("TE", te, ["A", "Cb", "CONS"], ["PA"], em)
        TE(_)

        def _(ve):
            OP("VE", ve, ["PA"], ["STTa"], lambda e: e.tensor_scalar(
                STT[:, 0:8], PA[:, 0:8], 1.0 / 2048.0, None, OP_.mult))
            OP("VE", ve, ["PA"], ["STTb"], lambda e: e.tensor_scalar(
                STT[:, 8:16], PA[:, 512:520], 1.0 / 2048.0, None, OP_.mult))
            OP("VE", ve, ["STTa"], ["STTc"], lambda e: e.tensor_tensor(
                STT[:, 16:24], STT[:, 0:8], STT[:, 0:8], OP_.mult))
            OP("VE", ve, ["STTb", "STTc"], ["STTb"], lambda e: e.tensor_tensor(
                STT[:, 8:16], STT[:, 8:16], STT[:, 16:24], OP_.subtract))
            OP("VE", ve, ["STTb"], ["STTb"], lambda e: e.tensor_scalar(
                STT[:, 8:16], STT[:, 8:16], 1e-5, None, OP_.add))
        VE(_)

        def _(se):
            OP("SE", se, ["STTb"], ["STTc"], lambda e: e.activation(
                STT[:, 16:24], STT[:, 8:16], AF.Sqrt))
        SE(_)

        def _(ve):
            OP("VE", ve, ["STTc"], ["STTb"], lambda e: e.reciprocal(
                STT[:, 8:16], STT[:, 16:24]))
            OP("VE", ve, ["STTa", "STTb"], ["STTc"], lambda e: e.tensor_tensor(
                STT[:, 16:24], STT[:, 0:8], STT[:, 8:16], OP_.mult))
            OP("VE", ve, ["STTc"], ["STTc"], lambda e: e.tensor_scalar(
                STT[:, 16:24], STT[:, 16:24], -1.0, None, OP_.mult))
        VE(_)

        def _(te):
            def em(e):
                e.transpose(PC[0:8, 0:128], STT[:, 8:16], IDF[:, :])
                return e.transpose(PC[0:8, 128:256], STT[:, 16:24],
                                   IDF[:, :])
            OP("TE", te, ["STTb", "STTc", "IDF"], ["PC"], em)
        TE(_)

        def _(ve):
            OP("VE", ve, ["PC"], ["SM0"], lambda e: e.tensor_copy(
                SM0[0:8, 0:256], PC[0:8, 0:256]))
        VE(_)

        def _(sy):
            OP("SY", sy, ["SM0"], [rb_res], lambda e: e.dma_start(
                out=rb_row, in_=SM0[0:8, 0:128]))
        SY(_)

        def _(te):
            OP("TE", te, [rb_res, "CONS"], ["PA"], lambda e: mmsteps(
                e, PA, 0, 1024, [(ONER, rb_row)]))
        TE(_)

        def _(sy):
            OP("SY", sy, ["SM0"], [rb_res], lambda e: e.dma_start(
                out=rb_row, in_=SM0[0:8, 128:256]))
        SY(_)

        def _(te):
            OP("TE", te, [rb_res, "CONS"], ["PB"], lambda e: mmsteps(
                e, PB, 0, 1024, [(ONER, rb_row)]))
        TE(_)

        def _(ve):
            OP("VE", ve, ["X", "PA"], CB_FAM, lambda e: [
                e.tensor_tensor(Cb[:, dc * 1024:(dc + 1) * 1024],
                                X[:, dc * 1024:(dc + 1) * 1024],
                                PA[:, 0:1024], OP_.mult)
                for dc in range(NDC)][-1:])
            OP("VE", ve, ["Cb", "PB"], ["A"] + A_FAM[1:], lambda e: [
                e.tensor_tensor(A[:, dc * 1024:(dc + 1) * 1024],
                                Cb[:, dc * 1024:(dc + 1) * 1024],
                                PB[:, 0:1024], OP_.add)
                for dc in range(NDC)][-1:])
        VE(_)

    emit_layernorm(RB_D, "RB_D")


    if DEBUG_DUMPS:
        def _(sy):
            OP("SY", sy, ["A"], ["dbg_xn1"], lambda e: e.dma_start(
                out=dbg["dbg_xn1"].ap(), in_=A[:, :]))
        SY(_)
    for cc in range(32):
        sl = cc % 2
        pn = "PA" if sl == 0 else "PB"
        rn = "PC" if sl == 0 else "PD"
        qb = [SM0, SM1][sl]
        qn = ["SM0", "SM1"][sl]
        t1 = T1S[sl]
        tn = ["T1S0", "T1S1"][sl]
        wn = ["WT0", "WT1"][sl]
        tbl = TBL[sl]
        tbn = ["TBL0", "TBL1"][sl]

        def _(gp, cc=cc, sl=sl, wn=wn):
            OP("GP", gp, [], [wn], lambda e: e.dma_start(
                out=[WT0, WT1][sl][:, 0:2048], in_=wqkv_st[cc]))
        GP(_)

        def _(sq, cc=cc, sl=sl, tbl=tbl, tbn=tbn):
            src = tabq if cc < 16 else tabk
            OP("SQ", sq, [], [tbn], lambda e: e.dma_start(
                out=tbl[:, 0:2048],
                in_=src[cc % 16].rearrange("w p t -> p w t")))
        SE(_)

        def _(te, cc=cc, sl=sl, pn=pn, wn=wn):
            OP("TE", te, [wn, "A"], [pn], lambda e: mmsteps(
                e, PSUM[pn], 0, 1024,
                [([WT0, WT1][sl][:, dc * 128:(dc + 1) * 128],
                  A[:, dc * 1024:(dc + 1) * 1024]) for dc in range(NDC)]))
        TE(_)

        def _(ve, cc=cc, sl=sl, pn=pn, qb=qb, qn=qn):
            OP("VE", ve, [pn], [qn], lambda e: e.tensor_copy(
                qb[:, 0:1024], PSUM[pn][:, 0:1024]))
        VE(_)

        def _(te, cc=cc, sl=sl, rn=rn, qb=qb, qn=qn):
            OP("TE", te, [qn, "CONS"], [rn], lambda e: mmsteps(
                e, PSUM[rn], 0, 1024, [(PSWAP, qb[:, 0:1024])]))
        TE(_)

        def _(ve, cc=cc, sl=sl, rn=rn, qb=qb, qn=qn, t1=t1, tn=tn, tbn=tbn):
            tbl_ = TBL[sl]
            OP("VE", ve, [qn, tbn], [tn], lambda e: e.tensor_tensor(
                t1[:, 0:1024], qb[:, 0:1024], tbl_[:, 0:1024], OP_.mult))
            OP("VE", ve, [rn, tbn], [qn], lambda e: e.tensor_tensor(
                qb[:, 0:1024], PSUM[rn][:, 0:1024], tbl_[:, 1024:2048],
                OP_.mult))
            if cc < 16:
                OP("VE", ve, [tn, qn], ["Bq"], lambda e: e.tensor_tensor(
                    Bq[:, cc * 1024:(cc + 1) * 1024], t1[:, 0:1024],
                    qb[:, 0:1024], OP_.add))
            else:
                OP("VE", ve, [tn, qn], [tn], lambda e: e.tensor_tensor(
                    t1[:, 0:1024], t1[:, 0:1024], qb[:, 0:1024], OP_.add))
        VE(_)

        if cc >= 16:
            def _(sy, cc=cc, t1=t1, tn=tn):
                OP("SY", sy, [tn], ["ag0_in"], lambda e: e.dma_start(
                    out=ag0_in[(cc - 16) * 128:(cc - 15) * 128, :],
                    in_=t1[:, 0:1024]))
            SY(_)

    for dvq in range(2):
        def _(gp, dvq=dvq):
            OP("GP", gp, [], CB_FAM, lambda e: [
                e.dma_start(out=Cb[:, dc * 1024:(dc + 1) * 1024],
                            in_=wv_mov[dc, :, dvq * 1024:(dvq + 1) * 1024])
                for dc in range(NDC)])
        GP(_)

        for tb in range(8):
            sl = tb % 2
            pn = "PA" if sl == 0 else "PB"

            def _(te, dvq=dvq, tb=tb, sl=sl, pn=pn):
                OP("TE", te, ["Cb", "A"], [pn], lambda e: mmsteps(
                    e, PSUM[pn], 0, 1024,
                    [(A[:, dc * 1024 + tb * 128:dc * 1024 + (tb + 1) * 128],
                      Cb[:, dc * 1024:(dc + 1) * 1024])
                     for dc in range(NDC)]))
            TE(_)

            def _(ve, dvq=dvq, tb=tb, sl=sl, pn=pn):
                OP("VE", ve, [pn], [["SM0", "SM1"][sl]],
                   lambda e: e.tensor_copy([SM0, SM1][sl][:, 0:1024],
                                           PSUM[pn][:, 0:1024]))
            VE(_)

            def _(sy, dvq=dvq, tb=tb, sl=sl):
                r0 = 2048 + (tb * 2 + dvq) * 128
                OP("SY", sy, [["SM0", "SM1"][sl]], ["ag0_in"],
                   lambda e: e.dma_start(out=ag0_in[r0:r0 + 128, :],
                                         in_=[SM0, SM1][sl][:, 0:1024]))
            SY(_)

    def _(gp):
        def em(e):
            return e.collective_compute(
                "AllGather", OP_.bypass,
                replica_groups=[[0, 1], [2, 3], [4, 5], [6, 7]],
                ins=[ag0_in.ap().opt()], outs=[ag0_out.ap().opt()])
        toks = []
        t = lastw.get("ag0_in")
        if t:
            toks.append(t)
        for (s, v) in toks:
            if seen["GP"].get(s, 0) < v:
                gp.wait_ge(sems[s][0], v)
                seen["GP"][s] = v
        ins_ = em(gp)
        sems["cAG0"][1] += 1
        ins_.then_inc(sems["cAG0"][0], 1)
        note("ag0_out", ("cAG0", sems["cAG0"][1]))
    GP(_)


    if DEBUG_DUMPS:
        def _(sy):
            OP("SY", sy, ["Bq"], ["dbg_q0"], lambda e: e.dma_start(
                out=dbg["dbg_q0"].ap(), in_=Bq[:, :]))
        SY(_)
    def _(te):
        def em(e):
            last = None
            for tb in range(8):
                for dc in range(NDC):
                    last = e.matmul(
                        PB[:, tb * 8:(tb + 1) * 8],
                        A[:, dc * 1024 + tb * 128:dc * 1024 + (tb + 1) * 128],
                        RWB[:, dc * 8:(dc + 1) * 8],
                        start=(dc == 0), stop=(dc == 15))
            return last
        OP("TE", te, ["A", "RWB"], ["PB"], em)
    TE(_)

    def _(ve):
        for tb in range(8):
            lo = PB[:, tb * 8:(tb + 1) * 8]
            OP("VE", ve, ["PB"], ["STTa"], lambda e, lo=lo: e.reduce_max(
                STT[:, 24:25], lo, axis=AX))
            OP("VE", ve, ["PB", "STTa"], ["STTb"],
               lambda e, lo=lo: e.tensor_scalar(
                   STT[:, 32:40], lo, STT[:, 24:25], None, OP_.subtract))
            OP("VE", ve, ["STTb"], ["STTc"], lambda e: e.tensor_scalar(
                STT[:, 40:48], STT[:, 32:40], 0.0, None, OP_.is_ge))
            OP("VE", ve, ["STTc"], ["STTd"], lambda e: e.tensor_scalar(
                STT[:, 48:56], STT[:, 40:48], -1e30, None, OP_.mult))
            OP("VE", ve, ["PB", "STTd"], ["STTb"],
               lambda e, lo=lo: e.tensor_tensor(
                   STT[:, 32:40], lo, STT[:, 48:56], OP_.add))
            OP("VE", ve, ["STTb"], ["STTa"], lambda e: e.reduce_max(
                STT[:, 25:26], STT[:, 32:40], axis=AX))
            OP("VE", ve, ["PB", "STTa"], ["STTc"],
               lambda e, lo=lo: e.tensor_scalar(
                   STT[:, 40:48], lo, STT[:, 25:26], None, OP_.is_ge))
            OP("VE", ve, ["PB", "STTc"], ["COMB"],
               lambda e, lo=lo, tb=tb: e.tensor_tensor(
                   STT[:, 160 + tb * 8:160 + (tb + 1) * 8], lo,
                   STT[:, 40:48], OP_.mult))
    VE(_)

    def _(te):
        def em(e):
            last = None
            for tb in range(8):
                last = e.transpose(PC[0:8, tb * 128:(tb + 1) * 128],
                                   STT[:, 160 + tb * 8:160 + (tb + 1) * 8],
                                   IDF[:, :])
            return last
        OP("TE", te, ["COMB", "IDF"], ["PC"], em)
    TE(_)

    def _(ve):
        OP("VE", ve, ["PC"], ["CT"], lambda e: e.tensor_copy(
            CT, PC[0:8, 0:1024]))
    VE(_)

    for e_ in range(9):
        shared = (e_ == 8)
        if not shared:
            def _(sy, e_=e_):
                OP("SY", sy, ["CT"], ["RB_C"], lambda e: e.dma_start(
                    out=RB_C, in_=CT[e_:e_ + 1, 0:1024]))
            SY(_)

            def _(te, e_=e_):
                OP("TE", te, ["RB_C", "CONS"], ["PC"], lambda e: mmsteps(
                    e, PC, 0, 1024, [(ONER, RB_C)]))
            TE(_)

            def _(ve, e_=e_):
                OP("VE", ve, ["PC"], ["HCB"], lambda e: e.tensor_copy(
                    HCB[:, 0:1024], PC[:, 0:1024]))
            VE(_)

        for icc in range(8):
            def _(gp, e_=e_, icc=icc):
                if shared:
                    OP("GP", gp, [], ["WT0"], lambda e: e.dma_start(
                        out=WT0[:, 0:2048], in_=sgu_st[0, icc]))
                    OP("GP", gp, [], ["WT1"], lambda e: e.dma_start(
                        out=WT1[:, 0:2048], in_=sgu_st[1, icc]))
                else:
                    OP("GP", gp, [], ["WT0"], lambda e: e.dma_start(
                        out=WT0[:, 0:2048], in_=wgu_st[0, e_, icc]))
                    OP("GP", gp, [], ["WT1"], lambda e: e.dma_start(
                        out=WT1[:, 0:2048], in_=wgu_st[1, e_, icc]))
            GP(_)

            def _(te, e_=e_, icc=icc):
                OP("TE", te, ["WT0", "A"], ["PA"], lambda e: mmsteps(
                    e, PA, 0, 1024,
                    [(WT0[:, dc * 128:(dc + 1) * 128],
                      A[:, dc * 1024:(dc + 1) * 1024])
                     for dc in range(NDC)]))
                OP("TE", te, ["WT1", "A"], ["PB"], lambda e: mmsteps(
                    e, PB, 0, 1024,
                    [(WT1[:, dc * 128:(dc + 1) * 128],
                      A[:, dc * 1024:(dc + 1) * 1024])
                     for dc in range(NDC)]))
            TE(_)

            def _(se, e_=e_, icc=icc, shared=shared):
                OP("SE", se, ["PA", "SVC"], ["SM0"], lambda e: e.activation(
                    SM0[:, 0:1024], PA[:, 0:1024], AF.Silu,
                    bias=(SVC[:, SV_SGB + icc:SV_SGB + icc + 1] if shared
                          else 0.0)))
            SE(_)

            def _(ve, e_=e_, icc=icc, shared=shared):
                if shared:
                    OP("VE", ve, ["PB", "SVC"], ["SM1"],
                       lambda e: e.tensor_scalar(
                           SM1[:, 0:1024], PB[:, 0:1024],
                           SVC[:, SV_SUB + icc:SV_SUB + icc + 1], None,
                           OP_.add))
                else:
                    OP("VE", ve, ["PB", "HCB"], ["SM1"],
                       lambda e: e.tensor_tensor(
                           SM1[:, 0:1024], PB[:, 0:1024], HCB[:, 0:1024],
                           OP_.mult))
                OP("VE", ve, ["SM0", "SM1"], ["Cb"],
                   lambda e: e.tensor_tensor(
                       Cb[:, icc * 1024:(icc + 1) * 1024],
                       SM0[:, 0:1024], SM1[:, 0:1024], OP_.mult))
            VE(_)

        for occ in range(NDC):
            sl = occ % 2
            pn = "PA" if sl == 0 else "PB"
            wn = ["WT0", "WT1"][sl]

            def _(gp, e_=e_, occ=occ, sl=sl, wn=wn, shared=shared):
                src = sd_st[occ] if shared else wd_st[e_, occ]
                OP("GP", gp, [], [wn], lambda e: e.dma_start(
                    out=[WT0, WT1][sl][:, 0:1024], in_=src))
            GP(_)

            def _(te, e_=e_, occ=occ, sl=sl, pn=pn, wn=wn):
                OP("TE", te, [wn, "Cb"], [pn], lambda e: mmsteps(
                    e, PSUM[pn], 0, 1024,
                    [([WT0, WT1][sl][:, icc * 128:(icc + 1) * 128],
                      Cb[:, icc * 1024:(icc + 1) * 1024])
                     for icc in range(8)]))
            TE(_)

            if e_ == 0:
                def _(se, occ=occ, sl=sl, pn=pn):
                    OP("SE", se, [pn], ["Dg"] + DG_FAM[1:],
                       lambda e: e.activation(
                           Dg[:, occ * 1024:(occ + 1) * 1024],
                           PSUM[pn][:, 0:1024], AF.Copy))
                SE(_)
            else:
                def _(ve, occ=occ, sl=sl, pn=pn):
                    def em(e):
                        with nc.allow_low_precision(
                                reason="bf16 moe partial-sum acc, in tol"):
                            return e.tensor_tensor(
                                Dg[:, occ * 1024:(occ + 1) * 1024],
                                Dg[:, occ * 1024:(occ + 1) * 1024],
                                PSUM[pn][:, 0:1024], OP_.add)
                    OP("VE", ve, [pn, "Dg"], ["Dg"], em)
                VE(_)

    def _(se):
        OP("SE", se, ["Dg", "SVC"], ["Dg"], lambda e: [
            e.activation(Dg[:, occ * 1024:(occ + 1) * 1024],
                         Dg[:, occ * 1024:(occ + 1) * 1024], AF.Silu,
                         bias=SVC[:, SV_SDB + occ:SV_SDB + occ + 1])
            for occ in range(NDC)][-1:])
    SE(_)


    if DEBUG_DUMPS:
        def _(sy):
            OP("SY", sy, ["Dg"], ["dbg_g0"], lambda e: e.dma_start(
                out=dbg["dbg_g0"].ap(), in_=Dg[:, :]))
        SY(_)
    for cc in range(NDC):
        sl = cc % 2
        pn = "PA" if sl == 0 else "PB"
        wn = ["WT0", "WT1"][sl]

        def _(gp, cc=cc, sl=sl, wn=wn):
            OP("GP", gp, [], [wn], lambda e: e.dma_start(
                out=[WT0, WT1][sl][:, 0:2048], in_=wg_st[cc]))
        GP(_)

        def _(te, cc=cc, sl=sl, pn=pn, wn=wn):
            OP("TE", te, [wn, "A"], [pn], lambda e: mmsteps(
                e, PSUM[pn], 0, 1024,
                [([WT0, WT1][sl][:, dc * 128:(dc + 1) * 128],
                  A[:, dc * 1024:(dc + 1) * 1024]) for dc in range(NDC)]))
        TE(_)

        def _(se, cc=cc, sl=sl, pn=pn):
            OP("SE", se, [pn], CB_FAM, lambda e: e.activation(
                Cb[:, cc * 1024:(cc + 1) * 1024], PSUM[pn][:, 0:1024],
                AF.Silu))
        SE(_)

    def _(sy):
        OP("SY", sy, [], ["AM", "A", "AKV", "AE0", "AE1", "GS", "RB_A0", "RB_A1"],
           lambda e: e.dma_start(
               out=AMm, in_=am_t.ap().rearrange("j p c -> p j c")))
    SY(_)

    def emit_attention(layer):
        for h in range(16):
            if layer == 0:
                def _(sy, h=h):
                    def em(e):
                        outl = []
                        for r in range(2):
                            outl.append(e.dma_start(
                                out=AK[:, r * 1024:(r + 1) * 1024],
                                in_=ag0_out[r * 4096 + h * 128:
                                            r * 4096 + (h + 1) * 128, :]))
                        dvq = h // 8
                        for g in range(16):
                            r, tb = OWNER[g], LOCIDX[g]
                            r0 = r * 4096 + 2048 + (tb * 2 + dvq) * 128
                            c0 = (h % 8) * 128
                            outl.append(e.dma_start(
                                out=AVt[:, g * 128:(g + 1) * 128],
                                in_=ag0_out[r0:r0 + 128, c0:c0 + 128]))
                        return outl
                    OP("SY", sy, ["ag0_out"], ["AKV"], em)
                SY(_)

            def emit_S(j):
                sl = j % 2
                pn = "PA" if sl == 0 else "PB"
                W = WJ[j]

                def _(te, h=h, j=j, sl=sl, pn=pn, W=W, layer=layer):
                    if layer == 0:
                        lh = AK[:, OWNER[j] * 1024 + LOCIDX[j] * 128:
                                OWNER[j] * 1024 + (LOCIDX[j] + 1) * 128]
                        kres = "AKV"
                    else:
                        gr = h // 4
                        lh = Cb[:, gr * 2048 + OWNER[j] * 1024
                                + LOCIDX[j] * 128:
                                gr * 2048 + OWNER[j] * 1024
                                + (LOCIDX[j] + 1) * 128]
                        kres = "Cb"
                    rh = Bq[:, h * 1024 + SOFF[j]:(h + 1) * 1024]
                    OP("TE", te, [kres, "Bq"], [pn], lambda e: mmsteps(
                        e, PSUM[pn], 0, W, [(lh, rh)]))
                TE(_)

            def emit_evict_av(j):
                sl = j % 2
                pn = "PA" if sl == 0 else "PB"
                aen = ["AE0", "AE1"][sl]
                W = WJ[j]
                if layer == 0:
                    def _(ve, h=h, j=j, sl=sl, pn=pn, aen=aen, W=W):
                        m = min(256, W)

                        def em(e):
                            i1 = e.tensor_tensor(
                                AE[sl][:, 0:m], PSUM[pn][:, 0:m],
                                AMm[:, j * 256:j * 256 + m], OP_.mult)
                            if W > 256:
                                i1 = e.tensor_copy(AE[sl][:, 256:W],
                                                   PSUM[pn][:, 256:W])
                            return i1
                        OP("VE", ve, [pn, "AM"], [aen], em)
                    VE(_)
                else:
                    def _(se, h=h, j=j, sl=sl, pn=pn, aen=aen, W=W):
                        OP("SE", se, [pn], [aen], lambda e: e.activation(
                            AE[sl][:, 0:W], PSUM[pn][:, 0:W], AF.Exp,
                            scale=INV_SQRT_D))
                    SE(_)

                    def _(ve, h=h, j=j, sl=sl, aen=aen, W=W):
                        m = min(256, W)
                        OP("VE", ve, [aen, "AM"], [aen],
                           lambda e: e.tensor_tensor(
                               AE[sl][:, 0:m], AE[sl][:, 0:m],
                               AMm[:, j * 256:j * 256 + m], OP_.mult))
                    VE(_)

                def _(te, h=h, j=j, sl=sl, aen=aen, W=W, layer=layer):
                    if layer == 0:
                        lh = AVt[:, j * 128:(j + 1) * 128]
                        vres = "AKV"
                    else:
                        lh = Cb[:, 8192 + j * 512 + (h // 4) * 128:
                                8192 + j * 512 + (h // 4 + 1) * 128]
                        vres = "Cb"

                    def em(e):
                        last = None
                        for (a, b_) in col_chunks(SOFF[j], 1024):
                            lastj = 7 if a < 512 else 15
                            last = e.matmul(
                                PC[:, a:b_], lh,
                                AE[sl][:, a - SOFF[j]:b_ - SOFF[j]],
                                start=(j == 0), stop=(j == lastj))
                        if layer == 1:
                            for (a, b_) in col_chunks(SOFF[j], 1024):
                                lastj = 7 if a < 512 else 15
                                last = e.matmul(
                                    PD[0:16, a:b_], ECOLt(h),
                                    AE[sl][:, a - SOFF[j]:b_ - SOFF[j]],
                                    start=(h == 0 and j == 0),
                                    stop=(h == 15 and j == lastj))
                        return last
                    wr = ["PC"] if layer == 0 else ["PC", "PD"]
                    OP("TE", te, [aen, vres, "CONS"], wr, em)
                TE(_)

            for j in range(16):
                emit_S(j)
                if j >= 1:
                    emit_evict_av(j - 1)
            emit_evict_av(15)

            def _(ve, h=h):
                OP("VE", ve, ["PC"], ["Bq"], lambda e: e.tensor_copy(
                    Bq[:, h * 1024:(h + 1) * 1024], PC[:, 0:1024]))
            VE(_)
    emit_attention(0)

    if DEBUG_DUMPS:
        def _(sy):
            OP("SY", sy, ["Bq"], ["dbg_y0"], lambda e: e.dma_start(
                out=dbg["dbg_y0"].ap(), in_=Bq[:, :]))
        SY(_)

    for h in range(16):
        sq = [SM0, SM1][h % 2]
        sqn = ["SM0", "SM1"][h % 2]

        def _(se, h=h, sq=sq, sqn=sqn):
            OP("SE", se, ["Bq"], [sqn], lambda e: e.activation(
                sq[:, 0:1024], Bq[:, h * 1024:(h + 1) * 1024], AF.Square))
        SE(_)

        def _(te, h=h, sq=sq, sqn=sqn):
            OP("TE", te, ["Bq", "CONS"], ["PA"], lambda e: mmsteps(
                e, PA, 0, 1024,
                [(ECOLt(h), Bq[:, h * 1024:(h + 1) * 1024])],
                start=(h == 0), stop=(h == 15), rows=16))
            OP("TE", te, [sqn, "CONS"], ["PB"], lambda e: mmsteps(
                e, PB, 0, 1024, [(ECOLt(h), sq[:, 0:1024])],
                start=(h == 0), stop=(h == 15), rows=16))
        TE(_)

    def _(ve):
        OP("VE", ve, ["PA"], ["GS"], lambda e: e.tensor_copy(
            GSA, PA[0:16, 0:1024]))
        OP("VE", ve, ["PB", "GS"], ["GS2"], lambda e: e.tensor_copy(
            GSB, PB[0:16, 0:1024]))
    VE(_)

    def _(gp):
        OP("GP", gp, ["GS"], ["STTa"], lambda e: [
            e.dma_start(
                out=STT[:, hh * 8:(hh + 1) * 8],
                in_=GSA[hh:hh + 1, :].rearrange("o (p i) -> o p i",
                                                p=128, i=8))
            for hh in range(16)])
        OP("GP", gp, ["GS2"], ["STTb"], lambda e: [
            e.dma_start(
                out=STT[:, 128 + hh * 8:128 + (hh + 1) * 8],
                in_=GSB[hh:hh + 1, :].rearrange("o (p i) -> o p i",
                                                p=128, i=8))
            for hh in range(16)])
    GP(_)

    def _(ve):
        OP("VE", ve, ["STTa"], ["STTa"], lambda e: e.tensor_scalar(
            STT[:, 0:128], STT[:, 0:128], 1.0 / 128.0, None, OP_.mult))
        OP("VE", ve, ["STTb"], ["STTb"], lambda e: e.tensor_scalar(
            STT[:, 128:256], STT[:, 128:256], 1.0 / 128.0, None, OP_.mult))
        for qq in range(4):
            OP("VE", ve, ["STTa"], ["AMX"], lambda e, qq=qq: e.tensor_tensor(
                AMX[:, 0:32], STT[:, qq * 32:(qq + 1) * 32],
                STT[:, qq * 32:(qq + 1) * 32], OP_.mult))
            OP("VE", ve, ["STTb", "AMX"], ["STTb"],
               lambda e, qq=qq: e.tensor_tensor(
                   STT[:, 128 + qq * 32:128 + (qq + 1) * 32],
                   STT[:, 128 + qq * 32:128 + (qq + 1) * 32],
                   AMX[:, 0:32], OP_.subtract))
        OP("VE", ve, ["STTb"], ["STTb"], lambda e: e.tensor_scalar(
            STT[:, 128:256], STT[:, 128:256], 1e-5, None, OP_.add))
    VE(_)

    def _(se):
        OP("SE", se, ["STTb"], ["STTb"], lambda e: e.activation(
            STT[:, 128:256], STT[:, 128:256], AF.Sqrt))
    SE(_)

    def _(ve):
        OP("VE", ve, ["STTb"], ["STTb"], lambda e: e.reciprocal(
            STT[:, 128:256], STT[:, 128:256]))
        OP("VE", ve, ["STTa", "STTb"], ["STTa"], lambda e: e.tensor_tensor(
            STT[:, 0:128], STT[:, 0:128], STT[:, 128:256], OP_.mult))
        OP("VE", ve, ["STTa"], ["STTa"], lambda e: e.tensor_scalar(
            STT[:, 0:128], STT[:, 0:128], -1.0, None, OP_.mult))
    VE(_)

    for h in range(16):
        rbs = RB_As[h % 2]
        rbn = ["RB_A0", "RB_A1"][h % 2]

        def _(gp, h=h, rbs=rbs, rbn=rbn):
            OP("GP", gp, ["STTa", "STTb"], [rbn], lambda e: [
                e.dma_start(out=rbs[0:1, 0:1024],
                            in_=STT[:, 128 + h * 8:128 + (h + 1) * 8]),
                e.dma_start(out=rbs[0:1, 1024:2048],
                            in_=STT[:, h * 8:(h + 1) * 8])])
        GP(_)

        def _(te, h=h, rbs=rbs, rbn=rbn):
            OP("TE", te, [rbn, "CONS"], ["PA"], lambda e: mmsteps(
                e, PA, 0, 1024, [(ONER, rbs[0:1, 0:1024])]))
            OP("TE", te, [rbn, "CONS"], ["PB"], lambda e: mmsteps(
                e, PB, 0, 1024, [(ONER, rbs[0:1, 1024:2048])]))
        TE(_)

        def _(ve, h=h):
            OP("VE", ve, ["Bq", "PA"], ["SM0"], lambda e: e.tensor_tensor(
                SM0[:, 0:1024], Bq[:, h * 1024:(h + 1) * 1024],
                PA[:, 0:1024], OP_.mult))
            OP("VE", ve, ["SM0", "PB"], ["SM1"], lambda e: e.tensor_tensor(
                SM1[:, 0:1024], SM0[:, 0:1024], PB[:, 0:1024], OP_.add))
            OP("VE", ve, ["SM1", "SVC"], ["SM0"], lambda e: e.tensor_scalar(
                SM0[:, 0:1024], SM1[:, 0:1024],
                SVC[:, SV_GNW + h:SV_GNW + h + 1],
                SVC[:, SV_GNB + h:SV_GNB + h + 1], OP_.mult, OP_.add))
            OP("VE", ve, ["SM0", "Cb"], ["Cb"], lambda e: e.tensor_tensor(
                Cb[:, h * 1024:(h + 1) * 1024],
                Cb[:, h * 1024:(h + 1) * 1024], SM0[:, 0:1024], OP_.mult))
        VE(_)


    if DEBUG_DUMPS:
        def _(sy):
            OP("SY", sy, ["Cb"], ["dbg_gated"], lambda e: e.dma_start(
                out=dbg["dbg_gated"].ap(), in_=Cb[:, :]))
        SY(_)
    for occ in range(NDC):
        sl = occ % 2
        pn = "PA" if sl == 0 else "PB"
        wn = ["WT0", "WT1"][sl]
        sqn = ["SM0", "SM1"][sl]

        def _(gp, occ=occ, sl=sl, wn=wn):
            OP("GP", gp, [], [wn], lambda e: e.dma_start(
                out=[WT0, WT1][sl][:, 0:2048], in_=wp_st[occ]))
        GP(_)

        def _(te, occ=occ, sl=sl, pn=pn, wn=wn):
            OP("TE", te, [wn, "Cb"], [pn], lambda e: mmsteps(
                e, PSUM[pn], 0, 1024,
                [([WT0, WT1][sl][:, cc * 128:(cc + 1) * 128],
                  Cb[:, cc * 1024:(cc + 1) * 1024]) for cc in range(NDC)]))
        TE(_)

        def _(ve, occ=occ, sl=sl, pn=pn, sqn=sqn):
            OP("VE", ve, [pn, "Dg"], [sqn], lambda e: e.tensor_tensor(
                [SM0, SM1][sl][:, 0:1024], PSUM[pn][:, 0:1024],
                Dg[:, occ * 1024:(occ + 1) * 1024], OP_.mult))
            OP("VE", ve, [sqn, "X"], ["X"], lambda e: e.tensor_tensor(
                X[:, occ * 1024:(occ + 1) * 1024],
                X[:, occ * 1024:(occ + 1) * 1024],
                [SM0, SM1][sl][:, 0:1024], OP_.add))
        VE(_)


    if DEBUG_DUMPS:
        def _(sy):
            OP("SY", sy, ["X"], ["dbg_x1"], lambda e: e.dma_start(
                out=dbg["dbg_x1"].ap(), in_=X[:, :]))
        SY(_)
    emit_layernorm(RB_D, "RB_D")

    def _(ve):
        OP("VE", ve, ["A"], ["AMX"], lambda e: [
            e.reduce_max(AMX[:, dc:dc + 1],
                         A[:, dc * 1024:(dc + 1) * 1024], axis=AX,
                         apply_absolute_value=True) for dc in range(NDC)][-1:])
        OP("VE", ve, ["AMX"], ["AMX2"], lambda e: e.reduce_max(
            AMX[:, 16:17], AMX[:, 0:16], axis=AX, apply_absolute_value=True))
    VE(_)

    def _(sy):
        OP("SY", sy, ["AMX2"], ["STTb"], lambda e: e.dma_start(
            out=STT[0:1, 128:256], in_=AMX[:, 16:17]))
    SY(_)

    def _(ve):
        OP("VE", ve, ["STTb", "XSR"], ["XSR"], lambda e: e.reduce_max(
            XSR[0:1, 0:1], STT[0:1, 128:256], axis=AX,
            apply_absolute_value=True))
    VE(_)

    def _(sy):
        OP("SY", sy, ["XSR"], ["ar_in"], lambda e: e.dma_start(
            out=ar_in[0:1, :], in_=XSR[0:1, 0:16]))
    SY(_)

    def _(gp):
        t = lastw.get("ar_in")
        if t and seen["GP"].get(t[0], 0) < t[1]:
            gp.wait_ge(sems[t[0]][0], t[1])
            seen["GP"][t[0]] = t[1]
        ins_ = gp.collective_compute(
            "AllReduce", OP_.max,
            replica_groups=[[0, 1, 2, 3, 4, 5, 6, 7]],
            ins=[ar_in.ap().opt()], outs=[ar_out.ap().opt()])
        sems["cAR"][1] += 1
        ins_.then_inc(sems["cAR"][0], 1)
        note("ar_out", ("cAR", sems["cAR"][1]))
    GP(_)

    def _(sy):
        OP("SY", sy, ["ar_out"], ["XSR"], lambda e: e.dma_start(
            out=XSR[0:1, 0:16], in_=ar_out[0:1, :]))
    SY(_)

    def _(ve):
        OP("VE", ve, ["XSR"], ["XSR"], lambda e: e.reciprocal(
            XSR[0:1, 2:3], XSR[0:1, 0:1]))
    VE(_)

    def _(sy):
        OP("SY", sy, ["XSR"], ["scl_dram"], lambda e: e.dma_start(
            out=scl_dram[0:1, 0:1], in_=XSR[0:1, 2:3]))
        srcap0 = bass.AP(scl_dram, 0, [[0, 128], [1, 1]])
        OP("SY", sy, ["scl_dram"], ["SCL"], lambda e: e.dma_start(
            out=SCL[:, 0:1], in_=srcap0))
    SY(_)

    def _(ve):
        for dc in range(NDC):
            for q4 in range(4):
                c0 = dc * 1024 + q4 * 256
                OP("VE", ve, ["A", "SCL"], ["SM8"],
                   lambda e, c0=c0: e.tensor_scalar(
                       SM8[:, 0:256], A[:, c0:c0 + 256], SCL[:, 0:1], None,
                       OP_.mult))
                OP("VE", ve, ["SM8"], CB_FAM if (dc == 0 and q4 == 0)
                   else ["Cb"],
                   lambda e, c0=c0: e.tensor_copy(Cb[:, c0:c0 + 256],
                                                  SM8[:, 0:256]))
    VE(_)

    for cc in range(NDC):
        sl = cc % 2
        pn = "PA" if sl == 0 else "PB"
        wn = ["WT0", "WT1"][sl]

        def _(gp, cc=cc, sl=sl, wn=wn):
            OP("GP", gp, [], [wn], lambda e: e.dma_start(
                out=[WT0, WT1][sl][:, 0:2048], in_=l1q_st[cc]))
        GP(_)

        def _(te, cc=cc, sl=sl, pn=pn, wn=wn):
            OP("TE", te, [wn, "A"], [pn], lambda e: mmsteps(
                e, PSUM[pn], 0, 1024,
                [([WT0, WT1][sl][:, dc * 128:(dc + 1) * 128],
                  A[:, dc * 1024:(dc + 1) * 1024]) for dc in range(NDC)]))
        TE(_)

        def _(se, cc=cc, sl=sl, pn=pn):
            OP("SE", se, [pn, "SVC"], ["Bq"], lambda e: e.activation(
                Bq[:, cc * 1024:(cc + 1) * 1024], PSUM[pn][:, 0:1024],
                AF.Identity, bias=SVC[:, SV_BQ + cc:SV_BQ + cc + 1]))
        SE(_)

    for kc in range(4):
        sl = kc % 2
        pn = "PA" if sl == 0 else "PB"
        wn = ["WT0", "WT1"][sl]
        sqn = ["SM0", "SM1"][sl]

        def _(gp, kc=kc, sl=sl, wn=wn):
            OP("GP", gp, [], [wn], lambda e: e.dma_start(
                out=[WT0, WT1][sl][:, 0:2048], in_=l1k_st[kc]))
        GP(_)

        def _(te, kc=kc, sl=sl, pn=pn, wn=wn):
            OP("TE", te, [wn, "A"], [pn], lambda e: mmsteps(
                e, PSUM[pn], 0, 1024,
                [([WT0, WT1][sl][:, dc * 128:(dc + 1) * 128],
                  A[:, dc * 1024:(dc + 1) * 1024]) for dc in range(NDC)]))
        TE(_)

        def _(se, kc=kc, sl=sl, pn=pn, sqn=sqn):
            OP("SE", se, [pn, "SVC"], [sqn], lambda e: e.activation(
                [SM0, SM1][sl][:, 0:1024], PSUM[pn][:, 0:1024],
                AF.Identity, bias=SVC[:, SV_BK + kc:SV_BK + kc + 1]))
        SE(_)

        def _(sy, kc=kc, sl=sl, sqn=sqn):
            OP("SY", sy, [sqn], ["ag1_in"], lambda e: e.dma_start(
                out=ag1_in[kc * 128:(kc + 1) * 128, :],
                in_=[SM0, SM1][sl][:, 0:1024]))
        SY(_)

    for ncc in range(NDC):
        sl = ncc % 2
        wn = ["WT0", "WT1"][sl]

        def _(sy, ncc=ncc, sl=sl, wn=wn):
            OP("SY", sy, [], [wn], lambda e: e.dma_start(
                out=[WT0, WT1][sl][:, 0:2048], in_=l1w_st[ncc]))
        SY(_)

        def _(ve, ncc=ncc, sl=sl, wn=wn):
            OP("VE", ve, [wn, "AMX2"], ["AMX"], lambda e: e.reduce_max(
                AMX[:, ncc:ncc + 1], [WT0, WT1][sl][:, 0:2048], axis=AX,
                apply_absolute_value=True))
        VE(_)

    def _(ve):
        OP("VE", ve, ["AMX"], ["AMX2"], lambda e: e.reduce_max(
            AMX[:, 17:18], AMX[:, 0:16], axis=AX, apply_absolute_value=True))
    VE(_)

    def _(gp):
        OP("GP", gp, [],
           ["VW", "RB_D", "TBL0", "TBL1", "T1S0", "T1S1"], lambda e: [
               e.dma_start(out=VW[:, dc * 512:(dc + 1) * 512],
                           in_=l1v_mov[dc]) for dc in range(NDC)])
        OP("GP", gp, [], ["RB_D2"], lambda e: e.dma_start(
            out=RB_D2[0:1, 0:512], in_=bv_row.ap()))
    GP(_)

    def _(te):
        OP("TE", te, ["RB_D2", "CONS"], ["PD"], lambda e: mmsteps(
            e, PD, 0, 512, [(ONER, RB_D2[0:1, 0:512])]))
    TE(_)

    def _(ve):
        OP("VE", ve, ["PD"], ["BVC"], lambda e: e.tensor_copy(
            BVC[:, 0:512], PD[:, 0:512]))
    VE(_)

    for tb in range(8):
        sl = tb % 2
        pn = "PA" if sl == 0 else "PB"
        sqn = ["SM0", "SM1"][sl]

        def _(te, tb=tb, sl=sl, pn=pn):
            OP("TE", te, ["VW", "A"], [pn], lambda e: mmsteps(
                e, PSUM[pn], 0, 512,
                [(A[:, dc * 1024 + tb * 128:dc * 1024 + (tb + 1) * 128],
                  VW[:, dc * 512:(dc + 1) * 512]) for dc in range(NDC)]))
        TE(_)

        def _(ve, tb=tb, sl=sl, pn=pn, sqn=sqn):
            OP("VE", ve, [pn, "BVC"], [sqn], lambda e: e.tensor_tensor(
                [SM0, SM1][sl][:, 0:512], PSUM[pn][:, 0:512],
                BVC[:, 0:512], OP_.add))
        VE(_)

        def _(sy, tb=tb, sl=sl, sqn=sqn):
            r0 = 512 + (tb // 2) * 128
            c0 = (tb % 2) * 512
            OP("SY", sy, [sqn], ["ag1_in"], lambda e: e.dma_start(
                out=ag1_in[r0:r0 + 128, c0:c0 + 512],
                in_=[SM0, SM1][sl][:, 0:512]))
        SY(_)

    def _(gp):
        t = lastw.get("ag1_in")
        if t and seen["GP"].get(t[0], 0) < t[1]:
            gp.wait_ge(sems[t[0]][0], t[1])
            seen["GP"][t[0]] = t[1]
        ins_ = gp.collective_compute(
            "AllGather", OP_.bypass,
            replica_groups=[[0, 1], [2, 3], [4, 5], [6, 7]],
            ins=[ag1_in.ap().opt()], outs=[ag1_out.ap().opt()])
        sems["cAG1"][1] += 1
        ins_.then_inc(sems["cAG1"][0], 1)
        note("ag1_out", ("cAG1", sems["cAG1"][1]))
    GP(_)


    if DEBUG_DUMPS:
        def _(sy):
            OP("SY", sy, ["Bq"], ["dbg_q1"], lambda e: e.dma_start(
                out=dbg["dbg_q1"].ap(), in_=Bq[:, :]))
        SY(_)

    def _(sy):
        OP("SY", sy, ["AMX2"], ["STTb"], lambda e: e.dma_start(
            out=STT[0:1, 128:256], in_=AMX[:, 17:18]))
    SY(_)

    def _(ve):
        OP("VE", ve, ["STTb", "XSR"], ["XSR"], lambda e: e.reduce_max(
            XSR[0:1, 1:2], STT[0:1, 128:256], axis=AX,
            apply_absolute_value=True))
        OP("VE", ve, ["XSR"], ["XSR"], lambda e: e.reciprocal(
            XSR[0:1, 3:4], XSR[0:1, 1:2]))
        OP("VE", ve, ["XSR"], ["XSR"], lambda e: e.tensor_tensor(
            XSR[0:1, 4:5], XSR[0:1, 0:1], XSR[0:1, 1:2], OP_.mult))
    VE(_)

    def _(sy):
        OP("SY", sy, ["XSR"], ["scl_dram"], lambda e: e.dma_start(
            out=scl_dram[0:1, 1:3], in_=XSR[0:1, 3:5]))
        srcap = bass.AP(scl_dram, 1, [[0, 128], [1, 2]])
        OP("SY", sy, ["scl_dram"], ["SCL"], lambda e: e.dma_start(
            out=SCL[:, 1:3], in_=srcap))
    SY(_)

    for ncc in range(NDC):
        sl = ncc % 2
        pn = "PA" if sl == 0 else "PB"
        wn = ["WT0", "WT1"][sl]
        sqn = ["SM0", "SM1"][sl]

        def _(sy, ncc=ncc, sl=sl, wn=wn):
            OP("SY", sy, [], [wn], lambda e: e.dma_start(
                out=[WT0, WT1][sl][:, 0:2048], in_=l1w_st[ncc]))
        SY(_)

        def _(ve, ncc=ncc, sl=sl, wn=wn):
            for q8 in range(8):
                c0 = q8 * 256
                OP("VE", ve, [wn, "SCL"], ["SM8"],
                   lambda e, c0=c0: e.tensor_scalar(
                       SM8[:, 0:256], [WT0, WT1][sl][:, c0:c0 + 256],
                       SCL[:, 1:2], None, OP_.mult))
                OP("VE", ve, ["SM8"], [wn],
                   lambda e, c0=c0: e.tensor_copy(
                       [WT0, WT1][sl][:, c0:c0 + 256], SM8[:, 0:256]))
        VE(_)

        def _(te, ncc=ncc, sl=sl, pn=pn, wn=wn):
            OP("TE", te, [wn, "Cb"], [pn], lambda e: mmsteps(
                e, PSUM[pn], 0, 1024,
                [([WT0, WT1][sl][:, kcc * 128:(kcc + 1) * 128],
                  Cb[:, kcc * 1024:(kcc + 1) * 1024])
                 for kcc in range(NDC)]))
        TE(_)

        def _(ve, ncc=ncc, sl=sl, pn=pn, sqn=sqn):
            OP("VE", ve, [pn, "SCL"], [sqn], lambda e: e.tensor_scalar(
                [SM0, SM1][sl][:, 0:1024], PSUM[pn][:, 0:1024],
                SCL[:, 2:3], None, OP_.mult))
        VE(_)

        def _(se, ncc=ncc, sl=sl, sqn=sqn):
            OP("SE", se, [sqn, "SVCb"],
               ["Dg"] + DG_FAM[1:] if ncc == 0 else ["Dg"],
               lambda e: e.activation(
                   Dg[:, ncc * 1024:(ncc + 1) * 1024],
                   [SM0, SM1][sl][:, 0:1024], AF.Silu,
                   bias=SVC[:, SV_L1B + ncc:SV_L1B + ncc + 1]))
        SE(_)


    if DEBUG_DUMPS:
        def _(sy):
            OP("SY", sy, ["Dg"], ["dbg_g1"], lambda e: e.dma_start(
                out=dbg["dbg_g1"].ap(), in_=Dg[:, :]))
        SY(_)
    def _(sy):
        def em(e):
            outl = []
            for r in range(2):
                for kc in range(4):
                    outl.append(e.dma_start(
                        out=Cb[:, kc * 2048 + r * 1024:
                               kc * 2048 + (r + 1) * 1024],
                        in_=ag1_out[r * 1024 + kc * 128:
                                    r * 1024 + (kc + 1) * 128, :]))
            for g in range(16):
                r, tb = OWNER[g], LOCIDX[g]
                r0 = r * 1024 + 512 + (tb // 2) * 128
                c0 = (tb % 2) * 512
                outl.append(e.dma_start(
                    out=Cb[:, 8192 + g * 512:8192 + (g + 1) * 512],
                    in_=ag1_out[r0:r0 + 128, c0:c0 + 512]))
            return outl
        OP("SY", sy, ["ag1_out"], CB_FAM, em)
        OP("SY", sy, [], ["AM", "A", "AKV", "AE0", "AE1", "GS", "RB_A0", "RB_A1"],
           lambda e: e.dma_start(
               out=AMm, in_=am_t.ap().rearrange("j p c -> p j c")))
    SY(_)

    emit_attention(1)

    if DEBUG_DUMPS:
        def _(sy):
            OP("SY", sy, ["Bq"], ["dbg_y1"], lambda e: e.dma_start(
                out=dbg["dbg_y1"].ap(), in_=Bq[:, :]))
        SY(_)

    def _(ve):
        OP("VE", ve, ["PD"], ["GS"], lambda e: e.tensor_copy(
            GSA, PD[0:16, 0:1024]))

        def em(e):
            with nc.allow_low_precision(
                    reason="bf16 softmax denominators, in tol"):
                return e.reciprocal(GSB, GSA)
        OP("VE", ve, ["GS"], ["GS2"], em)
    VE(_)

    for h in range(16):
        rbs = RB_As[h % 2]
        rbn = ["RB_A0", "RB_A1"][h % 2]

        def _(sy, h=h, rbs=rbs, rbn=rbn):
            OP("SY", sy, ["GS2"], [rbn], lambda e: e.dma_start(
                out=rbs[0:1, 0:1024], in_=GSB[h:h + 1, 0:1024]))
        SY(_)

        def _(te, h=h, rbs=rbs, rbn=rbn):
            OP("TE", te, [rbn, "CONS"], ["PA"], lambda e: mmsteps(
                e, PA, 0, 1024, [(ONER, rbs[0:1, 0:1024])]))
        TE(_)

        def _(ve, h=h):
            OP("VE", ve, ["Bq", "PA"], ["SM0"], lambda e: e.tensor_tensor(
                SM0[:, 0:1024], Bq[:, h * 1024:(h + 1) * 1024],
                PA[:, 0:1024], OP_.mult))
            OP("VE", ve, ["SM0", "Dg"], ["SM1"], lambda e: e.tensor_tensor(
                SM1[:, 0:1024], SM0[:, 0:1024],
                Dg[:, h * 1024:(h + 1) * 1024], OP_.mult))
            OP("VE", ve, ["SM1", "X"], ["X"], lambda e: e.tensor_tensor(
                X[:, h * 1024:(h + 1) * 1024],
                X[:, h * 1024:(h + 1) * 1024], SM1[:, 0:1024], OP_.add))
        VE(_)

    def _(sy):
        OP("SY", sy, ["X"], ["xout"], lambda e: [
            e.dma_start(out=xout[dc], in_=X[:, dc * 1024:(dc + 1) * 1024])
            for dc in range(NDC)])
        sy.wait_ge(sems["qSY_xout"][0], sems["qSY_xout"][1])
    SY(_)

    blk.__exit__(None, None, None)
    for t in reversed(ctx[:-1]):
        try:
            t.__exit__(None, None, None)
        except Exception:
            pass
    return nc


def _prep_weights(ins):
    f = np.float32
    out = {}
    wqkv = np.asarray(ins["L0_attn_w_qkv"], f)
    wq = wqkv[:, 0:2048].reshape(16, 128, 16, 128).transpose(2, 1, 0, 3)
    wk = wqkv[:, 2048:4096].reshape(16, 128, 16, 128).transpose(2, 1, 0, 3)
    out["wqkv_st"] = np.ascontiguousarray(
        np.concatenate([wq, wk], axis=0).reshape(32, 128, 2048))
    out["wv_mov"] = np.ascontiguousarray(
        wqkv[:, 4096:6144].reshape(16, 128, 2048))
    out["wg_st"] = np.ascontiguousarray(
        np.asarray(ins["L0_attn_w_g"], f).reshape(16, 128, 16, 128)
        .transpose(2, 1, 0, 3).reshape(16, 128, 2048))
    out["wp_st"] = np.ascontiguousarray(
        np.asarray(ins["L0_attn_w_proj"], f).reshape(16, 128, 16, 128)
        .transpose(2, 1, 0, 3).reshape(16, 128, 2048))
    out["rw_st"] = np.ascontiguousarray(
        np.asarray(ins["L0_ffn_router_w"], f).T.reshape(16, 128, 8))
    wgu = np.stack([np.asarray(ins["L0_ffn_wg"], f),
                    np.asarray(ins["L0_ffn_wu"], f)], axis=0)
    out["wgu_st"] = np.ascontiguousarray(
        wgu.reshape(2, 8, 16, 128, 8, 128).transpose(0, 1, 4, 3, 2, 5)
        .reshape(2, 8, 8, 128, 2048))
    out["wd_st"] = np.ascontiguousarray(
        np.asarray(ins["L0_ffn_wd"], f).reshape(8, 8, 128, 16, 128)
        .transpose(0, 3, 2, 1, 4).reshape(8, 16, 128, 1024))
    sgu = np.stack([np.asarray(ins["L0_ffn_sg_w"], f),
                    np.asarray(ins["L0_ffn_su_w"], f)], axis=0)
    out["sgu_st"] = np.ascontiguousarray(
        sgu.reshape(2, 16, 128, 8, 128).transpose(0, 3, 2, 1, 4)
        .reshape(2, 8, 128, 2048))
    out["sd_st"] = np.ascontiguousarray(
        np.asarray(ins["L0_ffn_sd_w"], f).reshape(8, 128, 16, 128)
        .transpose(2, 1, 0, 3).reshape(16, 128, 1024))
    out["l1q_st"] = np.ascontiguousarray(
        np.asarray(ins["L1_attn_Wq"], f).reshape(16, 16, 128, 128)
        .transpose(0, 2, 1, 3).reshape(16, 128, 2048))
    out["l1k_st"] = np.ascontiguousarray(
        np.asarray(ins["L1_attn_Wk"], f).reshape(4, 16, 128, 128)
        .transpose(0, 2, 1, 3).reshape(4, 128, 2048))
    out["l1v_mov"] = np.ascontiguousarray(
        np.asarray(ins["L1_attn_Wv"], f).reshape(4, 16, 128, 128)
        .transpose(1, 2, 0, 3).reshape(16, 128, 512))
    out["bv_row"] = np.ascontiguousarray(
        np.asarray(ins["L1_attn_bv"], f).reshape(1, 512))
    w1 = np.asarray(ins["L1_ffn_weight"])
    out["l1w_st"] = np.ascontiguousarray(
        w1.reshape(16, 128, 16, 128).transpose(0, 3, 2, 1)
        .reshape(16, 128, 2048))
    out["l1b_bf"] = np.ascontiguousarray(
        np.asarray(ins["L1_ffn_bias"]).reshape(16, 128).T)

    sv = np.zeros((128, 128), f)
    sv[:, SV_GNW:SV_GNW + 16] = np.asarray(
        ins["L0_attn_gn_weight"], f).reshape(16, 128).T
    sv[:, SV_GNB:SV_GNB + 16] = np.asarray(
        ins["L0_attn_gn_bias"], f).reshape(16, 128).T
    sv[:, SV_SGB:SV_SGB + 8] = np.asarray(
        ins["L0_ffn_sg_b"], f).reshape(8, 128).T
    sv[:, SV_SUB:SV_SUB + 8] = np.asarray(
        ins["L0_ffn_su_b"], f).reshape(8, 128).T
    sv[:, SV_SDB:SV_SDB + 16] = np.asarray(
        ins["L0_ffn_sd_b"], f).reshape(16, 128).T
    sv[:, SV_BQ:SV_BQ + 16] = np.asarray(
        ins["L1_attn_bq"], f).reshape(16, 128).T
    sv[:, SV_BK:SV_BK + 4] = np.asarray(
        ins["L1_attn_bk"], f).reshape(4, 128).T
    out["smallw"] = sv
    return out


def _xpos_tables(parity):
    f = np.float64
    Cd = 2048
    pos = np.array([ZIG[parity][t // 128] * 128 + (t % 128)
                    for t in range(TPC)], f)
    i2 = np.arange(0, Cd, 2, dtype=f)
    dim_scale = (i2 + 0.4 * Cd) / (1.4 * Cd)
    scale = dim_scale[None, :] ** (pos[:, None] / 512.0)
    inv_freq = 1.0 / 10000.0 ** (i2 / Cd)
    sinu = pos[:, None] * inv_freq[None, :]
    sin = np.repeat(np.sin(sinu) * scale, 2, axis=1)
    cos = np.repeat(np.cos(sinu) * scale, 2, axis=1)
    gam = 1.0 - 2.0 ** (-5.0 - np.arange(16, dtype=f))
    gq = gam[np.arange(Cd) // 128][:, None] ** pos[None, :]
    gk = gam[np.arange(Cd) // 128][:, None] ** (-pos[None, :])
    cq = cos.T * gq * (128.0 ** -0.5)
    sq = sin.T * gq * (128.0 ** -0.5)
    ck = cos.T * gk
    sk = sin.T * gk

    def tile(m):
        return m.reshape(16, 128, TPC).astype(ml_dtypes.bfloat16)
    tq = np.stack([tile(cq), tile(sq)], axis=1)
    tk = np.stack([tile(ck), tile(sk)], axis=1)
    return np.ascontiguousarray(tq), np.ascontiguousarray(tk)


def _mask_tables(parity):
    am = np.zeros((16, 128, 256), np.float32)
    zz = ZIG[parity]
    tri = (np.arange(128)[:, None] <= np.arange(128)[None, :]).astype(
        np.float32)
    for j in range(16):
        L = LPROF[j]
        for p01 in range(min(2, L)):
            gb = zz[8 - L + p01]
            if gb < j:
                blkm = np.zeros((128, 128), np.float32)
            elif gb == j:
                blkm = tri
            else:
                blkm = np.ones((128, 128), np.float32)
            am[j, :, p01 * 128:(p01 + 1) * 128] = blkm
    return np.ascontiguousarray(am.astype(ml_dtypes.bfloat16))


def _consts_arr():
    c = np.zeros((128, 648), np.float32)
    for i in range(64):
        c[2 * i + 1, 2 * i] = -1.0
        c[2 * i, 2 * i + 1] = 1.0
    for h in range(16):
        c[:, CN_ECOL + h * 16 + h] = 1.0
    c[:, CN_ONE] = 1.0
    c[0, CN_ONER:CN_ONER + 128] = 1.0
    return np.ascontiguousarray(c.astype(ml_dtypes.bfloat16))


_NC_CACHE = {}


def kernel(**ins):
    if "nc" not in _NC_CACHE:
        _NC_CACHE["nc"] = build_program()
    nc = _NC_CACHE["nc"]

    w = _prep_weights(ins)
    consts = _consts_arr()
    identf = np.eye(128, dtype=np.float32)
    tabs = [_xpos_tables(0), _xpos_tables(1)]
    ams = [_mask_tables(0), _mask_tables(1)]

    x = np.asarray(ins["L0_attn_x"], np.float32)
    in_maps = []
    for core in range(8):
        b, par = core // 2, core % 2
        xs = np.empty((16, 128, TPC), np.float32)
        for lb in range(8):
            g = ZIG[par][lb]
            blkx = x[b, g * 128:(g + 1) * 128, :]
            xs[:, :, lb * 128:(lb + 1) * 128] = blkx.T.reshape(16, 128, 128)
        m = dict(w)
        m["x0"] = xs
        m["tabq"], m["tabk"] = tabs[par]
        m["am_t"] = ams[par]
        m["consts"] = consts
        m["identf"] = identf
        in_maps.append(m)

    res = run_bass_kernel_spmd(nc, in_maps, core_ids=list(range(8)))

    out = np.empty((4, 2048, 2048), np.float32)
    for core in range(8):
        b, par = core // 2, core % 2
        xo = res.results[core]["xout"]
        for lb in range(8):
            g = ZIG[par][lb]
            out[b, g * 128:(g + 1) * 128, :] = \
                xo[:, :, lb * 128:(lb + 1) * 128].reshape(2048, 128).T
    return out


if __name__ == "__main__":
    nc = build_program()
    print("build ok:", sum(1 for _ in nc.all_instructions()), "instructions")
